# version 10
# baseline (speedup 1.0000x reference)
import numpy as np
import ml_dtypes
from contextlib import ExitStack
import concourse.bass as bass
import concourse.mybir as mybir
from concourse.bass_utils import run_bass_kernel_spmd

F32 = mybir.dt.float32
BF16 = mybir.dt.bfloat16
AF = mybir.ActivationFunctionType
ALU = mybir.AluOpType
AX = mybir.AxisListType
bfd = ml_dtypes.bfloat16

DM, DI, NH = 2048, 4096, 64
CONVD, DINP = 4352, 8576
NFC, NZC, NCC, NDC = 67, 32, 34, 16
ENG_OF = {"pe": "tensor", "dve": "vector", "act": "scalar", "pool": "gpsimd", "sync": "sync"}


def default_cfg():
    return dict(n_cores=8, s_core=4096, T=1024, TF=512,
                pairs=[[0, 1], [2, 3], [4, 5], [6, 7]],
                allcores=[[0, 1, 2, 3, 4, 5, 6, 7]])


class P:

    def __init__(self, nc, st):
        self.nc = nc
        self.st = st
        names = ["pe", "dve", "act", "pool", "coll", "bar"]
        self.S = {n: st.enter_context(nc.semaphore("s_" + n)) for n in names}
        self.C = {n: 0 for n in names}
        self.waited = {}
        self.thunks = {e: [] for e in ENG_OF}
        self.nbar = 0

    def _dsem(self, slot):
        name = "q_" + slot
        if name not in self.S:
            self.S[name] = self.st.enter_context(self.nc.semaphore(name))
            self.C[name] = 0
        return name

    def _waits(self, ename, toks):
        out = []
        for t in toks:
            if t is None:
                continue
            s, v = t
            if v <= 0:
                continue
            key = (ename, s)
            if self.waited.get(key, 0) >= v:
                continue
            self.waited[key] = v
            out.append((self.S[s], v))
        return out

    def _emit(self, ename, make, waits, sem, n):
        ws = self._waits(ename, waits)
        self.C[sem] += n
        semobj = self.S[sem]

        def th(e, ws=ws, make=make, semobj=semobj, n=n):
            for s, v in ws:
                e.wait_ge(s, v)
            make(e).then_inc(semobj, n)
        self.thunks[ename].append(th)
        return (sem, self.C[sem])

    def pe(self, make, waits=(), inc=True):
        if inc:
            return self._emit("pe", make, waits, "pe", 1)
        ws = self._waits("pe", waits)

        def th(e, ws=ws, make=make):
            for s, v in ws:
                e.wait_ge(s, v)
            make(e)
        self.thunks["pe"].append(th)
        return ("pe", self.C["pe"])

    def dve(self, make, waits=()):
        return self._emit("dve", make, (("dve", self.C["dve"]),) + tuple(waits), "dve", 1)

    def act(self, make, waits=()):
        return self._emit("act", make, (("act", self.C["act"]),) + tuple(waits), "act", 1)

    def pool(self, make, waits=()):
        return self._emit("pool", make, (("pool", self.C["pool"]),) + tuple(waits), "pool", 1)

    def dma(self, issuer, slot, out, in_, waits=(), **kw):
        q = self._dsem(slot)
        return self._emit(issuer, lambda e: e.dma_start(out=out, in_=in_, **kw), waits, q, 16)

    def coll(self, make, waits=()):
        return self._emit("pool", make, waits, "coll", 1)

    def tok(self, sem):
        return (sem, self.C[sem])

    def barrier(self):
        self.nbar += 1
        target = 5 * self.nbar
        snap = dict(self.C)
        dq = [k for k in snap if k.startswith("q_")] + ["coll"]
        for ename in ("sync", "act", "pool", "dve", "pe"):
            toks = [("pe", snap["pe"]), ("dve", snap["dve"]), ("act", snap["act"]),
                    ("pool", snap["pool"])] + ([(q, snap[q]) for q in dq]
                                               if ename == "sync" else [])
            ws = self._waits(ename, toks)
            self.C["bar"] += 1
            bs = self.S["bar"]

            def th(e, ws=ws, bs=bs, tgt=target):
                for s, v in ws:
                    e.wait_ge(s, v)
                e.nop().then_inc(bs, 1)
                e.wait_ge(bs, tgt)
            self.thunks[ename].append(th)
            self.waited[(ename, "bar")] = target
        for en in ENG_OF:
            for s, v in snap.items():
                self.waited[(en, s)] = max(self.waited.get((en, s), 0), v)

    def finalize(self, blk):
        for ename, ths in self.thunks.items():
            def runner(e, ths=ths):
                for th in ths:
                    th(e)
            getattr(blk, ENG_OF[ename])(runner)


class XfmShared:
    def __init__(self):
        self.v = None

    def __getitem__(self, i):
        return self.v

    def __setitem__(self, i, val):
        self.v = val


def mx(*toks):
    best = None
    for t in toks:
        if t is None:
            continue
        if best is None or (t[0] == best[0] and t[1] > best[1]):
            best = t
    return best


def build_program(cfg):
    n_cores = cfg["n_cores"]
    SC, T, TF = cfg["s_core"], cfg["T"], cfg["TF"]
    NTIL, CPT, NCHUNK = SC // T, T // 128, SC // 128
    NTF = SC // TF
    TBT, TBF = T // 128, TF // 128
    HLF = 256
    NQH = T // HLF
    CPH = HLF // 128

    nc = bass.Bass("TRN2", target_bir_lowering=False, debug=False)
    dt_ = nc.dram_tensor

    x_main = dt_("x_main", [SC, DM], F32, kind="ExternalInput").ap()
    x_halo = dt_("x_halo", [3, DM], F32, kind="ExternalInput").ap()
    w_in_blk = dt_("w_in_blk", [NFC, 128, DM], F32, kind="ExternalInput").ap()
    conv_w = dt_("conv_w", [CONVD, 4], F32, kind="ExternalInput").ap()
    conv_b = dt_("conv_b", [CONVD], F32, kind="ExternalInput").ap()
    a_in = dt_("a_in", [NH], F32, kind="ExternalInput").ap()
    d_in = dt_("d_in", [NH], F32, kind="ExternalInput").ap()
    dtb_in = dt_("dtb_in", [NH], F32, kind="ExternalInput").ap()
    nw_in = dt_("nw_in", [DI], F32, kind="ExternalInput").ap()
    w_out_blk = dt_("w_out_blk", [16, 128, 4096], F32, kind="ExternalInput").ap()
    w1_blk = dt_("w1_blk", [64, 128, DM], F32, kind="ExternalInput").ap()
    w1s_in = dt_("w1s_in", [8192 // n_cores, DM], F32, kind="ExternalInput").ap()
    b1_in = dt_("b1_in", [8192], F32, kind="ExternalInput").ap()
    w2_blk = dt_("w2_blk", [16, 128, 8192], F32, kind="ExternalInput").ap()
    w2s_in = dt_("w2s_in", [DM // n_cores, 8192], F32, kind="ExternalInput").ap()
    b2_in = dt_("b2_in", [DM], F32, kind="ExternalInput").ap()
    c_ident_b = dt_("c_ident_b", [128, 128], BF16, kind="ExternalInput").ap()
    c_ident_f = dt_("c_ident_f", [128, 128], F32, kind="ExternalInput").ap()
    c_ones_row = dt_("c_ones_row", [1, 128], BF16, kind="ExternalInput").ap()
    c_ones_row_f = dt_("c_ones_row_f", [1, 128], F32, kind="ExternalInput").ap()
    c_ones_col_f = dt_("c_ones_col_f", [128, 1], F32, kind="ExternalInput").ap()
    c_ones_col_b = dt_("c_ones_col_b", [128, 1], BF16, kind="ExternalInput").ap()
    c_masknegT = dt_("c_masknegT", [128, 128], BF16, kind="ExternalInput").ap()
    c_irep = dt_("c_irep", [128, 512], BF16, kind="ExternalInput").ap()
    c_hseld = dt_("c_hseld", [128, 8192], BF16, kind="ExternalInput").ap()
    c_tlow = dt_("c_tlow", [128, 128], F32, kind="ExternalInput").ap()
    c_smask = dt_("c_smask", [1, 1], F32, kind="ExternalInput").ap()
    out_d = dt_("out", [SC, DM], F32, kind="ExternalOutput").ap()

    wbin = dt_("wbin", [NFC, 128, DM], BF16, kind="Internal").ap()
    wboutc = dt_("wboutc", [16, 128, 4096], BF16, kind="Internal").ap()
    wb1 = dt_("wb1", [64, 128, DM], BF16, kind="Internal").ap()
    wb2c = dt_("wb2c", [16, 128, 8192], BF16, kind="Internal").ap()
    z_scr = dt_("z_scr", [NCHUNK, 128, NZC * 128], BF16, kind="Internal").ap()
    xfm_scr = dt_("xfm_scr", [NCHUNK, 128, NZC * 128], BF16, kind="Internal").ap()
    bc_scr = dt_("bc_scr", [NCHUNK, 128, 256], BF16, kind="Internal").ap()
    xtm_scr = dt_("xtm_scr", [NCHUNK, 128, DI], BF16, kind="Internal").ap()
    pv_scr = dt_("pv_scr", [NCHUNK, 128, DI], BF16, kind="Internal").ap()
    srec_scr = dt_("srec_scr", [NCHUNK, 128, 128], BF16, kind="Internal").ap()
    aflat_scr = dt_("aflat_scr", [NCHUNK, 2, 64, 128], BF16, kind="Internal").ap()
    cd_scr = dt_("cd_scr", [NCHUNK, 64], F32, kind="Internal").ap()
    x2_scr = dt_("x2_scr", [SC, DM], F32, kind="Internal").ap()
    rsum_in = dt_("rsum_in", [1, 2], F32, kind="Internal").ap()
    rsum_out = dt_("rsum_out", [1, 2], F32, kind="Internal").ap()
    ag_in = dt_("ag_in", [128, DI], F32, kind="Internal").ap()
    ag_out = dt_("ag_out", [2 * 128, DI], F32, kind="Internal").ap()

    st = ExitStack()
    p = P(nc, st)
    PB = [st.enter_context(nc.psum_tensor(f"pb{i}", [128, 512], F32)) for i in range(6)]

    def mksb(stk):
        return lambda name, shape, dt=F32: stk.enter_context(
            nc.sbuf_tensor(name, list(shape), dt))
    sbuf = mksb(st)

    ident_b = sbuf("ident_b", (128, 128), BF16)
    ident_f = sbuf("ident_f", (128, 128))
    ones_row = sbuf("ones_row", (1, 128), BF16)
    ones_row_f = sbuf("ones_row_f", (1, 128))
    ones_col_f = sbuf("ones_col_f", (128, 1))
    ones_col_b = sbuf("ones_col_b", (128, 1), BF16)
    masknegT = sbuf("masknegT", (128, 128), BF16)
    irep = sbuf("irep", (128, 512), BF16)
    tlow = sbuf("tlow", (128, 128))
    cw_cols = sbuf("cw_cols", (128, NCC * 4))
    cb_cols = sbuf("cb_cols", (128, NCC))
    a_col = sbuf("a_col", (64, 1))
    dtb_col = sbuf("dtb_col", (64, 1))
    nw_cols = sbuf("nw_cols", (128, NZC))
    b1_cols = sbuf("b1_cols", (128, 64))
    smask_c = sbuf("smask_c", (128, 1))
    rsb = sbuf("rsb", (128, 2))
    S_run = sbuf("S_run", (128, DI))
    carry = sbuf("carry", (128, NCC * 3))
    cp_row = sbuf("cp_row", (1, 64))
    drow = sbuf("drow", (1, 64))
    db_full = sbuf("db_full", (128, 64))
    rs_row = sbuf("rs_row", (1, 2))
    sums2x = sbuf("sums2x", (1, 2))
    stat = sbuf("stat", (128, 8))

    for o, i_, kw in [
        (ident_b[:], c_ident_b, {}), (ident_f[:], c_ident_f, {}),
        (ones_row[:], c_ones_row, {}), (ones_row_f[:], c_ones_row_f, {}),
        (ones_col_f[:], c_ones_col_f, {}), (ones_col_b[:], c_ones_col_b, {}),
        (masknegT[:], c_masknegT, {}), (irep[:], c_irep, {}), (tlow[:], c_tlow, {}),
        (cw_cols[:].rearrange("p (cc t) -> p cc t", t=4),
         conv_w.rearrange("(cc p) t -> p cc t", p=128), dict(allow_slow_non_contiguous=True)),
        (cb_cols[:], conv_b.rearrange("(cc p) -> p cc", p=128), dict(allow_slow_non_contiguous=True)),
        (a_col[:], a_in.rearrange("(h o) -> h o", o=1), {}),
        (dtb_col[:], dtb_in.rearrange("(h o) -> h o", o=1), {}),
        (nw_cols[:], nw_in.rearrange("(c p) -> p c", p=128), dict(allow_slow_non_contiguous=True)),
        (b1_cols[:], b1_in.rearrange("(c p) -> p c", p=128), dict(allow_slow_non_contiguous=True)),
        (smask_c[:], c_smask.broadcast_to((128, 1)), {}),
        (drow[:], d_in.rearrange("(o h) -> o h", o=1), {}),
    ]:
        p.dma("sync", "setup", o, i_, **kw)
    t_setup = p.tok("q_setup")
    td = p.pe(lambda t: t.matmul(PB[5][:, 0:64], ones_row_f[:], drow[:], start=True, stop=True),
              waits=[t_setup])
    p.dve(lambda v: v.tensor_copy(db_full[:], PB[5][:, 0:64]), waits=[td])
    p.dve(lambda v: v.memset(cp_row[:], 1.0))
    p.dve(lambda v: v.memset(S_run[:], 0.0))
    p.barrier()

    sc0 = ExitStack()
    sb0 = mksb(sc0)
    wtile = [sb0(f"p0_wt{i}", (128, DM)) for i in range(2)]
    wtb = [sb0(f"p0_wb{i}", (128, DM), BF16) for i in range(2)]
    cast_tok = [None, None]
    wr_tok = [None, None]
    for fc in range(NFC):
        i = fc % 2
        ld = p.dma("sync", f"wt{i}", wtile[i][:], w_in_blk[fc], waits=[cast_tok[i]])
        cast_tok[i] = p.dve(lambda v, i=i: v.tensor_copy(wtb[i][:], wtile[i][:]),
                            waits=[ld, wr_tok[i]])
        wr_tok[i] = p.dma("pool", f"wbin{i}", wbin[fc], wtb[i][:], waits=[cast_tok[i]])
    p.barrier()
    t_wbin = p.tok("bar")
    sc0.close()

    ln_guard = [None]

    def ln_block(xb_ap, out_ap, D, waits_in, sqv):
        Pq = xb_ap.shape[0]
        sv = stat[0:Pq, :]
        qv = sqv[0:Pq, 0:D]
        wi = list(waits_in) + [ln_guard[0]]
        tr = p.dve(lambda v: v.reduce_sum(sv[:, 0:1], xb_ap, axis=AX.X), waits=wi)
        ts_ = p.act(lambda s: s.activation(qv, xb_ap, AF.Square, accum_out=sv[:, 1:2]), waits=wi)
        p.dve(lambda v: v.tensor_scalar(sv[:, 2:3], sv[:, 0:1], 1.0 / D, None, ALU.mult), waits=[tr])
        p.dve(lambda v: v.tensor_scalar(sv[:, 3:4], sv[:, 1:2], 1.0 / D, None, ALU.mult), waits=[ts_])
        p.dve(lambda v: v.tensor_mul(sv[:, 4:5], sv[:, 2:3], sv[:, 2:3]))
        p.dve(lambda v: v.tensor_sub(sv[:, 3:4], sv[:, 3:4], sv[:, 4:5]))
        tv = p.dve(lambda v: v.tensor_scalar(sv[:, 3:4], sv[:, 3:4], 1e-5, None, ALU.add))
        tw = p.act(lambda s: s.activation(sv[:, 5:6], sv[:, 3:4], AF.Sqrt), waits=[tv])
        tx = p.dve(lambda v: v.reciprocal(sv[:, 5:6], sv[:, 5:6]), waits=[tw])
        tf = p.dve(lambda v: v.tensor_scalar(out_ap, xb_ap, sv[:, 2:3], sv[:, 5:6],
                                             ALU.subtract, ALU.mult), waits=[tx])
        ln_guard[0] = tf
        return tf

    sc1 = ExitStack()
    sb1 = mksb(sc1)
    PT = [sc1.enter_context(nc.psum_tensor(f"pt{i}", [128, 1024], BF16)) for i in range(2)]
    ln1_fm = sb1("ln1_fm", (128, NDC * T), BF16)
    xbuf = sb1("xbuf", (128, DM))
    sq_scr = sb1("sq_scr", (128, DM), BF16)
    lntb = [sb1(f"lntb{i}", (128, DM), BF16) for i in range(2)]
    slab = [sb1(f"slab{i}", (128, DM), BF16) for i in range(2)]
    raw = [sb1(f"raw{i}", (128, T + 3)) for i in range(3)]
    accv = [sb1(f"accv{i}", (128, T)) for i in range(2)]
    stage_bf = [sb1(f"stage{i}", (128, T), BF16) for i in range(2)]
    zstage = [sb1(f"zst{i}", (128, 512), BF16) for i in range(2)]
    dtraw = sb1("dtraw", (64, T))
    dts1 = sb1("dts1", (64, T))
    dts2 = sb1("dts2", (64, T))
    a_fm = sb1("a_fm", (64, T))
    u_fm = sb1("u_fm", (64, T))
    datm = sb1("datm", (128, 64))
    utm_buf = sb1("utm_buf", (128, CPT * 64))
    cd_buf = sb1("cd_buf", (64, CPT))
    cdrow_buf = sb1("cdrow_buf", (1, CPT * 64))
    shi_buf = sb1("shi_buf", (64, T), BF16)
    slo_buf = sb1("slo_buf", (64, T), BF16)
    ahi_buf = sb1("ahi_buf", (64, T), BF16)
    alo_buf = sb1("alo_buf", (64, T), BF16)
    bm_tm = sb1("bm_tm", (128, CPT * 128), BF16)
    xfm_c = [sb1("xfm_c0", (128, DI), BF16)]
    xfm_c = [xfm_c[0], xfm_c[0]]
    xtm_c = sb1("xtm_c", (128, DI), BF16)
    xu_c = sb1("xu_c", (128, DI), BF16)
    cdb_f = sb1("cdb_f", (128, 64))
    halo_fm = sb1("halo_fm", (128, NDC * 3), BF16)
    p0tile = [sb1(f"p0tile{i}", (128, 1024)) for i in range(3)]
    p0out = [sb1("p0out0", (128, 1024), BF16)]
    p0tmp = sb1("p0tmp", (128, 1024))
    p0sum = sb1("p0sum", (128, 4))

    dtp = dtraw
    ldt_v = dts1
    s32 = dts1
    dA_fm = dts2

    ld = p.dma("sync", "xin", xbuf[0:3, :], x_halo)
    thl = ln_block(xbuf[0:3, :], lntb[0][0:3, :], DM, [ld], sq_scr)
    hev = []
    for dc in range(NDC):
        tt_ = p.pe(lambda t, dc=dc: t.transpose(PT[dc % 2][:, 0:3],
                                                lntb[0][0:3, dc * 128:(dc + 1) * 128], ident_b[0:3, 0:3]),
                   waits=[thl, hev[dc - 2] if dc >= 2 else None])
        hev.append(p.dve(lambda v, dc=dc: v.tensor_copy(halo_fm[:, dc * 3:(dc + 1) * 3],
                                                        PT[dc % 2][:, 0:3]), waits=[tt_]))
    hslab_free = [None, None]
    hpb_free = [None, None]
    for k in range(NCC):
        fc = NZC + k
        i = k % 2
        sl = p.dma("sync", f"sl{i}", slab[i][:], wbin[fc], waits=[t_wbin, hslab_free[i]])
        mmt = None
        for dc in range(NDC):
            mmt = p.pe(lambda t, i=i, dc=dc: t.matmul(
                PB[i][:, 0:3], slab[i][:, dc * 128:(dc + 1) * 128], halo_fm[:, dc * 3:(dc + 1) * 3],
                start=(dc == 0), stop=(dc == NDC - 1)),
                waits=[sl, hev[dc], hpb_free[i]], inc=(dc == NDC - 1))
        hslab_free[i] = mmt
        hpb_free[i] = p.dve(lambda v, k=k, i=i: v.tensor_copy(carry[:, k * 3:(k + 1) * 3],
                                                              PB[i][:, 0:3]), waits=[mmt])
    t_carry = p.tok("dve")

    pbfree = [None] * 6
    ptfree = [None, None]
    slab_free = list(hslab_free)
    stage_free2 = [None, None]
    raw_free = [None] * 3
    acc_free = [None, None]
    stage_free = [None, None]
    zst_free = [None, None]
    lntb_free = [hev[-1], None]
    xbuf_free = [thl]
    xfm_free = XfmShared()
    xtm_free = [None]
    xtm_sp = [None]
    states_done = [None]
    pv_sp = [None]
    t_dt_prev = [None]
    conv_sp = {}
    bmtm_tok = [None]
    wr0 = [None, None, None]
    sum_free = [None, None]
    rsb_tok = [None]
    q_free = [None, None, None]
    wq_last = [None, None]
    t_tile_guard = [None]

    def p0b_gen_fn():
        for dmc in range(16):
            for qt in range(4):
                i = (dmc * 4 + qt) % 3
                oi = 0
                ld = p.dma("pool", f"p0_{i}", p0tile[i][:],
                           w_out_blk[dmc][:, qt * 1024:(qt + 1) * 1024],
                           waits=[wr0[i]])
                tq_ = p.dve(lambda v, qt=qt, i=i, oi=oi: v.tensor_tensor(
                    p0out[oi][:].rearrange("p (c m) -> p c m", m=128),
                    p0tile[i][:].rearrange("p (c m) -> p c m", m=128),
                    nw_cols[:, qt * 8:(qt + 1) * 8].rearrange(
                        "p (c o) -> p c o", o=1).broadcast_to((128, 8, 128)),
                    ALU.mult),
                    waits=[ld, wq_last[oi]])
                wr0[i] = tq_
                wq_last[oi] = p.dma("pool", f"wout{i}",
                                    wboutc[dmc][:, qt * 1024:(qt + 1) * 1024],
                                    p0out[oi][:], waits=[tq_])
                yield
        tz = p.dve(lambda v: v.memset(p0sum[:], 0.0))
        sum_free[0] = tz
        for k in range(8192 // 128 // n_cores):
            for hf in range(2):
                i = (k * 2 + hf) % 3
                ld = p.dma("pool", f"p0_{i}", p0tile[i][:],
                           w1s_in[k * 128:(k + 1) * 128, hf * 1024:(hf + 1) * 1024],
                           waits=[sum_free[0]])
                tr = p.dve(lambda v, i=i: v.reduce_sum(p0sum[:, 2:3], p0tile[i][:], axis=AX.X,
                                                       apply_absolute_value=True), waits=[ld])
                sum_free[0] = p.dve(lambda v: v.tensor_add(p0sum[:, 0:1], p0sum[:, 0:1],
                                                           p0sum[:, 2:3]), waits=[tr])
                yield
        for k in range(DM // 128 // n_cores):
            for hf in range(8):
                i = (k * 8 + hf) % 3
                ld = p.dma("pool", f"p0_{i}", p0tile[i][:],
                           w2s_in[k * 128:(k + 1) * 128, hf * 1024:(hf + 1) * 1024],
                           waits=[sum_free[0]])
                tr = p.dve(lambda v, i=i: v.reduce_sum(p0sum[:, 2:3], p0tile[i][:], axis=AX.X,
                                                       apply_absolute_value=True), waits=[ld])
                sum_free[0] = p.dve(lambda v: v.tensor_add(p0sum[:, 1:2], p0sum[:, 1:2],
                                                           p0sum[:, 2:3]), waits=[tr])
                yield
        tm1 = p.pe(lambda t_: t_.matmul(PB[3][0:1, 384:385], ones_col_f[:], p0sum[:, 0:1],
                                        start=True, stop=True), waits=[sum_free[0], pbfree[3]])
        tm2 = p.pe(lambda t_: t_.matmul(PB[3][0:1, 385:386], ones_col_f[:], p0sum[:, 1:2],
                                        start=True, stop=True))
        tv = p.dve(lambda v: v.tensor_copy(sums2x[:], PB[3][0:1, 384:386]), waits=[tm1, tm2])
        pbfree[3] = tv
        tdm = p.dma("pool", "rsum", rsum_in, sums2x[:], waits=[tv])
        tcl = p.coll(lambda g: g.collective_compute(
            "AllReduce", ALU.add, replica_groups=cfg["allcores"], ins=[rsum_in],
            outs=[rsum_out]), waits=[tdm])
        ld = p.dma("sync", "rsum2", rs_row[:], rsum_out, waits=[tcl])
        t1 = p.dve(lambda v: v.tensor_scalar(rs_row[:], rs_row[:], 1.0 / (8192.0 * DM), None,
                                             ALU.mult), waits=[ld])
        t2 = p.dve(lambda v: v.tensor_scalar(rs_row[:], rs_row[:], 1e-8, None, ALU.max),
                   waits=[t1])
        t3 = p.dve(lambda v: v.reciprocal(rs_row[:], rs_row[:]), waits=[t2])
        t4 = p.pe(lambda t_: t_.matmul(PB[3][:, 386:388], ones_row_f[:], rs_row[:],
                                       start=True, stop=True), waits=[t3, pbfree[3]])
        rsb_tok[0] = p.dve(lambda v: v.tensor_copy(rsb[:], PB[3][:, 386:388]), waits=[t4])
        pbfree[3] = rsb_tok[0]
        yield
        tmp_g = [None]
        for w2mode in (False, True):
            steps = ([(dmc, qc) for dmc in range(16) for qc in range(8)] if w2mode
                     else [(fc, qc) for fc in range(64) for qc in range(2)])
            for a, qc in steps:
                i = (a * (8 if w2mode else 2) + qc) % 3
                oi = 0
                srcw = (w2_blk[a][:, qc * 1024:(qc + 1) * 1024] if w2mode
                        else w1_blk[a][:, qc * 1024:(qc + 1) * 1024])
                dstw = (wb2c[a][:, qc * 1024:(qc + 1) * 1024] if w2mode
                        else wb1[a][:, qc * 1024:(qc + 1) * 1024])
                ld = p.dma("pool", f"p0_{i}", p0tile[i][:], srcw,
                           waits=[rsb_tok[0], q_free[i]])
                sc = 1 if w2mode else 0
                t1 = p.dve(lambda v, i=i, sc=sc: v.tensor_scalar(
                    p0tmp[:], p0tile[i][:], rsb[:, sc:sc + 1], None, ALU.mult),
                    waits=[ld, tmp_g[0]])
                t2 = p.dve(lambda v, i=i: v.tensor_scalar(
                    p0tile[i][:], p0tmp[:], 0.5, -0.5, ALU.min, ALU.max), waits=[t1])
                t3 = p.dve(lambda v, i=i: v.tensor_sub(p0tmp[:], p0tmp[:], p0tile[i][:]),
                           waits=[t2])
                q_free[i] = t3
                t4 = p.act(lambda s, oi=oi: s.activation(p0out[oi][:], p0tmp[:], AF.Sign),
                           waits=[t3, wq_last[oi]])
                tmp_g[0] = t4
                wq_last[oi] = p.dma("pool", "wq", dstw, p0out[oi][:], waits=[t4])
                yield
        yield

    def pass1_tile(t):
        gc0 = t * CPT
        t_srec = p.tok("q_srec") if "q_srec" in p.C else None
        for tb in range(TBT):
            i = tb % 2
            ldx = p.dma("sync", "xin", xbuf[:], x_main[t * T + tb * 128:t * T + (tb + 1) * 128, :],
                        waits=[xbuf_free[0]])
            tln = ln_block(xbuf[:], lntb[i][:], DM, [ldx, lntb_free[i]], sq_scr)
            xbuf_free[0] = tln
            for g8 in range(2):
                pti = (tb * 2 + g8) % 2
                tts = None
                for k in range(8):
                    dc = g8 * 8 + k
                    tts = p.pe(lambda t_, dc=dc, k=k, pti=pti, i=i: t_.transpose(
                        PT[pti][:, k * 128:(k + 1) * 128], lntb[i][:, dc * 128:(dc + 1) * 128],
                        ident_b[:]), waits=[tln, ptfree[pti]])
                tev = p.dve(lambda v, g8=g8, tb=tb, pti=pti: v.tensor_copy(
                    ln1_fm[:].rearrange("p (dc tok) -> p dc tok", tok=T)[:, g8 * 8:(g8 + 1) * 8,
                                                                        tb * 128:(tb + 1) * 128],
                    PT[pti][:].rearrange("p (k f) -> p k f", f=128)),
                    waits=[tts, t_tile_guard[0]])
                ptfree[pti] = tev
                lntb_free[i] = tev
        t_ln1 = p.tok("dve")

        NQ = T // 512
        dt_ev = [None] * NQ
        for fc in range(NFC):
            i = fc % 2
            sl = p.dma("sync", f"sl{i}", slab[i][:], wbin[fc], waits=[t_wbin, slab_free[i]])
            for q in range(NQ):
                r = (fc * NQ + q) % 3
                mmt = None
                for dc in range(NDC):
                    mmt = p.pe(lambda t_, i=i, dc=dc, q=q, r=r: t_.matmul(
                        PB[r][:], slab[i][:, dc * 128:(dc + 1) * 128],
                        ln1_fm[:, dc * T + q * 512:dc * T + (q + 1) * 512],
                        start=(dc == 0), stop=(dc == NDC - 1)),
                        waits=[sl, t_ln1, pbfree[r]], inc=(dc == NDC - 1))
                if q == NQ - 1:
                    slab_free[i] = mmt
                if fc < NZC:
                    zi = (fc * NQ + q) % 2
                    tz = p.act(lambda s, r=r, zi=zi: s.activation(zstage[zi][:], PB[r][:], AF.Copy),
                               waits=[mmt, zst_free[zi]])
                    pbfree[r] = tz
                    zst_free[zi] = p.dma("pool", f"zw{zi}",
                                         z_scr[t * CPT + q * 4:t * CPT + q * 4 + 4, :,
                                               fc * 128:(fc + 1) * 128].rearrange("c p f -> p c f"),
                                         zstage[zi][:].rearrange("p (c f) -> p c f", f=128),
                                         waits=[tz])
                elif fc < NZC + NCC:
                    cc = fc - NZC
                    ri = cc % 3
                    if q == 0:
                        p.dve(lambda v, cc=cc, ri=ri: v.tensor_copy(
                            raw[ri][:, 0:3], carry[:, cc * 3:(cc + 1) * 3]),
                            waits=[t_carry, raw_free[ri]])
                    te = p.act(lambda s, r=r, ri=ri, q=q: s.activation(
                        raw[ri][:, 3 + q * 512:3 + (q + 1) * 512], PB[r][:], AF.Copy),
                        waits=[mmt])
                    pbfree[r] = te
                    if q == NQ - 1:
                        ai = cc % 2
                        t0 = p.dve(lambda v, ri=ri, cc=cc, ai=ai: v.tensor_scalar(
                            accv[ai][:], raw[ri][:, 0:T], cw_cols[:, cc * 4:cc * 4 + 1], None,
                            ALU.mult), waits=[te, acc_free[ai]])
                        for k in range(1, 4):
                            t0 = p.dve(lambda v, ri=ri, cc=cc, ai=ai, k=k: v.scalar_tensor_tensor(
                                accv[ai][:], raw[ri][:, k:k + T],
                                cw_cols[:, cc * 4 + k:cc * 4 + k + 1],
                                accv[ai][:], ALU.mult, ALU.add), waits=[t0])
                        raw_free[ri] = p.dve(lambda v, ri=ri, cc=cc: v.tensor_copy(
                            carry[:, cc * 3:(cc + 1) * 3], raw[ri][:, T:T + 3]), waits=[te])
                        si = cc % 2
                        tsil = p.act(lambda s, ai=ai, cc=cc, si=si: s.activation(
                            stage_bf[si][:], accv[ai][:], AF.Silu, bias=cb_cols[:, cc:cc + 1]),
                            waits=[t0, stage_free[si], stage_free2[si]])
                        acc_free[ai] = tsil
                        if cc < NZC:
                            spt = p.dma("pool", f"st{si}",
                                        xfm_scr[t * CPT:(t + 1) * CPT, :,
                                                cc * 128:(cc + 1) * 128].rearrange("c p f -> p c f"),
                                        stage_bf[si][:].rearrange("p (c f) -> p c f", f=128),
                                        waits=[tsil])
                        else:
                            spt = p.dma("pool", f"st{si}",
                                        bc_scr[t * CPT:(t + 1) * CPT, :,
                                               (cc - NZC) * 128:(cc - NZC + 1) * 128].rearrange("c p f -> p c f"),
                                        stage_bf[si][:].rearrange("p (c f) -> p c f", f=128),
                                        waits=[tsil])
                        conv_sp[(t, cc)] = spt
                        stage_free[si] = spt
                        stage_free2[si] = None
                        if cc == NZC:
                            bev = None
                            bpe = None
                            for c in range(CPT):
                                pti = c % 2
                                bpe = p.pe(lambda t_, c=c, si=si, pti=pti: t_.transpose(
                                    PT[pti][:, 0:128], stage_bf[si][:, c * 128:(c + 1) * 128],
                                    ident_b[:]), waits=[tsil, ptfree[pti]])
                                bev = p.dve(lambda v, c=c, pti=pti: v.tensor_copy(
                                    bm_tm[:, c * 128:(c + 1) * 128], PT[pti][:, 0:128]),
                                    waits=[bpe])
                                ptfree[pti] = bev
                            bmtm_tok[0] = bev
                            stage_free2[si] = bpe
                else:
                    tdd = p.dve(lambda v, r=r, q=q: v.tensor_copy(dtraw[:, q * 512:(q + 1) * 512],
                                                                  PB[r][0:64, :]),
                                waits=[mmt, t_dt_prev[0]])
                    pbfree[r] = tdd
                    dt_ev[q] = tdd
            next(p0b_gen, None)
            next(p0b_gen, None)
        w0 = list(dt_ev)
        ta = p.act(lambda s: s.activation(dts1[:], dtraw[:], AF.Relu, bias=dtb_col[:]), waits=w0)
        tb_ = p.act(lambda s: s.activation(dts2[:], dtraw[:], AF.Abs, bias=dtb_col[:]), waits=w0)
        tc = p.act(lambda s: s.activation(dts2[:], dts2[:], AF.Exp, scale=-1.0), waits=[tb_])
        tdv = p.dve(lambda v: v.tensor_scalar(dts2[:], dts2[:], 1.0, None, ALU.add), waits=[tc])
        te = p.act(lambda s: s.activation(dts2[:], dts2[:], AF.Ln), waits=[tdv])
        tf = p.dve(lambda v: v.tensor_add(dtp[:], dts1[:], dts2[:]), waits=[te, ta])
        tg = p.act(lambda s: s.activation(ldt_v[:], dtp[:], AF.Ln), waits=[tf])
        th = p.dve(lambda v: v.tensor_scalar(dA_fm[:], dtp[:], a_col[:], None, ALU.mult),
                   waits=[tf, te])
        for c in range(CPT):
            t1 = p.pe(lambda t_, c=c: t_.transpose(PB[3][0:128, 0:64],
                                                   dA_fm[:, c * 128:(c + 1) * 128],
                                                   ident_f[0:64, 0:64]),
                      waits=[th, pbfree[3]])
            t2 = p.dve(lambda v: v.tensor_copy(datm[:], PB[3][:, 0:64]), waits=[t1])
            t3 = p.pe(lambda t_: t_.matmul(PB[3][0:64, 64:192], datm[:], tlow[:], start=True,
                                           stop=True), waits=[t2])
            t4 = p.dve(lambda v, c=c: v.tensor_copy(a_fm[:, c * 128:(c + 1) * 128],
                                                    PB[3][0:64, 64:192]), waits=[t3])
            tcd = p.act(lambda s, c=c: s.activation(cd_buf[:, c:c + 1],
                                                    a_fm[:, c * 128 + 127:c * 128 + 128], AF.Exp),
                        waits=[t4])
            tu = p.act(lambda s, c=c: s.activation(u_fm[:, c * 128:(c + 1) * 128],
                                                   a_fm[:, c * 128:(c + 1) * 128], AF.Exp,
                                                   scale=-1.0,
                                                   bias=a_fm[:, c * 128 + 127:c * 128 + 128]),
                       waits=[t4])
            tu2 = p.dve(lambda v, c=c: v.tensor_mul(u_fm[:, c * 128:(c + 1) * 128],
                                                    u_fm[:, c * 128:(c + 1) * 128],
                                                    dtp[:, c * 128:(c + 1) * 128]), waits=[tu])
            t5 = p.pe(lambda t_, c=c: t_.transpose(PB[3][0:128, 192:256],
                                                   u_fm[:, c * 128:(c + 1) * 128],
                                                   ident_f[0:64, 0:64]),
                      waits=[tu2, t4])
            t6 = p.dve(lambda v, c=c: v.tensor_copy(utm_buf[:, c * 64:(c + 1) * 64],
                                                    PB[3][:, 192:256]), waits=[t5])
            t7 = p.pe(lambda t_, c=c: t_.transpose(PB[3][0:1, 256:320], cd_buf[:, c:c + 1],
                                                   ident_f[0:64, 0:64]), waits=[tcd, t6])
            t8 = p.dve(lambda v, c=c: v.tensor_copy(cdrow_buf[:, c * 64:(c + 1) * 64],
                                                    PB[3][0:1, 256:320]), waits=[t7, t_srec])
            pbfree[3] = ("dve", max(t6[1], t8[1]))
            ta1 = p.dve(lambda v, c=c: v.tensor_copy(ahi_buf[:, c * 128:(c + 1) * 128],
                                                     a_fm[:, c * 128:(c + 1) * 128]),
                        waits=[t4, t_srec])
            p.dve(lambda v, c=c: v.tensor_sub(alo_buf[:, c * 128:(c + 1) * 128],
                                              a_fm[:, c * 128:(c + 1) * 128],
                                              ahi_buf[:, c * 128:(c + 1) * 128]), waits=[ta1])
            ts1 = p.dve(lambda v, c=c: v.tensor_sub(s32[:, c * 128:(c + 1) * 128],
                                                    ldt_v[:, c * 128:(c + 1) * 128],
                                                    a_fm[:, c * 128:(c + 1) * 128]),
                        waits=[tg, t4, t_srec])
            ts2 = p.dve(lambda v, c=c: v.tensor_copy(shi_buf[:, c * 128:(c + 1) * 128],
                                                     s32[:, c * 128:(c + 1) * 128]), waits=[ts1])
            p.dve(lambda v, c=c: v.tensor_sub(slo_buf[:, c * 128:(c + 1) * 128],
                                              s32[:, c * 128:(c + 1) * 128],
                                              shi_buf[:, c * 128:(c + 1) * 128]), waits=[ts2])
        t_dt = p.tok("dve")
        p.dma("pool", "srec", srec_scr[gc0:gc0 + CPT, 0:64, :].rearrange("c h i -> h c i"),
              shi_buf[:].rearrange("h (c i) -> h c i", i=128), waits=[t_dt])
        p.dma("pool", "srec", srec_scr[gc0:gc0 + CPT, 64:128, :].rearrange("c h i -> h c i"),
              slo_buf[:].rearrange("h (c i) -> h c i", i=128), waits=[t_dt])
        p.dma("pool", "srec", aflat_scr[gc0:gc0 + CPT, 0].rearrange("c h i -> h c i"),
              ahi_buf[:].rearrange("h (c i) -> h c i", i=128), waits=[t_dt])
        p.dma("pool", "srec", aflat_scr[gc0:gc0 + CPT, 1].rearrange("c h i -> h c i"),
              alo_buf[:].rearrange("h (c i) -> h c i", i=128), waits=[t_dt])
        p.dma("pool", "srec", cd_scr[gc0:gc0 + CPT, :].rearrange("(o c) h -> o c h", o=1),
              cdrow_buf[:].rearrange("o (c h) -> o c h", h=64), waits=[t_dt])

        for c in range(CPT):
            gc = gc0 + c
            i = c % 2
            lxf = p.dma("act", f"rx{i}", xfm_c[i][:], xfm_scr[gc],
                        waits=[conv_sp[(t, NZC - 1)], conv_sp[(t, NZC - 2)], xfm_free[i]])
            pv_sp[0] = p.dma("pool", "pvw", pv_scr[gc], S_run[:], waits=[states_done[0], pv_sp[0]])
            t_xtm = None
            for g8 in range(4):
                pti = g8 % 2
                tts = None
                for k in range(8):
                    cc = g8 * 8 + k
                    tts = p.pe(lambda t_, cc=cc, k=k, pti=pti, i=i: t_.transpose(
                        PT[pti][:, k * 128:(k + 1) * 128], xfm_c[i][:, cc * 128:(cc + 1) * 128],
                        ident_b[:]), waits=[lxf, ptfree[pti]])
                t_xtm = p.dve(lambda v, g8=g8, pti=pti: v.tensor_copy(
                    xtm_c[:, g8 * 1024:(g8 + 1) * 1024], PT[pti][:]),
                    waits=[tts, xtm_free[0], xtm_sp[0]])
                ptfree[pti] = t_xtm
            xfm_free[i] = p.tok("pe")
            xtm_sp[0] = p.dma("pool", "xtw", xtm_scr[gc], xtm_c[:], waits=[t_xtm])
            txu = p.dve(lambda v, c=c: v.tensor_mul(
                xu_c[:].rearrange("p (h f) -> p h f", f=64),
                xtm_c[:].rearrange("p (h f) -> p h f", f=64),
                utm_buf[:, c * 64:(c + 1) * 64].rearrange("p (h f) -> p h f", f=1
                                                          ).broadcast_to((128, 64, 64))),
                waits=[t_xtm, t_dt])
            xtm_free[0] = txu
            tc1 = p.pe(lambda t_, c=c: t_.matmul(PB[3][:, 320:384], ones_row_f[:],
                                                 cdrow_buf[0:1, c * 64:(c + 1) * 64],
                                                 start=True, stop=True), waits=[t_dt, pbfree[3]])
            tc2 = p.dve(lambda v: v.tensor_copy(cdb_f[:], PB[3][:, 320:384]), waits=[tc1])
            pbfree[3] = tc2
            sfin = None
            for grp in range(8):
                r = 2 + grp % 4
                tmm = p.pe(lambda t_, grp=grp, c=c, r=r: t_.matmul(
                    PB[r][:], bm_tm[:, c * 128:(c + 1) * 128], xu_c[:, grp * 512:(grp + 1) * 512],
                    start=True, stop=True), waits=[txu, bmtm_tok[0], pbfree[r]])
                tsc_ = p.dve(lambda v, grp=grp: v.tensor_mul(
                    S_run[:, grp * 512:(grp + 1) * 512].rearrange("p (h f) -> p h f", f=64),
                    S_run[:, grp * 512:(grp + 1) * 512].rearrange("p (h f) -> p h f", f=64),
                    cdb_f[:, grp * 8:(grp + 1) * 8].rearrange(
                        "p (h o) -> p h o", o=1).broadcast_to((128, 8, 64))),
                    waits=[tc2, pv_sp[0]])
                sfin = p.dve(lambda v, grp=grp, r=r: v.tensor_add(
                    S_run[:, grp * 512:(grp + 1) * 512],
                    S_run[:, grp * 512:(grp + 1) * 512], PB[r][:]),
                    waits=[tsc_, tmm])
                pbfree[r] = sfin
            states_done[0] = sfin
        t_dt_prev[0] = p.tok("dve")

    p0b_gen = p0b_gen_fn()
    for t in range(NTIL):
        t_tile_guard[0] = p.tok("pe")
        pass1_tile(t)
    for _ in p0b_gen:
        pass

    tag = p.dma("pool", "ag", ag_in, S_run[:], waits=[states_done[0]])
    p.coll(lambda g: g.collective_compute(
        "AllGather", ALU.bypass, replica_groups=cfg["pairs"], ins=[ag_in], outs=[ag_out]),
        waits=[tag])
    p.barrier()
    sc1.close()

    sc2 = ExitStack()
    sb2 = mksb(sc2)
    PBX = [sc2.enter_context(nc.psum_tensor(f"pbx{i}", [128, 512], F32)) for i in range(2)]
    hseld = sb2("hseld", (128, 8192), BF16)
    fpad = [sb2("fpad0", (128, 8192), BF16)]
    ones128 = sb2("ones128", (128, 128), BF16)
    sq_acc = [sb2(f"sq_acc{i}", (128, 512)) for i in range(2)]
    S_init = sb2("S_init", (128, DI), BF16)
    xfm_c3 = [sb2("xfm_c30", (128, DI), BF16)] * 2
    d_grp = sb2("d_grp", (128, 32))
    cd_tile = sb2("cd_tile", (1, (CPT + 1) * 64))
    z_c = [sb2(f"z_c{i}", (128, DI), BF16) for i in range(2)]
    xtm_c2 = [sb2(f"xtm_c2{i}", (128, DI), BF16) for i in range(2)]
    pvr_c = [sb2("pvr_c0", (128, DI), BF16)] * 2
    bc_c = [sb2(f"bc_c{i}", (128, 256), BF16) for i in range(2)]
    srec_c = [sb2(f"srec_c{i}", (128, 128), BF16) for i in range(2)]
    cpall = sb2("cpall", (1, CPT * 64))
    cpb_all = sb2("cpb_all", (128, CPT * 64), BF16)
    g_rep = sb2("g_rep", (128, 512), BF16)
    m_sb = [sb2(f"m_sb{i}", (128, 512), BF16) for i in range(4)]
    e_sb = [sb2(f"e_sb{i}", (128, 512), BF16) for i in range(2)]
    yoff_sb = [sb2(f"yoff_sb{i}", (128, 512), BF16) for i in range(2)]
    gt1 = [sb2(f"gt1_{i}", (128, 512)) for i in range(2)]
    gt2 = [sb2(f"gt2_{i}", (128, 512), BF16) for i in range(2)]
    zs_sb = [sb2(f"zs_sb{i}", (128, 512), BF16) for i in range(2)]
    pcor = sb2("pcor", (128, 512), BF16)
    sq_sb = [sb2("sq_sb0", (128, 512))] * 2
    sq_free = [None, None]
    sqa_free = [None, None]
    rms_pend = []
    yn_q = sb2("yn_q", (128, NZC * HLF), BF16)
    rms2 = sb2("rms2", (1, 128))
    rstd_cols = sb2("rstd_cols", (128, CPT))
    oslab = [sb2(f"oslab{i}", (128, 32 * 128), BF16) for i in range(2)]
    o_ev = sb2("o_ev", (128, HLF))
    o_tbs = sb2("o_tbs", (128, CPH * DM))
    xres = sb2("xres", (128, DM))
    lntb2 = sb2("lntb2", (128, DM), BF16)
    sq_scr2 = sb2("sq_scr2", (128, DM), BF16)

    lhs = p.dma("sync", "setup", hseld[:], c_hseld)
    p.dve(lambda v: v.memset(fpad[0][:], 0.0))
    p.dve(lambda v: v.memset(ones128[:], 1.0))
    lds = p.dma("pool", "ag", S_init[:], ag_out[0:128, :], waits=[p.tok("coll")])
    tsi = p.dve(lambda v: v.tensor_scalar(S_init[:], S_init[:], smask_c[:, 0:1], None, ALU.mult),
                waits=[lds])
    for pr in range(32):
        p.dve(lambda v, pr=pr: v.tensor_copy(d_grp[0:64, pr:pr + 1], db_full[0:64, 2 * pr:2 * pr + 1]))
        p.dve(lambda v, pr=pr: v.tensor_copy(d_grp[64:128, pr:pr + 1],
                                             db_full[64:128, 2 * pr + 1:2 * pr + 2]))
    p.barrier()

    rl_free = {}
    m_tok = [None] * 4
    e_free = [None, None]
    yo_free = [None, None]
    yoffb_free = [None, None]
    ymm_free = [None, None]
    gt_free = [None, None]
    pcor_free = [None]
    ynf = [None]
    oslab_free = [None, None]
    epi_free = [None]
    lnt2_free = [None]
    xres_free = [None]

    ld_pend = {}
    cpg = [None]

    def emit_reloads(t, c):
        gc = t * CPT + c
        i = c % 2
        lz = p.dma("pool", f"rz{i}", z_c[i][:], z_scr[gc], waits=[rl_free.get(("z", i))])
        lx = p.dma("act", "rxf", xfm_c3[0][:], xfm_scr[gc], waits=[rl_free.get(("xf", 0))])
        lt = p.dma("act", f"rxt{i}", xtm_c2[i][:], xtm_scr[gc], waits=[rl_free.get(("xt", i))])
        lp = p.dma("act", "rpv", pvr_c[0][:], pv_scr[gc], waits=[rl_free.get(("pv", 0))])
        lb = p.dma("pool", f"rbc{i}", bc_c[i][:], bc_scr[gc], waits=[rl_free.get(("bc", i))])
        lf = p.dma("act", "rfl", fpad[0][0:2, :],
                   aflat_scr[gc].rearrange("a h f -> a (h f)"),
                   waits=[rl_free.get(("fl", 0))])
        lr = p.dma("pool", f"rsr{i}", srec_c[i][:], srec_scr[gc], waits=[rl_free.get(("sr", i))])
        ld_pend[gc] = (lz, lx, lt, lp, lb, lf, lr)

    def pass2_attn_tile(t):
        gc0 = t * CPT
        if gc0 > 0:
            ldc = p.dma("pool", "rcd", cd_tile[:].rearrange("o (c h) -> o c h", h=64),
                        cd_scr[gc0 - 1:gc0 + CPT, :].rearrange("(o c) h -> o c h", o=1),
                        waits=[rl_free.get(("cd", 0))])
        else:
            ldc = p.dma("pool", "rcd", cd_tile[0:1, 64:].rearrange("o (c h) -> o c h", h=64),
                        cd_scr[0:CPT, :].rearrange("(o c) h -> o c h", o=1),
                        waits=[rl_free.get(("cd", 0))])
        tprev = None
        for c in range(CPT):
            if t == 0 and c == 0:
                tprev = p.dve(lambda v: v.tensor_copy(cpall[:, 0:64], cp_row[:]), waits=[cpg[0]])
            else:
                srcv = cp_row[:] if c == 0 else cpall[:, (c - 1) * 64:c * 64]
                tprev = p.dve(lambda v, c=c, srcv=srcv: v.tensor_mul(
                    cpall[:, c * 64:(c + 1) * 64], srcv, cd_tile[0:1, c * 64:(c + 1) * 64]),
                    waits=[ldc, cpg[0]])
        tcpr = p.dve(lambda v: v.tensor_copy(cp_row[:], cpall[:, (CPT - 1) * 64:CPT * 64]),
                     waits=[tprev])
        for off in range(0, CPT * 64, 512):
            w = min(512, CPT * 64 - off)
            cpmm = p.pe(lambda t_, off=off, w=w: t_.matmul(
                PB[3][:, 0:w], ones_row_f[:], cpall[0:1, off:off + w],
                start=True, stop=True), waits=[tprev, pbfree[3]])
            pbfree[3] = p.dve(lambda v, off=off, w=w: v.tensor_copy(
                cpb_all[:, off:off + w], PB[3][:, 0:w]), waits=[cpmm, cpg[0]])
        cpg[0] = pbfree[3]
        if gc0 == 0:
            emit_reloads(0, 0)
        for qh in range(NQH):
            yn_last = None
            for ci in range(CPH):
                c = qh * CPH + ci
                gc = t * CPT + c
                i = c % 2
                lz, lx, lt, lp, lb, lf, lr = ld_pend.pop(gc)

                def corr_grp(grp):
                    t1_ = p.dve(lambda v, grp=grp, c=c: v.tensor_mul(
                        pcor[:].rearrange("p (h f) -> p h f", f=64),
                        S_init[:, grp * 512:(grp + 1) * 512].rearrange("p (h f) -> p h f", f=64),
                        cpb_all[:, c * 64 + grp * 8:c * 64 + (grp + 1) * 8].rearrange(
                            "p (h o) -> p h o", o=1).broadcast_to((128, 8, 64))),
                        waits=[lp, tsi, cpg[0], pcor_free[0]])
                    tk = p.dve(lambda v, grp=grp, i=i: v.tensor_add(
                        pvr_c[i][:, grp * 512:(grp + 1) * 512],
                        pvr_c[i][:, grp * 512:(grp + 1) * 512], pcor[:]), waits=[t1_])
                    pcor_free[0] = tk
                    return tk
                tgm = p.pe(lambda t_, i=i: t_.matmul(PB[2][:, 0:128], bc_c[i][:, 0:128],
                                                     bc_c[i][:, 128:256], start=True, stop=True),
                           waits=[lb, pbfree[2]])
                tge = p.dve(lambda v: v.tensor_copy(g_rep[:, 0:128], PB[2][:, 0:128]),
                            waits=[tgm])
                pbfree[2] = tge
                tgr = p.dve(lambda v: v.tensor_copy(
                    g_rep[:, 128:512].rearrange("p (a f) -> p a f", f=128),
                    g_rep[:, 0:128].rearrange("p (a f) -> p a f", a=1).broadcast_to((128, 3, 128))),
                    waits=[tge])
                def mbuild(g8):
                    for s4 in range(2):
                        g4 = g8 * 2 + s4
                        r = g4 % 2
                        mi = g4 % 4
                        p.pe(lambda t_, r=r: t_.matmul(PB[r][:], masknegT[:], irep[:],
                                                       start=True, stop=False),
                             waits=[lf, lr, pbfree[r]], inc=False)
                        p.pe(lambda t_, g4=g4, r=r, i=i: t_.matmul(
                            PB[r][:], srec_c[i][:], hseld[:, g4 * 512:(g4 + 1) * 512],
                            start=False, stop=False), inc=False)
                        mmf = p.pe(lambda t_, g4=g4, r=r: t_.matmul(
                            PB[r][:], ones128[:], fpad[0][:, g4 * 512:(g4 + 1) * 512],
                            start=False, stop=True))
                        tex = p.act(lambda s, r=r, mi=mi: s.activation(m_sb[mi][:], PB[r][:],
                                                                      AF.Exp),
                                    waits=[mmf, m_tok[mi]])
                        pbfree[r] = tex
                        m_tok[mi] = p.dve(lambda v, mi=mi: v.tensor_mul(
                            m_sb[mi][:], m_sb[mi][:], g_rep[:]), waits=[tex, tgr])
                mbuild(0)
                while rms_pend:
                    rms_pend.pop(0)()
                for g8 in range(8):
                    gi = g8 % 2
                    tpc = corr_grp(g8)
                    tsl = p.act(lambda s, g8=g8, i=i, gi=gi: s.activation(
                        zs_sb[gi][:], z_c[i][:, g8 * 512:(g8 + 1) * 512], AF.Silu),
                        waits=[lz, gt_free[gi]])
                    tc2 = p.pool(lambda g, g8=g8, gi=gi: g.tensor_mul(
                        gt2[gi][:].rearrange("p (a f) -> p a f", f=128),
                        xfm_c3[i][:, g8 * 512:(g8 + 1) * 512].rearrange("p (a f) -> p a f", f=128),
                        d_grp[:, g8 * 4:(g8 + 1) * 4].rearrange("p (a o) -> p a o", o=1
                                                                ).broadcast_to((128, 4, 128))),
                        waits=[lx, gt_free[gi]])
                    fpv = fpad[0][:].rearrange("p (h f) -> p h f", f=128)
                    mmtop = p.pe(lambda t_, g8=g8, fpv=fpv: t_.matmul(
                        PB[2][0:64, :], ones128[:, 0:64], fpv[:, 8 * g8:8 * g8 + 8:2, :],
                        start=True, stop=True), waits=[lf, pbfree[2]])
                    mm3 = p.pe(lambda t_, g8=g8, fpv=fpv: t_.matmul(
                        PB[2][64:128, :], ones128[:, 0:64], fpv[:, 8 * g8 + 1:8 * g8 + 8:2, :],
                        start=True, stop=True, tile_position=(0, 64)))
                    tee = p.act(lambda s, gi=gi: s.activation(e_sb[gi][:], PB[2][:], AF.Exp),
                                waits=[mmtop, mm3, e_free[gi]])
                    pbfree[2] = tee
                    yob = PB[3] if gi == 0 else PBX[1]
                    yo = None
                    for pl in range(4):
                        pr = g8 * 4 + pl
                        yo = p.pe(lambda t_, pr=pr, pl=pl, i=i, yob=yob: t_.matmul(
                            yob[:, pl * 128:(pl + 1) * 128], pvr_c[i][:, pr * 128:(pr + 1) * 128],
                            bc_c[i][:, 128:256], start=True, stop=True),
                            waits=[tpc, yo_free[gi]] + ([pbfree[3]] if gi == 0 else []))
                    tyo = p.dve(lambda v, gi=gi, yob=yob: v.tensor_mul(
                        yoff_sb[gi][:], yob[:], e_sb[gi][:]),
                        waits=[yo, tee, yoffb_free[gi]])
                    yo_free[gi] = tyo
                    e_free[gi] = tyo
                    if gi == 0:
                        pbfree[3] = tyo
                    if g8 + 1 < 8:
                        mbuild(g8 + 1)
                    ymb = PB[4] if gi == 0 else PBX[0]
                    ymm = None
                    for k in range(8):
                        h = g8 * 8 + k
                        pl = k // 2
                        hf_ = k % 2
                        s4 = 0 if k < 4 else 1
                        mi = (g8 * 2 + s4) % 4
                        hl = k % 4
                        ymm = p.pe(lambda t_, h=h, pl=pl, hf_=hf_, mi=mi, hl=hl, ymb=ymb, i=i: t_.matmul(
                            ymb[hf_ * 64:(hf_ + 1) * 64, pl * 128:(pl + 1) * 128],
                            xtm_c2[i][:, h * 64:(h + 1) * 64], m_sb[mi][:, hl * 128:(hl + 1) * 128],
                            start=True, stop=True, tile_position=(0, hf_ * 64)),
                            waits=[lt, m_tok[mi], ymm_free[gi]])
                    tc1 = p.dve(lambda v, gi=gi, ymb=ymb: v.tensor_add(
                        gt1[gi][:], ymb[:], yoff_sb[gi][:]), waits=[ymm, tyo])
                    ymm_free[gi] = tc1
                    yoffb_free[gi] = tc1
                    tc3 = p.dve(lambda v, gi=gi: v.tensor_add(gt1[gi][:], gt1[gi][:], gt2[gi][:]),
                                waits=[tc1, tc2])
                    ynv = yn_q[:].rearrange("p (cc tok) -> p cc tok", tok=HLF)[
                        :, g8 * 4:(g8 + 1) * 4, ci * 128:(ci + 1) * 128]
                    tc4 = p.dve(lambda v, ynv=ynv, gi=gi: v.tensor_mul(
                        ynv, gt1[gi][:].rearrange("p (a f) -> p a f", f=128),
                        zs_sb[gi][:].rearrange("p (a f) -> p a f", f=128)),
                        waits=[tc3, tsl, ynf[0]])
                    gt_free[gi] = tc4
                    yn_last = tc4
                    sqi = 0
                    tsq = p.act(lambda s, sqi=sqi, ynv=ynv: s.activation(
                        sq_sb[sqi][:].rearrange("p (a f) -> p a f", f=128), ynv, AF.Square),
                        waits=[tc4, sq_free[sqi]])
                    sai = c % 2
                    tacc = p.pool(lambda g, sqi=sqi, g8=g8, sai=sai: (
                        g.tensor_copy(sq_acc[sai][:], sq_sb[sqi][:]) if g8 == 0
                        else g.tensor_add(sq_acc[sai][:], sq_acc[sai][:], sq_sb[sqi][:])),
                        waits=[tsq, sqa_free[sai] if g8 == 0 else None])
                    sq_free[sqi] = tacc
                dvnow = p.tok("dve")
                for nm, ii in [("z", i), ("xf", 0), ("xt", i), ("pv", 0), ("bc", i),
                               ("fl", 0), ("sr", i)]:
                    rl_free[(nm, ii)] = dvnow
                rl_free[("cd", 0)] = dvnow
                if gc + 1 < NCHUNK:
                    emit_reloads((gc + 1) // CPT, (gc + 1) % CPT)
                def rms_emit(c=c, tacc=tacc, sai=c % 2):
                    rms_mm = p.pe(lambda t_, sai=sai: t_.matmul(
                        PB[5][0:1, :], ones_col_f[:], sq_acc[sai][:],
                        start=True, stop=True), waits=[tacc, pbfree[5]])
                    sqa_free[sai] = rms_mm
                    tr1 = p.dve(lambda v: v.tensor_reduce(
                        rms2[:], PB[5][0:1, :].rearrange("o (s i) -> o i s", s=4), AX.X, ALU.add),
                        waits=[rms_mm])
                    tr2_ = p.dve(lambda v: v.tensor_scalar(rms2[:], rms2[:], 1.0 / DI, None,
                                                           ALU.mult), waits=[tr1])
                    tr3 = p.dve(lambda v: v.tensor_scalar(rms2[:], rms2[:], 1e-5, None, ALU.add),
                                waits=[tr2_])
                    tr4 = p.act(lambda s: s.activation(rms2[:], rms2[:], AF.Sqrt), waits=[tr3])
                    tr5 = p.dve(lambda v: v.reciprocal(rms2[:], rms2[:]), waits=[tr4])
                    tr6 = p.pe(lambda t_: t_.transpose(PB[5][0:128, 0:1], rms2[:],
                                                       ident_f[0:1, 0:1]), waits=[tr5, tr1])
                    tr7 = p.dve(lambda v, c=c: v.tensor_copy(rstd_cols[:, c:c + 1], PB[5][:, 0:1]),
                                waits=[tr6])
                    pbfree[5] = tr7
                rms_pend.append(rms_emit)
            while rms_pend:
                rms_pend.pop(0)()
            ot_done = None
            for dmc in range(16):
                io = dmc % 2
                lsl = p.dma("sync", f"os{io}", oslab[io][:], wboutc[dmc],
                            waits=[oslab_free[io]])
                r = dmc % 2
                mmt = None
                for dic in range(NZC):
                    mmt = p.pe(lambda t_, dmc=dmc, dic=dic, io=io, r=r: t_.matmul(
                        PB[r][:, 0:HLF], oslab[io][:, dic * 128:(dic + 1) * 128],
                        yn_q[:, dic * HLF:(dic + 1) * HLF], start=(dic == 0),
                        stop=(dic == NZC - 1)),
                        waits=[lsl, yn_last, pbfree[r]], inc=(dic == NZC - 1))
                oslab_free[io] = mmt
                tev = p.dve(lambda v, r=r: v.tensor_copy(o_ev[:], PB[r][:, 0:HLF]), waits=[mmt])
                pbfree[r] = tev
                for tb in range(CPH):
                    rr = 2 + tb % 2
                    tt_ = p.pe(lambda t_, tb=tb, rr=rr: t_.transpose(
                        PB[rr][:, 128:256], o_ev[:, tb * 128:(tb + 1) * 128], ident_f[:]),
                        waits=[tev, pbfree[rr]])
                    ot_done = p.dve(lambda v, tb=tb, dmc=dmc, rr=rr: v.tensor_copy(
                        o_tbs[:, tb * DM + dmc * 128:tb * DM + (dmc + 1) * 128],
                        PB[rr][:, 128:256]), waits=[tt_])
                    pbfree[rr] = ot_done
            ynf[0] = p.tok("pe")
            for tb in range(CPH):
                c = qh * CPH + tb
                row0 = t * T + qh * HLF + tb * 128
                ldr = p.dma("pool", "xr", xres[:], x_main[row0:row0 + 128, :], waits=[xres_free[0]])
                tsc = p.dve(lambda v, tb=tb, c=c: v.tensor_scalar(
                    o_tbs[:, tb * DM:(tb + 1) * DM], o_tbs[:, tb * DM:(tb + 1) * DM],
                    rstd_cols[:, c:c + 1], None, ALU.mult), waits=[ot_done])
                tln = ln_block(o_tbs[:, tb * DM:(tb + 1) * DM], lntb2[:], DM, [tsc, lnt2_free[0]],
                               sq_scr2)
                tx2 = p.dve(lambda v: v.tensor_add(xres[:], xres[:], lntb2[:]), waits=[tln, ldr])
                lnt2_free[0] = tx2
                xres_free[0] = p.dma("pool", "x2w", x2_scr[row0:row0 + 128, :], xres[:], waits=[tx2])

    for t in range(NTIL):
        pass2_attn_tile(t)
    p.barrier()
    sc2.close()

    sc3 = ExitStack()
    sb3 = mksb(sc3)
    PT3 = [sc3.enter_context(nc.psum_tensor(f"pt3{i}", [128, 1024], BF16)) for i in range(2)]
    ln3_fm = sb3("ln3_fm", (128, NDC * TF), BF16)
    h1_buf = sb3("h1_buf", (128, 64 * TF), BF16)
    fslab = [sb3(f"fslab{i}", (128, DM), BF16) for i in range(2)]
    f2slab = [sb3(f"f2slab{i}", (128, 64 * 128), BF16) for i in range(2)]
    x2buf = sb3("x2buf", (128, DM))
    lntb3 = [sb3(f"lntb3{i}", (128, DM), BF16) for i in range(2)]
    o_tbs3 = sb3("o_tbs3", (128, TBF * DM))
    o_ev3 = sb3("o_ev3", (128, TF))
    b2_rep = sb3("b2_rep", (128, DM))
    sq_scr3 = sb3("sq_scr3", (128, DM), BF16)
    xr3 = x2buf

    ldb2 = p.dma("sync", "setup", x2buf[0:1, :], b2_in.rearrange("(o m) -> o m", o=1))
    for q in range(4):
        tmq = p.pe(lambda t_, q=q: t_.matmul(PB[q][:], ones_row_f[:],
                                             x2buf[0:1, q * 512:(q + 1) * 512],
                                             start=True, stop=True), waits=[ldb2, pbfree[q]])
        pbfree[q] = p.dve(lambda v, q=q: v.tensor_copy(b2_rep[:, q * 512:(q + 1) * 512],
                                                       PB[q][:]), waits=[tmq])
    p.barrier()

    fsl_free = [None, None]
    f2_free = [None, None]
    x2_free = [None]
    xr3_free = [None]
    lnt3_free = [None, None]
    ffn_guard = [None]
    out_done = [None]

    def ffn_ln3(ft):
        row0 = ft * TF
        for tb in range(TBF):
            i = tb % 2
            ldx = p.dma("sync", "x2b", x2buf[:], x2_scr[row0 + tb * 128:row0 + (tb + 1) * 128, :],
                        waits=[x2_free[0], xr3_free[0]])
            tln = ln_block(x2buf[:], lntb3[i][:], DM, [ldx, lnt3_free[i]], sq_scr3)
            x2_free[0] = tln
            for g8 in range(2):
                pti = (tb * 2 + g8) % 2
                tts = None
                for k in range(8):
                    dc = g8 * 8 + k
                    tts = p.pe(lambda t_, dc=dc, k=k, pti=pti, i=i: t_.transpose(
                        PT3[pti][:, k * 128:(k + 1) * 128], lntb3[i][:, dc * 128:(dc + 1) * 128],
                        ident_b[:]), waits=[tln, ptfree[pti]])
                tev = p.dve(lambda v, g8=g8, tb=tb, pti=pti: v.tensor_copy(
                    ln3_fm[:].rearrange("p (dc tok) -> p dc tok", tok=TF)[:, g8 * 8:(g8 + 1) * 8,
                                                                         tb * 128:(tb + 1) * 128],
                    PT3[pti][:].rearrange("p (k f) -> p k f", f=128)),
                    waits=[tts, ffn_guard[0]])
                ptfree[pti] = tev
                lnt3_free[i] = tev
        ln3_tok[0] = p.tok("dve")

    def ffn_compute(ft):
        row0 = ft * TF
        t_ln3 = ln3_tok[0]
        for fc in range(64):
            i = fc % 2
            sl = p.dma("sync", f"fs{i}", fslab[i][:], wb1[fc], waits=[fsl_free[i]])
            r = fc % 2
            mmt = None
            for dc in range(NDC):
                mmt = p.pe(lambda t_, i=i, dc=dc, r=r: t_.matmul(
                    PB[r][:, 0:TF], fslab[i][:, dc * 128:(dc + 1) * 128],
                    ln3_fm[:, dc * TF:(dc + 1) * TF], start=(dc == 0), stop=(dc == NDC - 1)),
                    waits=[sl, t_ln3, pbfree[r]], inc=(dc == NDC - 1))
            fsl_free[i] = mmt
            th1 = p.act(lambda s, fc=fc, r=r: s.activation(
                h1_buf[:, fc * TF:(fc + 1) * TF], PB[r][:, 0:TF], AF.Gelu,
                bias=b1_cols[:, fc:fc + 1]),
                waits=[mmt])
            pbfree[r] = th1
        t_h1 = p.tok("act")
        ffn_guard[0] = p.tok("pe")
        if ft + 1 < NTF:
            ffn_ln3(ft + 1)
        for dmc in range(16):
            i = dmc % 2
            lsl = p.dma("act", f"f2{i}", f2slab[i][:], wb2c[dmc],
                        waits=[f2_free[i]])
            r = 2 + dmc % 2
            mmt = None
            for fcc in range(64):
                mmt = p.pe(lambda t_, dmc=dmc, fcc=fcc, i=i, r=r: t_.matmul(
                    PB[r][:, 0:TF], f2slab[i][:, fcc * 128:(fcc + 1) * 128],
                    h1_buf[:, fcc * TF:(fcc + 1) * TF], start=(fcc == 0), stop=(fcc == 63)),
                    waits=[lsl, t_h1, pbfree[r]], inc=(fcc == 63))
            f2_free[i] = mmt
            tev = p.dve(lambda v, r=r: v.tensor_copy(o_ev3[:], PB[r][:, 0:TF]),
                        waits=[mmt])
            pbfree[r] = tev
            for tb in range(TBF):
                rr = 4 + tb % 2
                tt_ = p.pe(lambda t_, tb=tb, rr=rr: t_.transpose(
                    PB[rr][:, 128:256], o_ev3[:, tb * 128:(tb + 1) * 128], ident_f[:]),
                    waits=[tev, pbfree[rr]])
                tot = p.dve(lambda v, tb=tb, dmc=dmc, rr=rr: v.tensor_copy(
                    o_tbs3[:, tb * DM + dmc * 128:tb * DM + (dmc + 1) * 128], PB[rr][:, 128:256]),
                    waits=[tt_])
                pbfree[rr] = tot
                out_done[0] = tot
        for tb in range(TBF):
            i = tb % 2
            ldr = p.dma("sync", "x2b", xr3[:], x2_scr[row0 + tb * 128:row0 + (tb + 1) * 128, :],
                        waits=[xr3_free[0], x2_free[0]])
            tb2 = p.dve(lambda v, tb=tb: v.tensor_add(o_tbs3[:, tb * DM:(tb + 1) * DM],
                                                      o_tbs3[:, tb * DM:(tb + 1) * DM], b2_rep[:]),
                        waits=[out_done[0]])
            tln = ln_block(o_tbs3[:, tb * DM:(tb + 1) * DM], lntb3[i][:], DM,
                           [tb2, lnt3_free[i]], sq_scr3)
            tfin = p.dve(lambda v, i=i: v.tensor_add(xr3[:], xr3[:], lntb3[i][:]),
                         waits=[tln, ldr])
            lnt3_free[i] = tfin
            x2_free[0] = tfin
            xr3_free[0] = p.dma("pool", "outw", out_d[row0 + tb * 128:row0 + (tb + 1) * 128, :], xr3[:],
                                waits=[tfin])

    ln3_tok = [None]
    ffn_guard[0] = p.tok("pe")
    ffn_ln3(0)
    for ft in range(NTF):
        ffn_compute(ft)

    p.barrier()
    sc3.close()

    blk = st.enter_context(nc.Block())
    p.finalize(blk)
    st.close()
    return nc


_CACHE = {}


def _consts():
    maskneg = np.where(np.arange(128)[None, :] >= np.arange(128)[:, None], 0.0, -30000.0)
    return {
        "c_ident_b": np.eye(128, dtype=bfd),
        "c_ident_f": np.eye(128, dtype=np.float32),
        "c_ones_row": np.ones((1, 128), dtype=bfd),
        "c_ones_row_f": np.ones((1, 128), dtype=np.float32),
        "c_ones_col_f": np.ones((128, 1), dtype=np.float32),
        "c_ones_col_b": np.ones((128, 1), dtype=bfd),
        "c_masknegT": np.ascontiguousarray(maskneg.T).astype(bfd),
        "c_irep": np.tile(np.eye(128, dtype=bfd), (1, 4)),
        "c_hseld": ((np.arange(128)[:, None] % 64) == (np.arange(8192)[None, :] // 128)).astype(bfd),
        "c_tlow": np.triu(np.ones((128, 128), dtype=np.float32)),
    }


def make_in_maps(cfg, L0_attn_hidden_states, L0_attn_W_in, L0_attn_conv_w, L0_attn_conv_b,
                 L0_attn_A, L0_attn_D, L0_attn_dt_bias, L0_attn_norm_w, L0_attn_W_out,
                 L0_ffn_w1, L0_ffn_b1, L0_ffn_w2, L0_ffn_b2):
    n = cfg["n_cores"]
    SC = cfg["s_core"]
    hs = np.asarray(L0_attn_hidden_states, dtype=np.float32)
    w_in = np.asarray(L0_attn_W_in, dtype=np.float32)
    w_in_p = np.zeros((DM, DINP), dtype=np.float32)
    w_in_p[:, :w_in.shape[1]] = w_in
    w1 = np.ascontiguousarray(np.asarray(L0_ffn_w1, dtype=np.float32))
    w2 = np.ascontiguousarray(np.asarray(L0_ffn_w2, dtype=np.float32))
    w_out = np.asarray(L0_attn_W_out, dtype=np.float32)
    w_in_blk = np.ascontiguousarray(
        w_in_p.reshape(16, 128, NFC, 128).transpose(2, 1, 0, 3).reshape(NFC, 128, DM))
    w_out_blk = np.ascontiguousarray(
        w_out.reshape(32, 128, 16, 128).transpose(2, 1, 0, 3).reshape(16, 128, 4096))
    w1_blk = np.ascontiguousarray(
        w1.reshape(64, 128, 16, 128).transpose(0, 3, 2, 1).reshape(64, 128, DM))
    w2_blk = np.ascontiguousarray(
        w2.reshape(16, 128, 64, 128).transpose(0, 3, 2, 1).reshape(16, 128, 8192))
    base = dict(
        w_in_blk=w_in_blk,
        conv_w=np.ascontiguousarray(np.asarray(L0_attn_conv_w, dtype=np.float32)),
        conv_b=np.ascontiguousarray(np.asarray(L0_attn_conv_b, dtype=np.float32)),
        a_in=np.ascontiguousarray(np.asarray(L0_attn_A, dtype=np.float32)),
        d_in=np.ascontiguousarray(np.asarray(L0_attn_D, dtype=np.float32)),
        dtb_in=np.ascontiguousarray(np.asarray(L0_attn_dt_bias, dtype=np.float32)),
        nw_in=np.ascontiguousarray(np.asarray(L0_attn_norm_w, dtype=np.float32)),
        w_out_blk=w_out_blk,
        w1_blk=w1_blk, b1_in=np.ascontiguousarray(np.asarray(L0_ffn_b1, dtype=np.float32)),
        w2_blk=w2_blk, b2_in=np.ascontiguousarray(np.asarray(L0_ffn_b2, dtype=np.float32)),
    )
    base.update(_consts())
    in_maps = []
    for k in range(n):
        b_, h = k // 2, k % 2
        m = dict(base)
        m["x_main"] = np.ascontiguousarray(hs[b_, h * SC:(h + 1) * SC, :])
        m["x_halo"] = (np.zeros((3, DM), np.float32) if h == 0
                       else np.ascontiguousarray(hs[b_, SC - 3:SC, :]))
        m["c_smask"] = np.full((1, 1), 1.0 if h == 1 else 0.0, np.float32)
        m["w1s_in"] = np.ascontiguousarray(w1[k * (8192 // n):(k + 1) * (8192 // n), :])
        m["w2s_in"] = np.ascontiguousarray(w2[k * (DM // n):(k + 1) * (DM // n), :])
        in_maps.append(m)
    return in_maps


def kernel(**inputs):
    cfg = default_cfg()
    n = cfg["n_cores"]
    SC = cfg["s_core"]
    in_maps = make_in_maps(cfg, **inputs)
    if "nc" not in _CACHE:
        _CACHE["nc"] = build_program(cfg)
    res = run_bass_kernel_spmd(_CACHE["nc"], in_maps, core_ids=list(range(n)))
    nb = n // 2
    out = np.empty((nb, 2 * SC, DM), dtype=np.float32)
    for k in range(n):
        b_, h = k // 2, k % 2
        out[b_, h * SC:(h + 1) * SC, :] = np.asarray(res.results[k]["out"])
    return out


# version 11
# speedup vs baseline: 1.1047x; 1.1047x over previous
import numpy as np
import ml_dtypes
from contextlib import ExitStack
import concourse.bass as bass
import concourse.mybir as mybir
from concourse.bass_utils import run_bass_kernel_spmd

F32 = mybir.dt.float32
BF16 = mybir.dt.bfloat16
AF = mybir.ActivationFunctionType
ALU = mybir.AluOpType
AX = mybir.AxisListType
bfd = ml_dtypes.bfloat16

DM, DI, NH = 2048, 4096, 64
CONVD, DINP = 4352, 8576
NFC, NZC, NCC, NDC = 67, 32, 34, 16
ENG_OF = {"pe": "tensor", "dve": "vector", "act": "scalar", "pool": "gpsimd", "sync": "sync"}


def default_cfg():
    return dict(n_cores=8, s_core=4096, T=1024, TF=512,
                pairs=[[0, 1], [2, 3], [4, 5], [6, 7]],
                allcores=[[0, 1, 2, 3, 4, 5, 6, 7]])


class P:

    def __init__(self, nc, st):
        self.nc = nc
        self.st = st
        names = ["pe", "dve", "act", "pool", "coll", "bar"]
        self.S = {n: st.enter_context(nc.semaphore("s_" + n)) for n in names}
        self.C = {n: 0 for n in names}
        self.waited = {}
        self.thunks = {e: [] for e in ENG_OF}
        self.nbar = 0

    def _dsem(self, slot):
        name = "q_" + slot
        if name not in self.S:
            self.S[name] = self.st.enter_context(self.nc.semaphore(name))
            self.C[name] = 0
        return name

    def _waits(self, ename, toks):
        out = []
        for t in toks:
            if t is None:
                continue
            s, v = t
            if v <= 0:
                continue
            key = (ename, s)
            if self.waited.get(key, 0) >= v:
                continue
            self.waited[key] = v
            out.append((self.S[s], v))
        return out

    def _emit(self, ename, make, waits, sem, n):
        ws = self._waits(ename, waits)
        self.C[sem] += n
        semobj = self.S[sem]

        def th(e, ws=ws, make=make, semobj=semobj, n=n):
            for s, v in ws:
                e.wait_ge(s, v)
            make(e).then_inc(semobj, n)
        self.thunks[ename].append(th)
        return (sem, self.C[sem])

    def pe(self, make, waits=(), inc=True):
        if inc:
            return self._emit("pe", make, waits, "pe", 1)
        ws = self._waits("pe", waits)

        def th(e, ws=ws, make=make):
            for s, v in ws:
                e.wait_ge(s, v)
            make(e)
        self.thunks["pe"].append(th)
        return ("pe", self.C["pe"])

    def dve(self, make, waits=()):
        return self._emit("dve", make, (("dve", self.C["dve"]),) + tuple(waits), "dve", 1)

    def act(self, make, waits=()):
        return self._emit("act", make, (("act", self.C["act"]),) + tuple(waits), "act", 1)

    def pool(self, make, waits=()):
        return self._emit("pool", make, (("pool", self.C["pool"]),) + tuple(waits), "pool", 1)

    def dma(self, issuer, slot, out, in_, waits=(), **kw):
        q = self._dsem(slot)
        return self._emit(issuer, lambda e: e.dma_start(out=out, in_=in_, **kw), waits, q, 16)

    def coll(self, make, waits=()):
        return self._emit("pool", make, waits, "coll", 1)

    def tok(self, sem):
        return (sem, self.C[sem])

    def barrier(self):
        self.nbar += 1
        target = 5 * self.nbar
        snap = dict(self.C)
        dq = [k for k in snap if k.startswith("q_")] + ["coll"]
        for ename in ("sync", "act", "pool", "dve", "pe"):
            toks = [("pe", snap["pe"]), ("dve", snap["dve"]), ("act", snap["act"]),
                    ("pool", snap["pool"])] + ([(q, snap[q]) for q in dq]
                                               if ename == "sync" else [])
            ws = self._waits(ename, toks)
            self.C["bar"] += 1
            bs = self.S["bar"]

            def th(e, ws=ws, bs=bs, tgt=target):
                for s, v in ws:
                    e.wait_ge(s, v)
                e.nop().then_inc(bs, 1)
                e.wait_ge(bs, tgt)
            self.thunks[ename].append(th)
            self.waited[(ename, "bar")] = target
        for en in ENG_OF:
            for s, v in snap.items():
                self.waited[(en, s)] = max(self.waited.get((en, s), 0), v)

    def finalize(self, blk):
        for ename, ths in self.thunks.items():
            def runner(e, ths=ths):
                for th in ths:
                    th(e)
            getattr(blk, ENG_OF[ename])(runner)


class XfmShared:
    def __init__(self):
        self.v = None

    def __getitem__(self, i):
        return self.v

    def __setitem__(self, i, val):
        self.v = val


def mx(*toks):
    best = None
    for t in toks:
        if t is None:
            continue
        if best is None or (t[0] == best[0] and t[1] > best[1]):
            best = t
    return best


def build_program(cfg):
    n_cores = cfg["n_cores"]
    SC, T, TF = cfg["s_core"], cfg["T"], cfg["TF"]
    NTIL, CPT, NCHUNK = SC // T, T // 128, SC // 128
    NTF = SC // TF
    TBT, TBF = T // 128, TF // 128
    HLF = 256
    NQH = T // HLF
    CPH = HLF // 128

    nc = bass.Bass("TRN2", target_bir_lowering=False, debug=False)
    dt_ = nc.dram_tensor

    x_main = dt_("x_main", [SC, DM], F32, kind="ExternalInput").ap()
    x_halo = dt_("x_halo", [3, DM], F32, kind="ExternalInput").ap()
    w_in_blk = dt_("w_in_blk", [NFC, 128, DM], F32, kind="ExternalInput").ap()
    conv_w = dt_("conv_w", [CONVD, 4], F32, kind="ExternalInput").ap()
    conv_b = dt_("conv_b", [CONVD], F32, kind="ExternalInput").ap()
    a_in = dt_("a_in", [NH], F32, kind="ExternalInput").ap()
    d_in = dt_("d_in", [NH], F32, kind="ExternalInput").ap()
    dtb_in = dt_("dtb_in", [NH], F32, kind="ExternalInput").ap()
    nw_in = dt_("nw_in", [DI], F32, kind="ExternalInput").ap()
    w_out_blk = dt_("w_out_blk", [16, 128, 4096], F32, kind="ExternalInput").ap()
    w1_blk = dt_("w1_blk", [64, 128, DM], F32, kind="ExternalInput").ap()
    w1s_in = dt_("w1s_in", [8192 // n_cores, DM], F32, kind="ExternalInput").ap()
    b1_in = dt_("b1_in", [8192], F32, kind="ExternalInput").ap()
    w2_blk = dt_("w2_blk", [16, 128, 8192], F32, kind="ExternalInput").ap()
    w2s_in = dt_("w2s_in", [DM // n_cores, 8192], F32, kind="ExternalInput").ap()
    b2_in = dt_("b2_in", [DM], F32, kind="ExternalInput").ap()
    c_ident_b = dt_("c_ident_b", [128, 128], BF16, kind="ExternalInput").ap()
    c_ident_f = dt_("c_ident_f", [128, 128], F32, kind="ExternalInput").ap()
    c_ones_row = dt_("c_ones_row", [1, 128], BF16, kind="ExternalInput").ap()
    c_ones_row_f = dt_("c_ones_row_f", [1, 128], F32, kind="ExternalInput").ap()
    c_ones_col_f = dt_("c_ones_col_f", [128, 1], F32, kind="ExternalInput").ap()
    c_ones_col_b = dt_("c_ones_col_b", [128, 1], BF16, kind="ExternalInput").ap()
    c_masknegT = dt_("c_masknegT", [128, 128], BF16, kind="ExternalInput").ap()
    c_irep = dt_("c_irep", [128, 512], BF16, kind="ExternalInput").ap()
    c_hseld = dt_("c_hseld", [128, 8192], BF16, kind="ExternalInput").ap()
    c_tlow = dt_("c_tlow", [128, 128], F32, kind="ExternalInput").ap()
    c_smask = dt_("c_smask", [1, 1], F32, kind="ExternalInput").ap()
    out_d = dt_("out", [SC, DM], F32, kind="ExternalOutput").ap()

    wbin = dt_("wbin", [NFC, 128, DM], BF16, kind="Internal").ap()
    wboutc = dt_("wboutc", [16, 128, 4096], BF16, kind="Internal").ap()
    wb1 = dt_("wb1", [64, 128, DM], BF16, kind="Internal").ap()
    wb2c = dt_("wb2c", [16, 128, 8192], BF16, kind="Internal").ap()
    z_scr = dt_("z_scr", [NCHUNK, 128, NZC * 128], BF16, kind="Internal").ap()
    xfm_scr = dt_("xfm_scr", [NCHUNK, 128, NZC * 128], BF16, kind="Internal").ap()
    bc_scr = dt_("bc_scr", [NCHUNK, 128, 256], BF16, kind="Internal").ap()
    xtm_scr = dt_("xtm_scr", [NCHUNK, 128, DI], BF16, kind="Internal").ap()
    pv_scr = dt_("pv_scr", [NCHUNK, 128, DI], BF16, kind="Internal").ap()
    srec_scr = dt_("srec_scr", [NCHUNK, 128, 128], BF16, kind="Internal").ap()
    aflat_scr = dt_("aflat_scr", [NCHUNK, 2, 64, 128], BF16, kind="Internal").ap()
    cd_scr = dt_("cd_scr", [NCHUNK, 64], F32, kind="Internal").ap()
    x2_scr = dt_("x2_scr", [SC, DM], F32, kind="Internal").ap()
    rsum_in = dt_("rsum_in", [1, 2], F32, kind="Internal").ap()
    rsum_out = dt_("rsum_out", [1, 2], F32, kind="Internal").ap()
    ag_in = dt_("ag_in", [128, DI], F32, kind="Internal").ap()
    ag_out = dt_("ag_out", [2 * 128, DI], F32, kind="Internal").ap()

    st = ExitStack()
    p = P(nc, st)
    PB = [st.enter_context(nc.psum_tensor(f"pb{i}", [128, 512], F32)) for i in range(6)]

    def mksb(stk):
        return lambda name, shape, dt=F32: stk.enter_context(
            nc.sbuf_tensor(name, list(shape), dt))
    sbuf = mksb(st)

    ident_b = sbuf("ident_b", (128, 128), BF16)
    ident_f = sbuf("ident_f", (128, 128))
    ones_row = sbuf("ones_row", (1, 128), BF16)
    ones_row_f = sbuf("ones_row_f", (1, 128))
    ones_col_f = sbuf("ones_col_f", (128, 1))
    ones_col_b = sbuf("ones_col_b", (128, 1), BF16)
    masknegT = sbuf("masknegT", (128, 128), BF16)
    irep = sbuf("irep", (128, 512), BF16)
    tlow = sbuf("tlow", (128, 128))
    cw_cols = sbuf("cw_cols", (128, NCC * 4))
    cb_cols = sbuf("cb_cols", (128, NCC))
    a_col = sbuf("a_col", (64, 1))
    dtb_col = sbuf("dtb_col", (64, 1))
    nw_cols = sbuf("nw_cols", (128, NZC))
    b1_cols = sbuf("b1_cols", (128, 64))
    smask_c = sbuf("smask_c", (128, 1))
    rsb = sbuf("rsb", (128, 2))
    S_run = sbuf("S_run", (128, DI))
    carry = sbuf("carry", (128, NCC * 3))
    cp_row = sbuf("cp_row", (1, 64))
    drow = sbuf("drow", (1, 64))
    db_full = sbuf("db_full", (128, 64))
    rs_row = sbuf("rs_row", (1, 2))
    sums2x = sbuf("sums2x", (1, 2))
    stat = sbuf("stat", (128, 8))

    for o, i_, kw in [
        (ident_b[:], c_ident_b, {}), (ident_f[:], c_ident_f, {}),
        (ones_row[:], c_ones_row, {}), (ones_row_f[:], c_ones_row_f, {}),
        (ones_col_f[:], c_ones_col_f, {}), (ones_col_b[:], c_ones_col_b, {}),
        (masknegT[:], c_masknegT, {}), (irep[:], c_irep, {}), (tlow[:], c_tlow, {}),
        (cw_cols[:].rearrange("p (cc t) -> p cc t", t=4),
         conv_w.rearrange("(cc p) t -> p cc t", p=128), dict(allow_slow_non_contiguous=True)),
        (cb_cols[:], conv_b.rearrange("(cc p) -> p cc", p=128), dict(allow_slow_non_contiguous=True)),
        (a_col[:], a_in.rearrange("(h o) -> h o", o=1), {}),
        (dtb_col[:], dtb_in.rearrange("(h o) -> h o", o=1), {}),
        (nw_cols[:], nw_in.rearrange("(c p) -> p c", p=128), dict(allow_slow_non_contiguous=True)),
        (b1_cols[:], b1_in.rearrange("(c p) -> p c", p=128), dict(allow_slow_non_contiguous=True)),
        (smask_c[:], c_smask.broadcast_to((128, 1)), {}),
        (drow[:], d_in.rearrange("(o h) -> o h", o=1), {}),
    ]:
        p.dma("sync", "setup", o, i_, **kw)
    t_setup = p.tok("q_setup")
    td = p.pe(lambda t: t.matmul(PB[5][:, 0:64], ones_row_f[:], drow[:], start=True, stop=True),
              waits=[t_setup])
    p.dve(lambda v: v.tensor_copy(db_full[:], PB[5][:, 0:64]), waits=[td])
    p.dve(lambda v: v.memset(cp_row[:], 1.0))
    p.dve(lambda v: v.memset(S_run[:], 0.0))
    p.barrier()

    sc0 = ExitStack()
    sb0 = mksb(sc0)
    wtile = [sb0(f"p0_wt{i}", (128, DM)) for i in range(2)]
    wtb = [sb0(f"p0_wb{i}", (128, DM), BF16) for i in range(2)]
    cast_tok = [None, None]
    wr_tok = [None, None]
    for fc in range(NFC):
        i = fc % 2
        ld = p.dma("sync", f"wt{i}", wtile[i][:], w_in_blk[fc], waits=[cast_tok[i]])
        cast_tok[i] = p.dve(lambda v, i=i: v.tensor_copy(wtb[i][:], wtile[i][:]),
                            waits=[ld, wr_tok[i]])
        wr_tok[i] = p.dma("pool", f"wbin{i}", wbin[fc], wtb[i][:], waits=[cast_tok[i]])
    p.barrier()
    t_wbin = p.tok("bar")
    sc0.close()

    ln_guard = [None]

    def ln_block(xb_ap, out_ap, D, waits_in, sqv):
        Pq = xb_ap.shape[0]
        sv = stat[0:Pq, :]
        qv = sqv[0:Pq, 0:D]
        wi = list(waits_in) + [ln_guard[0]]
        tr = p.dve(lambda v: v.reduce_sum(sv[:, 0:1], xb_ap, axis=AX.X), waits=wi)
        ts_ = p.act(lambda s: s.activation(qv, xb_ap, AF.Square, accum_out=sv[:, 1:2]), waits=wi)
        p.dve(lambda v: v.tensor_scalar(sv[:, 2:3], sv[:, 0:1], 1.0 / D, None, ALU.mult), waits=[tr])
        p.dve(lambda v: v.tensor_scalar(sv[:, 3:4], sv[:, 1:2], 1.0 / D, None, ALU.mult), waits=[ts_])
        p.dve(lambda v: v.tensor_mul(sv[:, 4:5], sv[:, 2:3], sv[:, 2:3]))
        p.dve(lambda v: v.tensor_sub(sv[:, 3:4], sv[:, 3:4], sv[:, 4:5]))
        tv = p.dve(lambda v: v.tensor_scalar(sv[:, 3:4], sv[:, 3:4], 1e-5, None, ALU.add))
        tw = p.act(lambda s: s.activation(sv[:, 5:6], sv[:, 3:4], AF.Sqrt), waits=[tv])
        tx = p.dve(lambda v: v.reciprocal(sv[:, 5:6], sv[:, 5:6]), waits=[tw])
        tf = p.dve(lambda v: v.tensor_scalar(out_ap, xb_ap, sv[:, 2:3], sv[:, 5:6],
                                             ALU.subtract, ALU.mult), waits=[tx])
        ln_guard[0] = tf
        return tf

    sc1 = ExitStack()
    sb1 = mksb(sc1)
    PT = [sc1.enter_context(nc.psum_tensor(f"pt{i}", [128, 1024], BF16)) for i in range(2)]
    ln1_fm = sb1("ln1_fm", (128, NDC * T), BF16)
    xbuf = sb1("xbuf", (128, DM))
    sq_scr = sb1("sq_scr", (128, DM), BF16)
    lntb = [sb1(f"lntb{i}", (128, DM), BF16) for i in range(2)]
    slab = [sb1(f"slab{i}", (128, DM), BF16) for i in range(2)]
    raw = [sb1(f"raw{i}", (128, T + 3)) for i in range(3)]
    accv = [sb1(f"accv{i}", (128, T)) for i in range(2)]
    stage_bf = [sb1(f"stage{i}", (128, T), BF16) for i in range(2)]
    zstage = [sb1(f"zst{i}", (128, 512), BF16) for i in range(2)]
    dtraw = sb1("dtraw", (64, T))
    dts1 = sb1("dts1", (64, T))
    dts2 = sb1("dts2", (64, T))
    a_fm = sb1("a_fm", (64, T))
    u_fm = sb1("u_fm", (64, T))
    datm = sb1("datm", (128, 64))
    utm_buf = sb1("utm_buf", (128, CPT * 64))
    cd_buf = sb1("cd_buf", (64, CPT))
    cdrow_buf = sb1("cdrow_buf", (1, CPT * 64))
    shi_buf = sb1("shi_buf", (64, T), BF16)
    slo_buf = sb1("slo_buf", (64, T), BF16)
    ahi_buf = sb1("ahi_buf", (64, T), BF16)
    alo_buf = sb1("alo_buf", (64, T), BF16)
    bm_tm = sb1("bm_tm", (128, CPT * 128), BF16)
    xfm_c = [sb1("xfm_c0", (128, DI), BF16)]
    xfm_c = [xfm_c[0], xfm_c[0]]
    xtm_c = sb1("xtm_c", (128, DI), BF16)
    xu_c = sb1("xu_c", (128, DI), BF16)
    cdb_f = sb1("cdb_f", (128, 64))
    halo_fm = sb1("halo_fm", (128, NDC * 3), BF16)
    p0tile = [sb1(f"p0tile{i}", (128, 1024)) for i in range(3)]
    p0out = [sb1("p0out0", (128, 1024), BF16)]
    p0tmp = sb1("p0tmp", (128, 1024))
    p0sum = sb1("p0sum", (128, 4))

    dtp = dtraw
    ldt_v = dts1
    s32 = dts1
    dA_fm = dts2

    ld = p.dma("sync", "xin", xbuf[0:3, :], x_halo)
    thl = ln_block(xbuf[0:3, :], lntb[0][0:3, :], DM, [ld], sq_scr)
    hev = []
    for dc in range(NDC):
        tt_ = p.pe(lambda t, dc=dc: t.transpose(PT[dc % 2][:, 0:3],
                                                lntb[0][0:3, dc * 128:(dc + 1) * 128], ident_b[0:3, 0:3]),
                   waits=[thl, hev[dc - 2] if dc >= 2 else None])
        hev.append(p.dve(lambda v, dc=dc: v.tensor_copy(halo_fm[:, dc * 3:(dc + 1) * 3],
                                                        PT[dc % 2][:, 0:3]), waits=[tt_]))
    hslab_free = [None, None]
    hpb_free = [None, None]
    for k in range(NCC):
        fc = NZC + k
        i = k % 2
        sl = p.dma("sync", f"sl{i}", slab[i][:], wbin[fc], waits=[t_wbin, hslab_free[i]])
        mmt = None
        for dc in range(NDC):
            mmt = p.pe(lambda t, i=i, dc=dc: t.matmul(
                PB[i][:, 0:3], slab[i][:, dc * 128:(dc + 1) * 128], halo_fm[:, dc * 3:(dc + 1) * 3],
                start=(dc == 0), stop=(dc == NDC - 1)),
                waits=[sl, hev[dc], hpb_free[i]], inc=(dc == NDC - 1))
        hslab_free[i] = mmt
        hpb_free[i] = p.dve(lambda v, k=k, i=i: v.tensor_copy(carry[:, k * 3:(k + 1) * 3],
                                                              PB[i][:, 0:3]), waits=[mmt])
    t_carry = p.tok("dve")

    pbfree = [None] * 6
    ptfree = [None, None]
    slab_free = list(hslab_free)
    stage_free2 = [None, None]
    raw_free = [None] * 3
    acc_free = [None, None]
    stage_free = [None, None]
    zst_free = [None, None]
    lntb_free = [hev[-1], None]
    xbuf_free = [thl]
    xfm_free = XfmShared()
    xtm_free = [None]
    xtm_sp = [None]
    states_done = [None]
    pv_sp = [None]
    t_dt_prev = [None]
    conv_sp = {}
    bmtm_tok = [None]
    wr0 = [None, None, None]
    sum_free = [None, None]
    rsb_tok = [None]
    q_free = [None, None, None]
    wq_last = [None, None]
    t_tile_guard = [None]

    def p0b_gen_fn():
        for dmc in range(16):
            for qt in range(4):
                i = (dmc * 4 + qt) % 3
                oi = 0
                ld = p.dma("sync", f"p0_{i}", p0tile[i][:],
                           w_out_blk[dmc][:, qt * 1024:(qt + 1) * 1024],
                           waits=[wr0[i]])
                tq_ = p.dve(lambda v, qt=qt, i=i, oi=oi: v.tensor_tensor(
                    p0out[oi][:].rearrange("p (c m) -> p c m", m=128),
                    p0tile[i][:].rearrange("p (c m) -> p c m", m=128),
                    nw_cols[:, qt * 8:(qt + 1) * 8].rearrange(
                        "p (c o) -> p c o", o=1).broadcast_to((128, 8, 128)),
                    ALU.mult),
                    waits=[ld, wq_last[oi]])
                wr0[i] = tq_
                wq_last[oi] = p.dma("pool", f"wout{i}",
                                    wboutc[dmc][:, qt * 1024:(qt + 1) * 1024],
                                    p0out[oi][:], waits=[tq_])
                yield
        tz = p.dve(lambda v: v.memset(p0sum[:], 0.0))
        sum_free[0] = tz
        for k in range(8192 // 128 // n_cores):
            for hf in range(2):
                i = (k * 2 + hf) % 3
                ld = p.dma("sync", f"p0_{i}", p0tile[i][:],
                           w1s_in[k * 128:(k + 1) * 128, hf * 1024:(hf + 1) * 1024],
                           waits=[sum_free[0]])
                tr = p.dve(lambda v, i=i: v.reduce_sum(p0sum[:, 2:3], p0tile[i][:], axis=AX.X,
                                                       apply_absolute_value=True), waits=[ld])
                sum_free[0] = p.dve(lambda v: v.tensor_add(p0sum[:, 0:1], p0sum[:, 0:1],
                                                           p0sum[:, 2:3]), waits=[tr])
                yield
        for k in range(DM // 128 // n_cores):
            for hf in range(8):
                i = (k * 8 + hf) % 3
                ld = p.dma("sync", f"p0_{i}", p0tile[i][:],
                           w2s_in[k * 128:(k + 1) * 128, hf * 1024:(hf + 1) * 1024],
                           waits=[sum_free[0]])
                tr = p.dve(lambda v, i=i: v.reduce_sum(p0sum[:, 2:3], p0tile[i][:], axis=AX.X,
                                                       apply_absolute_value=True), waits=[ld])
                sum_free[0] = p.dve(lambda v: v.tensor_add(p0sum[:, 1:2], p0sum[:, 1:2],
                                                           p0sum[:, 2:3]), waits=[tr])
                yield
        tm1 = p.pe(lambda t_: t_.matmul(PB[3][0:1, 384:385], ones_col_f[:], p0sum[:, 0:1],
                                        start=True, stop=True), waits=[sum_free[0], pbfree[3]])
        tm2 = p.pe(lambda t_: t_.matmul(PB[3][0:1, 385:386], ones_col_f[:], p0sum[:, 1:2],
                                        start=True, stop=True))
        tv = p.dve(lambda v: v.tensor_copy(sums2x[:], PB[3][0:1, 384:386]), waits=[tm1, tm2])
        pbfree[3] = tv
        tdm = p.dma("pool", "rsum", rsum_in, sums2x[:], waits=[tv])
        tcl = p.coll(lambda g: g.collective_compute(
            "AllReduce", ALU.add, replica_groups=cfg["allcores"], ins=[rsum_in],
            outs=[rsum_out]), waits=[tdm])
        ld = p.dma("sync", "rsum2", rs_row[:], rsum_out, waits=[tcl])
        t1 = p.dve(lambda v: v.tensor_scalar(rs_row[:], rs_row[:], 1.0 / (8192.0 * DM), None,
                                             ALU.mult), waits=[ld])
        t2 = p.dve(lambda v: v.tensor_scalar(rs_row[:], rs_row[:], 1e-8, None, ALU.max),
                   waits=[t1])
        t3 = p.dve(lambda v: v.reciprocal(rs_row[:], rs_row[:]), waits=[t2])
        t4 = p.pe(lambda t_: t_.matmul(PB[3][:, 386:388], ones_row_f[:], rs_row[:],
                                       start=True, stop=True), waits=[t3, pbfree[3]])
        rsb_tok[0] = p.dve(lambda v: v.tensor_copy(rsb[:], PB[3][:, 386:388]), waits=[t4])
        pbfree[3] = rsb_tok[0]
        yield
        tmp_g = [None]
        for w2mode in (False, True):
            steps = ([(dmc, qc) for dmc in range(16) for qc in range(8)] if w2mode
                     else [(fc, qc) for fc in range(64) for qc in range(2)])
            for a, qc in steps:
                i = (a * (8 if w2mode else 2) + qc) % 3
                oi = 0
                srcw = (w2_blk[a][:, qc * 1024:(qc + 1) * 1024] if w2mode
                        else w1_blk[a][:, qc * 1024:(qc + 1) * 1024])
                dstw = (wb2c[a][:, qc * 1024:(qc + 1) * 1024] if w2mode
                        else wb1[a][:, qc * 1024:(qc + 1) * 1024])
                ld = p.dma("sync", f"p0_{i}", p0tile[i][:], srcw,
                           waits=[rsb_tok[0], q_free[i]])
                sc = 1 if w2mode else 0
                t1 = p.dve(lambda v, i=i, sc=sc: v.tensor_scalar(
                    p0tmp[:], p0tile[i][:], rsb[:, sc:sc + 1], None, ALU.mult),
                    waits=[ld, tmp_g[0]])
                t2 = p.dve(lambda v, i=i: v.tensor_scalar(
                    p0tile[i][:], p0tmp[:], 0.5, -0.5, ALU.min, ALU.max), waits=[t1])
                t3 = p.dve(lambda v, i=i: v.tensor_sub(p0tmp[:], p0tmp[:], p0tile[i][:]),
                           waits=[t2])
                q_free[i] = t3
                t4 = p.act(lambda s, oi=oi: s.activation(p0out[oi][:], p0tmp[:], AF.Sign),
                           waits=[t3, wq_last[oi]])
                tmp_g[0] = t4
                wq_last[oi] = p.dma("pool", "wq", dstw, p0out[oi][:], waits=[t4])
                yield
        yield

    def pass1_tile(t):
        gc0 = t * CPT
        t_srec = p.tok("q_srec") if "q_srec" in p.C else None
        for tb in range(TBT):
            i = tb % 2
            ldx = p.dma("sync", "xin", xbuf[:], x_main[t * T + tb * 128:t * T + (tb + 1) * 128, :],
                        waits=[xbuf_free[0]])
            tln = ln_block(xbuf[:], lntb[i][:], DM, [ldx, lntb_free[i]], sq_scr)
            xbuf_free[0] = tln
            for g8 in range(2):
                pti = (tb * 2 + g8) % 2
                tts = None
                for k in range(8):
                    dc = g8 * 8 + k
                    tts = p.pe(lambda t_, dc=dc, k=k, pti=pti, i=i: t_.transpose(
                        PT[pti][:, k * 128:(k + 1) * 128], lntb[i][:, dc * 128:(dc + 1) * 128],
                        ident_b[:]), waits=[tln, ptfree[pti]])
                tev = p.dve(lambda v, g8=g8, tb=tb, pti=pti: v.tensor_copy(
                    ln1_fm[:].rearrange("p (dc tok) -> p dc tok", tok=T)[:, g8 * 8:(g8 + 1) * 8,
                                                                        tb * 128:(tb + 1) * 128],
                    PT[pti][:].rearrange("p (k f) -> p k f", f=128)),
                    waits=[tts, t_tile_guard[0]])
                ptfree[pti] = tev
                lntb_free[i] = tev
        t_ln1 = p.tok("dve")

        NQ = T // 512
        dt_ev = [None] * NQ
        for fc in range(NFC):
            i = fc % 2
            sl = p.dma("sync", f"sl{i}", slab[i][:], wbin[fc], waits=[t_wbin, slab_free[i]])
            for q in range(NQ):
                r = (fc * NQ + q) % 3
                mmt = None
                for dc in range(NDC):
                    mmt = p.pe(lambda t_, i=i, dc=dc, q=q, r=r: t_.matmul(
                        PB[r][:], slab[i][:, dc * 128:(dc + 1) * 128],
                        ln1_fm[:, dc * T + q * 512:dc * T + (q + 1) * 512],
                        start=(dc == 0), stop=(dc == NDC - 1)),
                        waits=[sl, t_ln1, pbfree[r]], inc=(dc == NDC - 1))
                if q == NQ - 1:
                    slab_free[i] = mmt
                if fc < NZC:
                    zi = (fc * NQ + q) % 2
                    tz = p.act(lambda s, r=r, zi=zi: s.activation(zstage[zi][:], PB[r][:], AF.Copy),
                               waits=[mmt, zst_free[zi]])
                    pbfree[r] = tz
                    zst_free[zi] = p.dma("pool", f"zw{zi}",
                                         z_scr[t * CPT + q * 4:t * CPT + q * 4 + 4, :,
                                               fc * 128:(fc + 1) * 128].rearrange("c p f -> p c f"),
                                         zstage[zi][:].rearrange("p (c f) -> p c f", f=128),
                                         waits=[tz])
                elif fc < NZC + NCC:
                    cc = fc - NZC
                    ri = cc % 3
                    if q == 0:
                        p.dve(lambda v, cc=cc, ri=ri: v.tensor_copy(
                            raw[ri][:, 0:3], carry[:, cc * 3:(cc + 1) * 3]),
                            waits=[t_carry, raw_free[ri]])
                    te = p.act(lambda s, r=r, ri=ri, q=q: s.activation(
                        raw[ri][:, 3 + q * 512:3 + (q + 1) * 512], PB[r][:], AF.Copy),
                        waits=[mmt])
                    pbfree[r] = te
                    if q == NQ - 1:
                        ai = cc % 2
                        t0 = p.dve(lambda v, ri=ri, cc=cc, ai=ai: v.tensor_scalar(
                            accv[ai][:], raw[ri][:, 0:T], cw_cols[:, cc * 4:cc * 4 + 1], None,
                            ALU.mult), waits=[te, acc_free[ai]])
                        for k in range(1, 4):
                            t0 = p.dve(lambda v, ri=ri, cc=cc, ai=ai, k=k: v.scalar_tensor_tensor(
                                accv[ai][:], raw[ri][:, k:k + T],
                                cw_cols[:, cc * 4 + k:cc * 4 + k + 1],
                                accv[ai][:], ALU.mult, ALU.add), waits=[t0])
                        raw_free[ri] = p.dve(lambda v, ri=ri, cc=cc: v.tensor_copy(
                            carry[:, cc * 3:(cc + 1) * 3], raw[ri][:, T:T + 3]), waits=[te])
                        si = cc % 2
                        tsil = p.act(lambda s, ai=ai, cc=cc, si=si: s.activation(
                            stage_bf[si][:], accv[ai][:], AF.Silu, bias=cb_cols[:, cc:cc + 1]),
                            waits=[t0, stage_free[si], stage_free2[si]])
                        acc_free[ai] = tsil
                        if cc < NZC:
                            spt = p.dma("pool", f"st{si}",
                                        xfm_scr[t * CPT:(t + 1) * CPT, :,
                                                cc * 128:(cc + 1) * 128].rearrange("c p f -> p c f"),
                                        stage_bf[si][:].rearrange("p (c f) -> p c f", f=128),
                                        waits=[tsil])
                        else:
                            spt = p.dma("pool", f"st{si}",
                                        bc_scr[t * CPT:(t + 1) * CPT, :,
                                               (cc - NZC) * 128:(cc - NZC + 1) * 128].rearrange("c p f -> p c f"),
                                        stage_bf[si][:].rearrange("p (c f) -> p c f", f=128),
                                        waits=[tsil])
                        conv_sp[(t, cc)] = spt
                        stage_free[si] = spt
                        stage_free2[si] = None
                        if cc == NZC:
                            bev = None
                            bpe = None
                            for c in range(CPT):
                                pti = c % 2
                                bpe = p.pe(lambda t_, c=c, si=si, pti=pti: t_.transpose(
                                    PT[pti][:, 0:128], stage_bf[si][:, c * 128:(c + 1) * 128],
                                    ident_b[:]), waits=[tsil, ptfree[pti]])
                                bev = p.dve(lambda v, c=c, pti=pti: v.tensor_copy(
                                    bm_tm[:, c * 128:(c + 1) * 128], PT[pti][:, 0:128]),
                                    waits=[bpe])
                                ptfree[pti] = bev
                            bmtm_tok[0] = bev
                            stage_free2[si] = bpe
                else:
                    tdd = p.dve(lambda v, r=r, q=q: v.tensor_copy(dtraw[:, q * 512:(q + 1) * 512],
                                                                  PB[r][0:64, :]),
                                waits=[mmt, t_dt_prev[0]])
                    pbfree[r] = tdd
                    dt_ev[q] = tdd
            next(p0b_gen, None)
        w0 = list(dt_ev)
        ta = p.act(lambda s: s.activation(dts1[:], dtraw[:], AF.Relu, bias=dtb_col[:]), waits=w0)
        tb_ = p.act(lambda s: s.activation(dts2[:], dtraw[:], AF.Abs, bias=dtb_col[:]), waits=w0)
        tc = p.act(lambda s: s.activation(dts2[:], dts2[:], AF.Exp, scale=-1.0), waits=[tb_])
        tdv = p.dve(lambda v: v.tensor_scalar(dts2[:], dts2[:], 1.0, None, ALU.add), waits=[tc])
        te = p.act(lambda s: s.activation(dts2[:], dts2[:], AF.Ln), waits=[tdv])
        tf = p.dve(lambda v: v.tensor_add(dtp[:], dts1[:], dts2[:]), waits=[te, ta])
        tg = p.act(lambda s: s.activation(ldt_v[:], dtp[:], AF.Ln), waits=[tf])
        th = p.dve(lambda v: v.tensor_scalar(dA_fm[:], dtp[:], a_col[:], None, ALU.mult),
                   waits=[tf, te])
        for c in range(CPT):
            t1 = p.pe(lambda t_, c=c: t_.transpose(PB[3][0:128, 0:64],
                                                   dA_fm[:, c * 128:(c + 1) * 128],
                                                   ident_f[0:64, 0:64]),
                      waits=[th, pbfree[3]])
            t2 = p.dve(lambda v: v.tensor_copy(datm[:], PB[3][:, 0:64]), waits=[t1])
            t3 = p.pe(lambda t_: t_.matmul(PB[3][0:64, 64:192], datm[:], tlow[:], start=True,
                                           stop=True), waits=[t2])
            t4 = p.dve(lambda v, c=c: v.tensor_copy(a_fm[:, c * 128:(c + 1) * 128],
                                                    PB[3][0:64, 64:192]), waits=[t3])
            tcd = p.act(lambda s, c=c: s.activation(cd_buf[:, c:c + 1],
                                                    a_fm[:, c * 128 + 127:c * 128 + 128], AF.Exp),
                        waits=[t4])
            tu = p.act(lambda s, c=c: s.activation(u_fm[:, c * 128:(c + 1) * 128],
                                                   a_fm[:, c * 128:(c + 1) * 128], AF.Exp,
                                                   scale=-1.0,
                                                   bias=a_fm[:, c * 128 + 127:c * 128 + 128]),
                       waits=[t4])
            tu2 = p.dve(lambda v, c=c: v.tensor_mul(u_fm[:, c * 128:(c + 1) * 128],
                                                    u_fm[:, c * 128:(c + 1) * 128],
                                                    dtp[:, c * 128:(c + 1) * 128]), waits=[tu])
            t5 = p.pe(lambda t_, c=c: t_.transpose(PB[3][0:128, 192:256],
                                                   u_fm[:, c * 128:(c + 1) * 128],
                                                   ident_f[0:64, 0:64]),
                      waits=[tu2, t4])
            t6 = p.dve(lambda v, c=c: v.tensor_copy(utm_buf[:, c * 64:(c + 1) * 64],
                                                    PB[3][:, 192:256]), waits=[t5])
            t7 = p.pe(lambda t_, c=c: t_.transpose(PB[3][0:1, 256:320], cd_buf[:, c:c + 1],
                                                   ident_f[0:64, 0:64]), waits=[tcd, t6])
            t8 = p.dve(lambda v, c=c: v.tensor_copy(cdrow_buf[:, c * 64:(c + 1) * 64],
                                                    PB[3][0:1, 256:320]), waits=[t7, t_srec])
            pbfree[3] = ("dve", max(t6[1], t8[1]))
            ta1 = p.dve(lambda v, c=c: v.tensor_copy(ahi_buf[:, c * 128:(c + 1) * 128],
                                                     a_fm[:, c * 128:(c + 1) * 128]),
                        waits=[t4, t_srec])
            p.dve(lambda v, c=c: v.tensor_sub(alo_buf[:, c * 128:(c + 1) * 128],
                                              a_fm[:, c * 128:(c + 1) * 128],
                                              ahi_buf[:, c * 128:(c + 1) * 128]), waits=[ta1])
            ts1 = p.dve(lambda v, c=c: v.tensor_sub(s32[:, c * 128:(c + 1) * 128],
                                                    ldt_v[:, c * 128:(c + 1) * 128],
                                                    a_fm[:, c * 128:(c + 1) * 128]),
                        waits=[tg, t4, t_srec])
            ts2 = p.dve(lambda v, c=c: v.tensor_copy(shi_buf[:, c * 128:(c + 1) * 128],
                                                     s32[:, c * 128:(c + 1) * 128]), waits=[ts1])
            p.dve(lambda v, c=c: v.tensor_sub(slo_buf[:, c * 128:(c + 1) * 128],
                                              s32[:, c * 128:(c + 1) * 128],
                                              shi_buf[:, c * 128:(c + 1) * 128]), waits=[ts2])
        t_dt = p.tok("dve")
        p.dma("pool", "srec", srec_scr[gc0:gc0 + CPT, 0:64, :].rearrange("c h i -> h c i"),
              shi_buf[:].rearrange("h (c i) -> h c i", i=128), waits=[t_dt])
        p.dma("pool", "srec", srec_scr[gc0:gc0 + CPT, 64:128, :].rearrange("c h i -> h c i"),
              slo_buf[:].rearrange("h (c i) -> h c i", i=128), waits=[t_dt])
        p.dma("pool", "srec", aflat_scr[gc0:gc0 + CPT, 0].rearrange("c h i -> h c i"),
              ahi_buf[:].rearrange("h (c i) -> h c i", i=128), waits=[t_dt])
        p.dma("pool", "srec", aflat_scr[gc0:gc0 + CPT, 1].rearrange("c h i -> h c i"),
              alo_buf[:].rearrange("h (c i) -> h c i", i=128), waits=[t_dt])
        p.dma("pool", "srec", cd_scr[gc0:gc0 + CPT, :].rearrange("(o c) h -> o c h", o=1),
              cdrow_buf[:].rearrange("o (c h) -> o c h", h=64), waits=[t_dt])

        for c in range(CPT):
            gc = gc0 + c
            i = c % 2
            lxf = p.dma("act", f"rx{i}", xfm_c[i][:], xfm_scr[gc],
                        waits=[conv_sp[(t, NZC - 1)], conv_sp[(t, NZC - 2)], xfm_free[i]])
            pv_sp[0] = p.dma("pool", "pvw", pv_scr[gc], S_run[:], waits=[states_done[0], pv_sp[0]])
            t_xtm = None
            for g8 in range(4):
                pti = g8 % 2
                tts = None
                for k in range(8):
                    cc = g8 * 8 + k
                    tts = p.pe(lambda t_, cc=cc, k=k, pti=pti, i=i: t_.transpose(
                        PT[pti][:, k * 128:(k + 1) * 128], xfm_c[i][:, cc * 128:(cc + 1) * 128],
                        ident_b[:]), waits=[lxf, ptfree[pti]])
                t_xtm = p.dve(lambda v, g8=g8, pti=pti: v.tensor_copy(
                    xtm_c[:, g8 * 1024:(g8 + 1) * 1024], PT[pti][:]),
                    waits=[tts, xtm_free[0], xtm_sp[0]])
                ptfree[pti] = t_xtm
            xfm_free[i] = p.tok("pe")
            xtm_sp[0] = p.dma("pool", "xtw", xtm_scr[gc], xtm_c[:], waits=[t_xtm])
            txu = p.dve(lambda v, c=c: v.tensor_mul(
                xu_c[:].rearrange("p (h f) -> p h f", f=64),
                xtm_c[:].rearrange("p (h f) -> p h f", f=64),
                utm_buf[:, c * 64:(c + 1) * 64].rearrange("p (h f) -> p h f", f=1
                                                          ).broadcast_to((128, 64, 64))),
                waits=[t_xtm, t_dt])
            xtm_free[0] = txu
            tc1 = p.pe(lambda t_, c=c: t_.matmul(PB[3][:, 320:384], ones_row_f[:],
                                                 cdrow_buf[0:1, c * 64:(c + 1) * 64],
                                                 start=True, stop=True), waits=[t_dt, pbfree[3]])
            tc2 = p.dve(lambda v: v.tensor_copy(cdb_f[:], PB[3][:, 320:384]), waits=[tc1])
            pbfree[3] = tc2
            sfin = None
            for grp in range(8):
                r = 2 + grp % 4
                tmm = p.pe(lambda t_, grp=grp, c=c, r=r: t_.matmul(
                    PB[r][:], bm_tm[:, c * 128:(c + 1) * 128], xu_c[:, grp * 512:(grp + 1) * 512],
                    start=True, stop=True), waits=[txu, bmtm_tok[0], pbfree[r]])
                tsc_ = p.dve(lambda v, grp=grp: v.tensor_mul(
                    S_run[:, grp * 512:(grp + 1) * 512].rearrange("p (h f) -> p h f", f=64),
                    S_run[:, grp * 512:(grp + 1) * 512].rearrange("p (h f) -> p h f", f=64),
                    cdb_f[:, grp * 8:(grp + 1) * 8].rearrange(
                        "p (h o) -> p h o", o=1).broadcast_to((128, 8, 64))),
                    waits=[tc2, pv_sp[0]])
                sfin = p.dve(lambda v, grp=grp, r=r: v.tensor_add(
                    S_run[:, grp * 512:(grp + 1) * 512],
                    S_run[:, grp * 512:(grp + 1) * 512], PB[r][:]),
                    waits=[tsc_, tmm])
                pbfree[r] = sfin
            states_done[0] = sfin
            for _ in range(3):
                next(p0b_gen, None)
        t_dt_prev[0] = p.tok("dve")

    p0b_gen = p0b_gen_fn()
    for t in range(NTIL):
        t_tile_guard[0] = p.tok("pe")
        pass1_tile(t)
    for _ in p0b_gen:
        pass

    tag = p.dma("pool", "ag", ag_in, S_run[:], waits=[states_done[0]])
    p.coll(lambda g: g.collective_compute(
        "AllGather", ALU.bypass, replica_groups=cfg["pairs"], ins=[ag_in], outs=[ag_out]),
        waits=[tag])
    p.barrier()
    sc1.close()

    sc2 = ExitStack()
    sb2 = mksb(sc2)
    PBX = [sc2.enter_context(nc.psum_tensor(f"pbx{i}", [128, 512], F32)) for i in range(2)]
    hseld = sb2("hseld", (128, 8192), BF16)
    fpad = [sb2("fpad0", (128, 8192), BF16)]
    ones128 = sb2("ones128", (128, 128), BF16)
    sq_acc = [sb2(f"sq_acc{i}", (128, 512)) for i in range(2)]
    S_init = sb2("S_init", (128, DI), BF16)
    xfm_c3 = [sb2("xfm_c30", (128, DI), BF16)] * 2
    d_grp = sb2("d_grp", (128, 32))
    cd_tile = sb2("cd_tile", (1, (CPT + 1) * 64))
    z_c = [sb2(f"z_c{i}", (128, DI), BF16) for i in range(2)]
    xtm_c2 = [sb2(f"xtm_c2{i}", (128, DI), BF16) for i in range(2)]
    pvr_c = [sb2("pvr_c0", (128, DI), BF16)] * 2
    bc_c = [sb2(f"bc_c{i}", (128, 256), BF16) for i in range(2)]
    srec_c = [sb2(f"srec_c{i}", (128, 128), BF16) for i in range(2)]
    cpall = sb2("cpall", (1, CPT * 64))
    cpb_all = sb2("cpb_all", (128, CPT * 64), BF16)
    g_rep = sb2("g_rep", (128, 512), BF16)
    m_sb = [sb2(f"m_sb{i}", (128, 512), BF16) for i in range(4)]
    e_sb = [sb2(f"e_sb{i}", (128, 512), BF16) for i in range(2)]
    yoff_sb = [sb2(f"yoff_sb{i}", (128, 512), BF16) for i in range(2)]
    gt1 = [sb2(f"gt1_{i}", (128, 512)) for i in range(2)]
    gt2 = [sb2(f"gt2_{i}", (128, 512), BF16) for i in range(2)]
    zs_sb = [sb2(f"zs_sb{i}", (128, 512), BF16) for i in range(2)]
    pcor = sb2("pcor", (128, 512), BF16)
    sq_sb = [sb2("sq_sb0", (128, 512))] * 2
    sq_free = [None, None]
    sqa_free = [None, None]
    rms_pend = []
    yn_q = sb2("yn_q", (128, NZC * HLF), BF16)
    rms2 = sb2("rms2", (1, 128))
    rstd_cols = sb2("rstd_cols", (128, CPT))
    oslab = [sb2(f"oslab{i}", (128, 32 * 128), BF16) for i in range(2)]
    o_ev = sb2("o_ev", (128, HLF))
    o_tbs = sb2("o_tbs", (128, CPH * DM))
    xres = sb2("xres", (128, DM))
    lntb2 = sb2("lntb2", (128, DM), BF16)
    sq_scr2 = sb2("sq_scr2", (128, DM), BF16)

    lhs = p.dma("sync", "setup", hseld[:], c_hseld)
    p.dve(lambda v: v.memset(fpad[0][:], 0.0))
    p.dve(lambda v: v.memset(ones128[:], 1.0))
    lds = p.dma("pool", "ag", S_init[:], ag_out[0:128, :], waits=[p.tok("coll")])
    tsi = p.dve(lambda v: v.tensor_scalar(S_init[:], S_init[:], smask_c[:, 0:1], None, ALU.mult),
                waits=[lds])
    for pr in range(32):
        p.dve(lambda v, pr=pr: v.tensor_copy(d_grp[0:64, pr:pr + 1], db_full[0:64, 2 * pr:2 * pr + 1]))
        p.dve(lambda v, pr=pr: v.tensor_copy(d_grp[64:128, pr:pr + 1],
                                             db_full[64:128, 2 * pr + 1:2 * pr + 2]))
    p.barrier()

    rl_free = {}
    m_tok = [None] * 4
    e_free = [None, None]
    yo_free = [None, None]
    yoffb_free = [None, None]
    ymm_free = [None, None]
    gt_free = [None, None]
    pcor_free = [None]
    ynf = [None]
    oslab_free = [None, None]
    epi_free = [None]
    lnt2_free = [None]
    xres_free = [None]

    ld_pend = {}
    cpg = [None]

    def emit_reloads(t, c):
        gc = t * CPT + c
        i = c % 2
        lz = p.dma("pool", f"rz{i}", z_c[i][:], z_scr[gc], waits=[rl_free.get(("z", i))])
        lx = p.dma("act", "rxf", xfm_c3[0][:], xfm_scr[gc], waits=[rl_free.get(("xf", 0))])
        lt = p.dma("act", f"rxt{i}", xtm_c2[i][:], xtm_scr[gc], waits=[rl_free.get(("xt", i))])
        lp = p.dma("sync", "rpv", pvr_c[0][:], pv_scr[gc], waits=[rl_free.get(("pv", 0))])
        lb = p.dma("pool", f"rbc{i}", bc_c[i][:], bc_scr[gc], waits=[rl_free.get(("bc", i))])
        lf = p.dma("act", "rfl", fpad[0][0:2, :],
                   aflat_scr[gc].rearrange("a h f -> a (h f)"),
                   waits=[rl_free.get(("fl", 0))])
        lr = p.dma("pool", f"rsr{i}", srec_c[i][:], srec_scr[gc], waits=[rl_free.get(("sr", i))])
        ld_pend[gc] = (lz, lx, lt, lp, lb, lf, lr)

    def pass2_attn_tile(t):
        gc0 = t * CPT
        if gc0 > 0:
            ldc = p.dma("pool", "rcd", cd_tile[:].rearrange("o (c h) -> o c h", h=64),
                        cd_scr[gc0 - 1:gc0 + CPT, :].rearrange("(o c) h -> o c h", o=1),
                        waits=[rl_free.get(("cd", 0))])
        else:
            ldc = p.dma("pool", "rcd", cd_tile[0:1, 64:].rearrange("o (c h) -> o c h", h=64),
                        cd_scr[0:CPT, :].rearrange("(o c) h -> o c h", o=1),
                        waits=[rl_free.get(("cd", 0))])
        tprev = None
        for c in range(CPT):
            if t == 0 and c == 0:
                tprev = p.dve(lambda v: v.tensor_copy(cpall[:, 0:64], cp_row[:]), waits=[cpg[0]])
            else:
                srcv = cp_row[:] if c == 0 else cpall[:, (c - 1) * 64:c * 64]
                tprev = p.dve(lambda v, c=c, srcv=srcv: v.tensor_mul(
                    cpall[:, c * 64:(c + 1) * 64], srcv, cd_tile[0:1, c * 64:(c + 1) * 64]),
                    waits=[ldc, cpg[0]])
        tcpr = p.dve(lambda v: v.tensor_copy(cp_row[:], cpall[:, (CPT - 1) * 64:CPT * 64]),
                     waits=[tprev])
        for off in range(0, CPT * 64, 512):
            w = min(512, CPT * 64 - off)
            cpmm = p.pe(lambda t_, off=off, w=w: t_.matmul(
                PB[3][:, 0:w], ones_row_f[:], cpall[0:1, off:off + w],
                start=True, stop=True), waits=[tprev, pbfree[3]])
            pbfree[3] = p.dve(lambda v, off=off, w=w: v.tensor_copy(
                cpb_all[:, off:off + w], PB[3][:, 0:w]), waits=[cpmm, cpg[0]])
        cpg[0] = pbfree[3]
        if gc0 == 0:
            emit_reloads(0, 0)
        for qh in range(NQH):
            yn_last = None
            for ci in range(CPH):
                c = qh * CPH + ci
                gc = t * CPT + c
                i = c % 2
                lz, lx, lt, lp, lb, lf, lr = ld_pend.pop(gc)

                def corr_grp(grp):
                    t1_ = p.dve(lambda v, grp=grp, c=c: v.tensor_mul(
                        pcor[:].rearrange("p (h f) -> p h f", f=64),
                        S_init[:, grp * 512:(grp + 1) * 512].rearrange("p (h f) -> p h f", f=64),
                        cpb_all[:, c * 64 + grp * 8:c * 64 + (grp + 1) * 8].rearrange(
                            "p (h o) -> p h o", o=1).broadcast_to((128, 8, 64))),
                        waits=[lp, tsi, cpg[0], pcor_free[0]])
                    tk = p.dve(lambda v, grp=grp, i=i: v.tensor_add(
                        pvr_c[i][:, grp * 512:(grp + 1) * 512],
                        pvr_c[i][:, grp * 512:(grp + 1) * 512], pcor[:]), waits=[t1_])
                    pcor_free[0] = tk
                    return tk
                tgm = p.pe(lambda t_, i=i: t_.matmul(PB[2][:, 0:128], bc_c[i][:, 0:128],
                                                     bc_c[i][:, 128:256], start=True, stop=True),
                           waits=[lb, pbfree[2]])
                tge = p.dve(lambda v: v.tensor_copy(g_rep[:, 0:128], PB[2][:, 0:128]),
                            waits=[tgm])
                pbfree[2] = tge
                tgr = p.dve(lambda v: v.tensor_copy(
                    g_rep[:, 128:512].rearrange("p (a f) -> p a f", f=128),
                    g_rep[:, 0:128].rearrange("p (a f) -> p a f", a=1).broadcast_to((128, 3, 128))),
                    waits=[tge])
                def mbuild(g8):
                    for s4 in range(2):
                        g4 = g8 * 2 + s4
                        r = g4 % 2
                        mi = g4 % 4
                        p.pe(lambda t_, r=r: t_.matmul(PB[r][:], masknegT[:], irep[:],
                                                       start=True, stop=False),
                             waits=[lf, lr, pbfree[r]], inc=False)
                        p.pe(lambda t_, g4=g4, r=r, i=i: t_.matmul(
                            PB[r][:], srec_c[i][:], hseld[:, g4 * 512:(g4 + 1) * 512],
                            start=False, stop=False), inc=False)
                        mmf = p.pe(lambda t_, g4=g4, r=r: t_.matmul(
                            PB[r][:], ones128[:], fpad[0][:, g4 * 512:(g4 + 1) * 512],
                            start=False, stop=True))
                        tex = p.act(lambda s, r=r, mi=mi: s.activation(m_sb[mi][:], PB[r][:],
                                                                      AF.Exp),
                                    waits=[mmf, m_tok[mi]])
                        pbfree[r] = tex
                        m_tok[mi] = p.dve(lambda v, mi=mi: v.tensor_mul(
                            m_sb[mi][:], m_sb[mi][:], g_rep[:]), waits=[tex, tgr])
                mbuild(0)
                while rms_pend:
                    rms_pend.pop(0)()
                for g8 in range(8):
                    gi = g8 % 2
                    tpc = corr_grp(g8)
                    tsl = p.act(lambda s, g8=g8, i=i, gi=gi: s.activation(
                        zs_sb[gi][:], z_c[i][:, g8 * 512:(g8 + 1) * 512], AF.Silu),
                        waits=[lz, gt_free[gi]])
                    tc2 = p.pool(lambda g, g8=g8, gi=gi: g.tensor_mul(
                        gt2[gi][:].rearrange("p (a f) -> p a f", f=128),
                        xfm_c3[i][:, g8 * 512:(g8 + 1) * 512].rearrange("p (a f) -> p a f", f=128),
                        d_grp[:, g8 * 4:(g8 + 1) * 4].rearrange("p (a o) -> p a o", o=1
                                                                ).broadcast_to((128, 4, 128))),
                        waits=[lx, gt_free[gi]])
                    fpv = fpad[0][:].rearrange("p (h f) -> p h f", f=128)
                    mmtop = p.pe(lambda t_, g8=g8, fpv=fpv: t_.matmul(
                        PB[2][0:64, :], ones128[:, 0:64], fpv[:, 8 * g8:8 * g8 + 8:2, :],
                        start=True, stop=True), waits=[lf, pbfree[2]])
                    mm3 = p.pe(lambda t_, g8=g8, fpv=fpv: t_.matmul(
                        PB[2][64:128, :], ones128[:, 0:64], fpv[:, 8 * g8 + 1:8 * g8 + 8:2, :],
                        start=True, stop=True, tile_position=(0, 64)))
                    tee = p.act(lambda s, gi=gi: s.activation(e_sb[gi][:], PB[2][:], AF.Exp),
                                waits=[mmtop, mm3, e_free[gi]])
                    pbfree[2] = tee
                    yob = PB[3] if gi == 0 else PBX[1]
                    yo = None
                    for pl in range(4):
                        pr = g8 * 4 + pl
                        yo = p.pe(lambda t_, pr=pr, pl=pl, i=i, yob=yob: t_.matmul(
                            yob[:, pl * 128:(pl + 1) * 128], pvr_c[i][:, pr * 128:(pr + 1) * 128],
                            bc_c[i][:, 128:256], start=True, stop=True),
                            waits=[tpc, yo_free[gi]] + ([pbfree[3]] if gi == 0 else []))
                    tyo = p.dve(lambda v, gi=gi, yob=yob: v.tensor_mul(
                        yoff_sb[gi][:], yob[:], e_sb[gi][:]),
                        waits=[yo, tee, yoffb_free[gi]])
                    yo_free[gi] = tyo
                    e_free[gi] = tyo
                    if gi == 0:
                        pbfree[3] = tyo
                    if g8 + 1 < 8:
                        mbuild(g8 + 1)
                    ymb = PB[4] if gi == 0 else PBX[0]
                    ymm = None
                    for k in range(8):
                        h = g8 * 8 + k
                        pl = k // 2
                        hf_ = k % 2
                        s4 = 0 if k < 4 else 1
                        mi = (g8 * 2 + s4) % 4
                        hl = k % 4
                        ymm = p.pe(lambda t_, h=h, pl=pl, hf_=hf_, mi=mi, hl=hl, ymb=ymb, i=i: t_.matmul(
                            ymb[hf_ * 64:(hf_ + 1) * 64, pl * 128:(pl + 1) * 128],
                            xtm_c2[i][:, h * 64:(h + 1) * 64], m_sb[mi][:, hl * 128:(hl + 1) * 128],
                            start=True, stop=True, tile_position=(0, hf_ * 64)),
                            waits=[lt, m_tok[mi], ymm_free[gi]])
                    tc1 = p.dve(lambda v, gi=gi, ymb=ymb: v.tensor_add(
                        gt1[gi][:], ymb[:], yoff_sb[gi][:]), waits=[ymm, tyo])
                    ymm_free[gi] = tc1
                    yoffb_free[gi] = tc1
                    tc3 = p.dve(lambda v, gi=gi: v.tensor_add(gt1[gi][:], gt1[gi][:], gt2[gi][:]),
                                waits=[tc1, tc2])
                    ynv = yn_q[:].rearrange("p (cc tok) -> p cc tok", tok=HLF)[
                        :, g8 * 4:(g8 + 1) * 4, ci * 128:(ci + 1) * 128]
                    tc4 = p.dve(lambda v, ynv=ynv, gi=gi: v.tensor_mul(
                        ynv, gt1[gi][:].rearrange("p (a f) -> p a f", f=128),
                        zs_sb[gi][:].rearrange("p (a f) -> p a f", f=128)),
                        waits=[tc3, tsl, ynf[0]])
                    gt_free[gi] = tc4
                    yn_last = tc4
                    sqi = 0
                    tsq = p.act(lambda s, sqi=sqi, ynv=ynv: s.activation(
                        sq_sb[sqi][:].rearrange("p (a f) -> p a f", f=128), ynv, AF.Square),
                        waits=[tc4, sq_free[sqi]])
                    sai = c % 2
                    tacc = p.pool(lambda g, sqi=sqi, g8=g8, sai=sai: (
                        g.tensor_copy(sq_acc[sai][:], sq_sb[sqi][:]) if g8 == 0
                        else g.tensor_add(sq_acc[sai][:], sq_acc[sai][:], sq_sb[sqi][:])),
                        waits=[tsq, sqa_free[sai] if g8 == 0 else None])
                    sq_free[sqi] = tacc
                dvnow = p.tok("dve")
                for nm, ii in [("z", i), ("xf", 0), ("xt", i), ("pv", 0), ("bc", i),
                               ("fl", 0), ("sr", i)]:
                    rl_free[(nm, ii)] = dvnow
                rl_free[("cd", 0)] = dvnow
                if gc + 1 < NCHUNK:
                    emit_reloads((gc + 1) // CPT, (gc + 1) % CPT)

                def rms_emit(c=c, tacc=tacc, sai=c % 2):
                    rms_mm = p.pe(lambda t_, sai=sai: t_.matmul(
                        PB[5][0:1, :], ones_col_f[:], sq_acc[sai][:],
                        start=True, stop=True), waits=[tacc, pbfree[5]])
                    sqa_free[sai] = rms_mm
                    tr1 = p.dve(lambda v: v.tensor_reduce(
                        rms2[:], PB[5][0:1, :].rearrange("o (s i) -> o i s", s=4), AX.X, ALU.add),
                        waits=[rms_mm])
                    tr2_ = p.dve(lambda v: v.tensor_scalar(rms2[:], rms2[:], 1.0 / DI, None,
                                                           ALU.mult), waits=[tr1])
                    tr3 = p.dve(lambda v: v.tensor_scalar(rms2[:], rms2[:], 1e-5, None, ALU.add),
                                waits=[tr2_])
                    tr4 = p.act(lambda s: s.activation(rms2[:], rms2[:], AF.Sqrt), waits=[tr3])
                    tr5 = p.dve(lambda v: v.reciprocal(rms2[:], rms2[:]), waits=[tr4])
                    tr6 = p.pe(lambda t_: t_.transpose(PB[5][0:128, 0:1], rms2[:],
                                                       ident_f[0:1, 0:1]), waits=[tr5, tr1])
                    tr7 = p.dve(lambda v, c=c: v.tensor_copy(rstd_cols[:, c:c + 1], PB[5][:, 0:1]),
                                waits=[tr6])
                    pbfree[5] = tr7
                rms_pend.append(rms_emit)
            while rms_pend:
                rms_pend.pop(0)()
            ot_done = None
            for dmc in range(16):
                io = dmc % 2
                lsl = p.dma("sync", f"os{io}", oslab[io][:], wboutc[dmc],
                            waits=[oslab_free[io]])
                r = dmc % 2
                mmt = None
                for dic in range(NZC):
                    mmt = p.pe(lambda t_, dmc=dmc, dic=dic, io=io, r=r: t_.matmul(
                        PB[r][:, 0:HLF], oslab[io][:, dic * 128:(dic + 1) * 128],
                        yn_q[:, dic * HLF:(dic + 1) * HLF], start=(dic == 0),
                        stop=(dic == NZC - 1)),
                        waits=[lsl, yn_last, pbfree[r]], inc=(dic == NZC - 1))
                oslab_free[io] = mmt
                tev = p.dve(lambda v, r=r: v.tensor_copy(o_ev[:], PB[r][:, 0:HLF]), waits=[mmt])
                pbfree[r] = tev
                for tb in range(CPH):
                    rr = 2 + tb % 2
                    tt_ = p.pe(lambda t_, tb=tb, rr=rr: t_.transpose(
                        PB[rr][:, 128:256], o_ev[:, tb * 128:(tb + 1) * 128], ident_f[:]),
                        waits=[tev, pbfree[rr]])
                    ot_done = p.dve(lambda v, tb=tb, dmc=dmc, rr=rr: v.tensor_copy(
                        o_tbs[:, tb * DM + dmc * 128:tb * DM + (dmc + 1) * 128],
                        PB[rr][:, 128:256]), waits=[tt_])
                    pbfree[rr] = ot_done
            ynf[0] = p.tok("pe")
            for tb in range(CPH):
                c = qh * CPH + tb
                row0 = t * T + qh * HLF + tb * 128
                ldr = p.dma("sync", "xr", xres[:], x_main[row0:row0 + 128, :], waits=[xres_free[0]])
                tsc = p.dve(lambda v, tb=tb, c=c: v.tensor_scalar(
                    o_tbs[:, tb * DM:(tb + 1) * DM], o_tbs[:, tb * DM:(tb + 1) * DM],
                    rstd_cols[:, c:c + 1], None, ALU.mult), waits=[ot_done])
                tln = ln_block(o_tbs[:, tb * DM:(tb + 1) * DM], lntb2[:], DM, [tsc, lnt2_free[0]],
                               sq_scr2)
                tx2 = p.dve(lambda v: v.tensor_add(xres[:], xres[:], lntb2[:]), waits=[tln, ldr])
                lnt2_free[0] = tx2
                xres_free[0] = p.dma("pool", "x2w", x2_scr[row0:row0 + 128, :], xres[:], waits=[tx2])

    for t in range(NTIL):
        pass2_attn_tile(t)
    p.barrier()
    sc2.close()

    sc3 = ExitStack()
    sb3 = mksb(sc3)
    PT3 = [sc3.enter_context(nc.psum_tensor(f"pt3{i}", [128, 1024], BF16)) for i in range(2)]
    ln3_fm = sb3("ln3_fm", (128, NDC * TF), BF16)
    h1_buf = sb3("h1_buf", (128, 64 * TF), BF16)
    fslab = [sb3(f"fslab{i}", (128, DM), BF16) for i in range(2)]
    f2slab = [sb3(f"f2slab{i}", (128, 64 * 128), BF16) for i in range(2)]
    x2buf = sb3("x2buf", (128, DM))
    lntb3 = [sb3(f"lntb3{i}", (128, DM), BF16) for i in range(2)]
    o_tbs3 = sb3("o_tbs3", (128, TBF * DM))
    o_ev3 = sb3("o_ev3", (128, TF))
    b2_rep = sb3("b2_rep", (128, DM))
    sq_scr3 = sb3("sq_scr3", (128, DM), BF16)
    xr3 = x2buf

    ldb2 = p.dma("sync", "setup", x2buf[0:1, :], b2_in.rearrange("(o m) -> o m", o=1))
    for q in range(4):
        tmq = p.pe(lambda t_, q=q: t_.matmul(PB[q][:], ones_row_f[:],
                                             x2buf[0:1, q * 512:(q + 1) * 512],
                                             start=True, stop=True), waits=[ldb2, pbfree[q]])
        pbfree[q] = p.dve(lambda v, q=q: v.tensor_copy(b2_rep[:, q * 512:(q + 1) * 512],
                                                       PB[q][:]), waits=[tmq])
    p.barrier()

    fsl_free = [None, None]
    f2_free = [None, None]
    x2_free = [None]
    xr3_free = [None]
    lnt3_free = [None, None]
    ffn_guard = [None]
    out_done = [None]

    def ffn_ln3(ft):
        row0 = ft * TF
        for tb in range(TBF):
            i = tb % 2
            ldx = p.dma("sync", "x2b", x2buf[:], x2_scr[row0 + tb * 128:row0 + (tb + 1) * 128, :],
                        waits=[x2_free[0], xr3_free[0]])
            tln = ln_block(x2buf[:], lntb3[i][:], DM, [ldx, lnt3_free[i]], sq_scr3)
            x2_free[0] = tln
            for g8 in range(2):
                pti = (tb * 2 + g8) % 2
                tts = None
                for k in range(8):
                    dc = g8 * 8 + k
                    tts = p.pe(lambda t_, dc=dc, k=k, pti=pti, i=i: t_.transpose(
                        PT3[pti][:, k * 128:(k + 1) * 128], lntb3[i][:, dc * 128:(dc + 1) * 128],
                        ident_b[:]), waits=[tln, ptfree[pti]])
                tev = p.dve(lambda v, g8=g8, tb=tb, pti=pti: v.tensor_copy(
                    ln3_fm[:].rearrange("p (dc tok) -> p dc tok", tok=TF)[:, g8 * 8:(g8 + 1) * 8,
                                                                         tb * 128:(tb + 1) * 128],
                    PT3[pti][:].rearrange("p (k f) -> p k f", f=128)),
                    waits=[tts, ffn_guard[0]])
                ptfree[pti] = tev
                lnt3_free[i] = tev
        ln3_tok[0] = p.tok("dve")

    def ffn_compute(ft):
        row0 = ft * TF
        t_ln3 = ln3_tok[0]
        for fc in range(64):
            i = fc % 2
            sl = p.dma("sync", f"fs{i}", fslab[i][:], wb1[fc], waits=[fsl_free[i]])
            r = fc % 2
            mmt = None
            for dc in range(NDC):
                mmt = p.pe(lambda t_, i=i, dc=dc, r=r: t_.matmul(
                    PB[r][:, 0:TF], fslab[i][:, dc * 128:(dc + 1) * 128],
                    ln3_fm[:, dc * TF:(dc + 1) * TF], start=(dc == 0), stop=(dc == NDC - 1)),
                    waits=[sl, t_ln3, pbfree[r]], inc=(dc == NDC - 1))
            fsl_free[i] = mmt
            th1 = p.act(lambda s, fc=fc, r=r: s.activation(
                h1_buf[:, fc * TF:(fc + 1) * TF], PB[r][:, 0:TF], AF.Gelu,
                bias=b1_cols[:, fc:fc + 1]),
                waits=[mmt])
            pbfree[r] = th1
        t_h1 = p.tok("act")
        ffn_guard[0] = p.tok("pe")
        if ft + 1 < NTF:
            ffn_ln3(ft + 1)
        for dmc in range(16):
            i = dmc % 2
            lsl = p.dma("act", f"f2{i}", f2slab[i][:], wb2c[dmc],
                        waits=[f2_free[i]])
            r = 2 + dmc % 2
            mmt = None
            for fcc in range(64):
                mmt = p.pe(lambda t_, dmc=dmc, fcc=fcc, i=i, r=r: t_.matmul(
                    PB[r][:, 0:TF], f2slab[i][:, fcc * 128:(fcc + 1) * 128],
                    h1_buf[:, fcc * TF:(fcc + 1) * TF], start=(fcc == 0), stop=(fcc == 63)),
                    waits=[lsl, t_h1, pbfree[r]], inc=(fcc == 63))
            f2_free[i] = mmt
            tev = p.dve(lambda v, r=r: v.tensor_copy(o_ev3[:], PB[r][:, 0:TF]),
                        waits=[mmt])
            pbfree[r] = tev
            for tb in range(TBF):
                rr = 4 + tb % 2
                tt_ = p.pe(lambda t_, tb=tb, rr=rr: t_.transpose(
                    PB[rr][:, 128:256], o_ev3[:, tb * 128:(tb + 1) * 128], ident_f[:]),
                    waits=[tev, pbfree[rr]])
                tot = p.dve(lambda v, tb=tb, dmc=dmc, rr=rr: v.tensor_copy(
                    o_tbs3[:, tb * DM + dmc * 128:tb * DM + (dmc + 1) * 128], PB[rr][:, 128:256]),
                    waits=[tt_])
                pbfree[rr] = tot
                out_done[0] = tot
        for tb in range(TBF):
            i = tb % 2
            ldr = p.dma("sync", "x2b", xr3[:], x2_scr[row0 + tb * 128:row0 + (tb + 1) * 128, :],
                        waits=[xr3_free[0], x2_free[0]])
            tb2 = p.dve(lambda v, tb=tb: v.tensor_add(o_tbs3[:, tb * DM:(tb + 1) * DM],
                                                      o_tbs3[:, tb * DM:(tb + 1) * DM], b2_rep[:]),
                        waits=[out_done[0]])
            tln = ln_block(o_tbs3[:, tb * DM:(tb + 1) * DM], lntb3[i][:], DM,
                           [tb2, lnt3_free[i]], sq_scr3)
            tfin = p.dve(lambda v, i=i: v.tensor_add(xr3[:], xr3[:], lntb3[i][:]),
                         waits=[tln, ldr])
            lnt3_free[i] = tfin
            x2_free[0] = tfin
            xr3_free[0] = p.dma("pool", "outw", out_d[row0 + tb * 128:row0 + (tb + 1) * 128, :], xr3[:],
                                waits=[tfin])

    ln3_tok = [None]
    ffn_guard[0] = p.tok("pe")
    ffn_ln3(0)
    for ft in range(NTF):
        ffn_compute(ft)

    p.barrier()
    sc3.close()

    blk = st.enter_context(nc.Block())
    p.finalize(blk)
    st.close()
    return nc


_CACHE = {}


def _consts():
    maskneg = np.where(np.arange(128)[None, :] >= np.arange(128)[:, None], 0.0, -30000.0)
    return {
        "c_ident_b": np.eye(128, dtype=bfd),
        "c_ident_f": np.eye(128, dtype=np.float32),
        "c_ones_row": np.ones((1, 128), dtype=bfd),
        "c_ones_row_f": np.ones((1, 128), dtype=np.float32),
        "c_ones_col_f": np.ones((128, 1), dtype=np.float32),
        "c_ones_col_b": np.ones((128, 1), dtype=bfd),
        "c_masknegT": np.ascontiguousarray(maskneg.T).astype(bfd),
        "c_irep": np.tile(np.eye(128, dtype=bfd), (1, 4)),
        "c_hseld": ((np.arange(128)[:, None] % 64) == (np.arange(8192)[None, :] // 128)).astype(bfd),
        "c_tlow": np.triu(np.ones((128, 128), dtype=np.float32)),
    }


def make_in_maps(cfg, L0_attn_hidden_states, L0_attn_W_in, L0_attn_conv_w, L0_attn_conv_b,
                 L0_attn_A, L0_attn_D, L0_attn_dt_bias, L0_attn_norm_w, L0_attn_W_out,
                 L0_ffn_w1, L0_ffn_b1, L0_ffn_w2, L0_ffn_b2):
    n = cfg["n_cores"]
    SC = cfg["s_core"]
    hs = np.asarray(L0_attn_hidden_states, dtype=np.float32)
    w_in = np.asarray(L0_attn_W_in, dtype=np.float32)
    w_in_p = np.zeros((DM, DINP), dtype=np.float32)
    w_in_p[:, :w_in.shape[1]] = w_in
    w1 = np.ascontiguousarray(np.asarray(L0_ffn_w1, dtype=np.float32))
    w2 = np.ascontiguousarray(np.asarray(L0_ffn_w2, dtype=np.float32))
    w_out = np.asarray(L0_attn_W_out, dtype=np.float32)
    w_in_blk = np.ascontiguousarray(
        w_in_p.reshape(16, 128, NFC, 128).transpose(2, 1, 0, 3).reshape(NFC, 128, DM))
    w_out_blk = np.ascontiguousarray(
        w_out.reshape(32, 128, 16, 128).transpose(2, 1, 0, 3).reshape(16, 128, 4096))
    w1_blk = np.ascontiguousarray(
        w1.reshape(64, 128, 16, 128).transpose(0, 3, 2, 1).reshape(64, 128, DM))
    w2_blk = np.ascontiguousarray(
        w2.reshape(16, 128, 64, 128).transpose(0, 3, 2, 1).reshape(16, 128, 8192))
    base = dict(
        w_in_blk=w_in_blk,
        conv_w=np.ascontiguousarray(np.asarray(L0_attn_conv_w, dtype=np.float32)),
        conv_b=np.ascontiguousarray(np.asarray(L0_attn_conv_b, dtype=np.float32)),
        a_in=np.ascontiguousarray(np.asarray(L0_attn_A, dtype=np.float32)),
        d_in=np.ascontiguousarray(np.asarray(L0_attn_D, dtype=np.float32)),
        dtb_in=np.ascontiguousarray(np.asarray(L0_attn_dt_bias, dtype=np.float32)),
        nw_in=np.ascontiguousarray(np.asarray(L0_attn_norm_w, dtype=np.float32)),
        w_out_blk=w_out_blk,
        w1_blk=w1_blk, b1_in=np.ascontiguousarray(np.asarray(L0_ffn_b1, dtype=np.float32)),
        w2_blk=w2_blk, b2_in=np.ascontiguousarray(np.asarray(L0_ffn_b2, dtype=np.float32)),
    )
    base.update(_consts())
    in_maps = []
    for k in range(n):
        b_, h = k // 2, k % 2
        m = dict(base)
        m["x_main"] = np.ascontiguousarray(hs[b_, h * SC:(h + 1) * SC, :])
        m["x_halo"] = (np.zeros((3, DM), np.float32) if h == 0
                       else np.ascontiguousarray(hs[b_, SC - 3:SC, :]))
        m["c_smask"] = np.full((1, 1), 1.0 if h == 1 else 0.0, np.float32)
        m["w1s_in"] = np.ascontiguousarray(w1[k * (8192 // n):(k + 1) * (8192 // n), :])
        m["w2s_in"] = np.ascontiguousarray(w2[k * (DM // n):(k + 1) * (DM // n), :])
        in_maps.append(m)
    return in_maps


def kernel(**inputs):
    cfg = default_cfg()
    n = cfg["n_cores"]
    SC = cfg["s_core"]
    in_maps = make_in_maps(cfg, **inputs)
    if "nc" not in _CACHE:
        _CACHE["nc"] = build_program(cfg)
    res = run_bass_kernel_spmd(_CACHE["nc"], in_maps, core_ids=list(range(n)))
    nb = n // 2
    out = np.empty((nb, 2 * SC, DM), dtype=np.float32)
    for k in range(n):
        b_, h = k // 2, k % 2
        out[b_, h * SC:(h + 1) * SC, :] = np.asarray(res.results[k]["out"])
    return out


# version 12
# speedup vs baseline: 1.1503x; 1.0412x over previous
import numpy as np
import ml_dtypes
from contextlib import ExitStack
import concourse.bass as bass
import concourse.mybir as mybir
from concourse.bass_utils import run_bass_kernel_spmd

F32 = mybir.dt.float32
BF16 = mybir.dt.bfloat16
AF = mybir.ActivationFunctionType
ALU = mybir.AluOpType
AX = mybir.AxisListType
bfd = ml_dtypes.bfloat16

DM, DI, NH = 2048, 4096, 64
CONVD, DINP = 4352, 8576
NFC, NZC, NCC, NDC = 67, 32, 34, 16
ENG_OF = {"pe": "tensor", "dve": "vector", "act": "scalar", "pool": "gpsimd", "sync": "sync"}


def default_cfg():
    return dict(n_cores=8, s_core=4096, T=1024, TF=512,
                pairs=[[0, 1], [2, 3], [4, 5], [6, 7]],
                allcores=[[0, 1, 2, 3, 4, 5, 6, 7]])


class P:

    def __init__(self, nc, st):
        self.nc = nc
        self.st = st
        names = ["pe", "dve", "act", "pool", "coll", "bar"]
        self.S = {n: st.enter_context(nc.semaphore("s_" + n)) for n in names}
        self.C = {n: 0 for n in names}
        self.waited = {}
        self.thunks = {e: [] for e in ENG_OF}
        self.nbar = 0

    def _dsem(self, slot):
        name = "q_" + slot
        if name not in self.S:
            self.S[name] = self.st.enter_context(self.nc.semaphore(name))
            self.C[name] = 0
        return name

    def _waits(self, ename, toks):
        out = []
        for t in toks:
            if t is None:
                continue
            s, v = t
            if v <= 0:
                continue
            key = (ename, s)
            if self.waited.get(key, 0) >= v:
                continue
            self.waited[key] = v
            out.append((self.S[s], v))
        return out

    def _emit(self, ename, make, waits, sem, n):
        ws = self._waits(ename, waits)
        self.C[sem] += n
        semobj = self.S[sem]

        def th(e, ws=ws, make=make, semobj=semobj, n=n):
            for s, v in ws:
                e.wait_ge(s, v)
            make(e).then_inc(semobj, n)
        self.thunks[ename].append(th)
        return (sem, self.C[sem])

    def pe(self, make, waits=(), inc=True):
        if inc:
            return self._emit("pe", make, waits, "pe", 1)
        ws = self._waits("pe", waits)

        def th(e, ws=ws, make=make):
            for s, v in ws:
                e.wait_ge(s, v)
            make(e)
        self.thunks["pe"].append(th)
        return ("pe", self.C["pe"])

    def dve(self, make, waits=()):
        return self._emit("dve", make, (("dve", self.C["dve"]),) + tuple(waits), "dve", 1)

    def act(self, make, waits=()):
        return self._emit("act", make, (("act", self.C["act"]),) + tuple(waits), "act", 1)

    def pool(self, make, waits=()):
        return self._emit("pool", make, (("pool", self.C["pool"]),) + tuple(waits), "pool", 1)

    def dma(self, issuer, slot, out, in_, waits=(), **kw):
        q = self._dsem(slot)
        return self._emit(issuer, lambda e: e.dma_start(out=out, in_=in_, **kw), waits, q, 16)

    def coll(self, make, waits=()):
        return self._emit("pool", make, waits, "coll", 1)

    def tok(self, sem):
        return (sem, self.C[sem])

    def barrier(self):
        self.nbar += 1
        target = 5 * self.nbar
        snap = dict(self.C)
        dq = [k for k in snap if k.startswith("q_")] + ["coll"]
        for ename in ("sync", "act", "pool", "dve", "pe"):
            toks = [("pe", snap["pe"]), ("dve", snap["dve"]), ("act", snap["act"]),
                    ("pool", snap["pool"])] + ([(q, snap[q]) for q in dq]
                                               if ename == "sync" else [])
            ws = self._waits(ename, toks)
            self.C["bar"] += 1
            bs = self.S["bar"]

            def th(e, ws=ws, bs=bs, tgt=target):
                for s, v in ws:
                    e.wait_ge(s, v)
                e.nop().then_inc(bs, 1)
                e.wait_ge(bs, tgt)
            self.thunks[ename].append(th)
            self.waited[(ename, "bar")] = target
        for en in ENG_OF:
            for s, v in snap.items():
                self.waited[(en, s)] = max(self.waited.get((en, s), 0), v)

    def finalize(self, blk):
        for ename, ths in self.thunks.items():
            def runner(e, ths=ths):
                for th in ths:
                    th(e)
            getattr(blk, ENG_OF[ename])(runner)


class XfmShared:
    def __init__(self):
        self.v = None

    def __getitem__(self, i):
        return self.v

    def __setitem__(self, i, val):
        self.v = val


def mx(*toks):
    best = None
    for t in toks:
        if t is None:
            continue
        if best is None or (t[0] == best[0] and t[1] > best[1]):
            best = t
    return best


def build_program(cfg):
    n_cores = cfg["n_cores"]
    SC, T, TF = cfg["s_core"], cfg["T"], cfg["TF"]
    NTIL, CPT, NCHUNK = SC // T, T // 128, SC // 128
    NTF = SC // TF
    TBT, TBF = T // 128, TF // 128
    HLF = 256
    NQH = T // HLF
    CPH = HLF // 128

    nc = bass.Bass("TRN2", target_bir_lowering=False, debug=False)
    dt_ = nc.dram_tensor

    x_main = dt_("x_main", [SC, DM], F32, kind="ExternalInput").ap()
    x_halo = dt_("x_halo", [3, DM], F32, kind="ExternalInput").ap()
    w_in_blk = dt_("w_in_blk", [NFC, 128, DM], F32, kind="ExternalInput").ap()
    conv_w = dt_("conv_w", [CONVD, 4], F32, kind="ExternalInput").ap()
    conv_b = dt_("conv_b", [CONVD], F32, kind="ExternalInput").ap()
    a_in = dt_("a_in", [NH], F32, kind="ExternalInput").ap()
    d_in = dt_("d_in", [NH], F32, kind="ExternalInput").ap()
    dtb_in = dt_("dtb_in", [NH], F32, kind="ExternalInput").ap()
    nw_in = dt_("nw_in", [DI], F32, kind="ExternalInput").ap()
    w_out_blk = dt_("w_out_blk", [16, 128, 4096], F32, kind="ExternalInput").ap()
    w1_blk = dt_("w1_blk", [64, 128, DM], F32, kind="ExternalInput").ap()
    w1s_in = dt_("w1s_in", [8192 // n_cores, DM], F32, kind="ExternalInput").ap()
    b1_in = dt_("b1_in", [8192], F32, kind="ExternalInput").ap()
    w2_blk = dt_("w2_blk", [16, 128, 8192], F32, kind="ExternalInput").ap()
    w2s_in = dt_("w2s_in", [DM // n_cores, 8192], F32, kind="ExternalInput").ap()
    b2_in = dt_("b2_in", [DM], F32, kind="ExternalInput").ap()
    c_ident_b = dt_("c_ident_b", [128, 128], BF16, kind="ExternalInput").ap()
    c_ident_f = dt_("c_ident_f", [128, 128], F32, kind="ExternalInput").ap()
    c_ones_row = dt_("c_ones_row", [1, 128], BF16, kind="ExternalInput").ap()
    c_ones_row_f = dt_("c_ones_row_f", [1, 128], F32, kind="ExternalInput").ap()
    c_ones_col_f = dt_("c_ones_col_f", [128, 1], F32, kind="ExternalInput").ap()
    c_ones_col_b = dt_("c_ones_col_b", [128, 1], BF16, kind="ExternalInput").ap()
    c_masknegT = dt_("c_masknegT", [128, 128], BF16, kind="ExternalInput").ap()
    c_irep = dt_("c_irep", [128, 512], BF16, kind="ExternalInput").ap()
    c_hseld = dt_("c_hseld", [128, 8192], BF16, kind="ExternalInput").ap()
    c_tlow = dt_("c_tlow", [128, 128], F32, kind="ExternalInput").ap()
    c_smask = dt_("c_smask", [1, 1], F32, kind="ExternalInput").ap()
    out_d = dt_("out", [SC, DM], F32, kind="ExternalOutput").ap()

    wbin = dt_("wbin", [NFC, 128, DM], BF16, kind="Internal").ap()
    wboutc = dt_("wboutc", [16, 128, 4096], BF16, kind="Internal").ap()
    wb1 = dt_("wb1", [64, 128, DM], BF16, kind="Internal").ap()
    wb2c = dt_("wb2c", [16, 128, 8192], BF16, kind="Internal").ap()
    z_scr = dt_("z_scr", [NCHUNK, 128, NZC * 128], BF16, kind="Internal").ap()
    xfm_scr = dt_("xfm_scr", [NCHUNK, 128, NZC * 128], BF16, kind="Internal").ap()
    bc_scr = dt_("bc_scr", [NCHUNK, 128, 256], BF16, kind="Internal").ap()
    xtm_scr = dt_("xtm_scr", [NCHUNK, 128, DI], BF16, kind="Internal").ap()
    pv_scr = dt_("pv_scr", [NCHUNK, 128, DI], BF16, kind="Internal").ap()
    srec_scr = dt_("srec_scr", [NCHUNK, 128, 128], BF16, kind="Internal").ap()
    aflat_scr = dt_("aflat_scr", [NCHUNK, 2, 64, 128], BF16, kind="Internal").ap()
    cd_scr = dt_("cd_scr", [NCHUNK, 64], F32, kind="Internal").ap()
    x2_scr = dt_("x2_scr", [SC, DM], F32, kind="Internal").ap()
    rsum_in = dt_("rsum_in", [1, 2], F32, kind="Internal").ap()
    rsum_out = dt_("rsum_out", [1, 2], F32, kind="Internal").ap()
    ag_in = dt_("ag_in", [128, DI], F32, kind="Internal").ap()
    ag_out = dt_("ag_out", [2 * 128, DI], F32, kind="Internal").ap()

    st = ExitStack()
    p = P(nc, st)
    PB = [st.enter_context(nc.psum_tensor(f"pb{i}", [128, 512], F32)) for i in range(6)]

    def mksb(stk):
        return lambda name, shape, dt=F32: stk.enter_context(
            nc.sbuf_tensor(name, list(shape), dt))
    sbuf = mksb(st)

    ident_b = sbuf("ident_b", (128, 128), BF16)
    ident_f = sbuf("ident_f", (128, 128))
    ones_row = sbuf("ones_row", (1, 128), BF16)
    ones_row_f = sbuf("ones_row_f", (1, 128))
    ones_col_f = sbuf("ones_col_f", (128, 1))
    ones_col_b = sbuf("ones_col_b", (128, 1), BF16)
    masknegT = sbuf("masknegT", (128, 128), BF16)
    irep = sbuf("irep", (128, 512), BF16)
    tlow = sbuf("tlow", (128, 128))
    cw_cols = sbuf("cw_cols", (128, NCC * 4))
    cb_cols = sbuf("cb_cols", (128, NCC))
    a_col = sbuf("a_col", (64, 1))
    dtb_col = sbuf("dtb_col", (64, 1))
    nw_cols = sbuf("nw_cols", (128, NZC))
    b1_cols = sbuf("b1_cols", (128, 64))
    smask_c = sbuf("smask_c", (128, 1))
    rsb = sbuf("rsb", (128, 2))
    S_run = sbuf("S_run", (128, DI))
    carry = sbuf("carry", (128, NCC * 3))
    cp_row = sbuf("cp_row", (1, 64))
    drow = sbuf("drow", (1, 64))
    db_full = sbuf("db_full", (128, 64))
    rs_row = sbuf("rs_row", (1, 2))
    sums2x = sbuf("sums2x", (1, 2))
    stat = sbuf("stat", (128, 8))

    for o, i_, kw in [
        (ident_b[:], c_ident_b, {}), (ident_f[:], c_ident_f, {}),
        (ones_row[:], c_ones_row, {}), (ones_row_f[:], c_ones_row_f, {}),
        (ones_col_f[:], c_ones_col_f, {}), (ones_col_b[:], c_ones_col_b, {}),
        (masknegT[:], c_masknegT, {}), (irep[:], c_irep, {}), (tlow[:], c_tlow, {}),
        (cw_cols[:].rearrange("p (cc t) -> p cc t", t=4),
         conv_w.rearrange("(cc p) t -> p cc t", p=128), dict(allow_slow_non_contiguous=True)),
        (cb_cols[:], conv_b.rearrange("(cc p) -> p cc", p=128), dict(allow_slow_non_contiguous=True)),
        (a_col[:], a_in.rearrange("(h o) -> h o", o=1), {}),
        (dtb_col[:], dtb_in.rearrange("(h o) -> h o", o=1), {}),
        (nw_cols[:], nw_in.rearrange("(c p) -> p c", p=128), dict(allow_slow_non_contiguous=True)),
        (b1_cols[:], b1_in.rearrange("(c p) -> p c", p=128), dict(allow_slow_non_contiguous=True)),
        (smask_c[:], c_smask.broadcast_to((128, 1)), {}),
        (drow[:], d_in.rearrange("(o h) -> o h", o=1), {}),
    ]:
        p.dma("sync", "setup", o, i_, **kw)
    t_setup = p.tok("q_setup")
    td = p.pe(lambda t: t.matmul(PB[5][:, 0:64], ones_row_f[:], drow[:], start=True, stop=True),
              waits=[t_setup])
    p.dve(lambda v: v.tensor_copy(db_full[:], PB[5][:, 0:64]), waits=[td])
    p.dve(lambda v: v.memset(cp_row[:], 1.0))
    p.dve(lambda v: v.memset(S_run[:], 0.0))
    p.barrier()

    sc0 = ExitStack()
    sb0 = mksb(sc0)
    wtile = [sb0(f"p0_wt{i}", (128, DM)) for i in range(2)]
    wtb = [sb0(f"p0_wb{i}", (128, DM), BF16) for i in range(2)]
    cast_tok = [None, None]
    wr_tok = [None, None]
    for fc in range(NFC):
        i = fc % 2
        ld = p.dma("sync", f"wt{i}", wtile[i][:], w_in_blk[fc], waits=[cast_tok[i]])
        cast_tok[i] = p.dve(lambda v, i=i: v.tensor_copy(wtb[i][:], wtile[i][:]),
                            waits=[ld, wr_tok[i]])
        wr_tok[i] = p.dma("pool", f"wbin{i}", wbin[fc], wtb[i][:], waits=[cast_tok[i]])
    p.barrier()
    t_wbin = p.tok("bar")
    sc0.close()

    ln_guard = [None]

    def ln_block(xb_ap, out_ap, D, waits_in, sqv):
        Pq = xb_ap.shape[0]
        sv = stat[0:Pq, :]
        qv = sqv[0:Pq, 0:D]
        wi = list(waits_in) + [ln_guard[0]]
        tr = p.dve(lambda v: v.reduce_sum(sv[:, 0:1], xb_ap, axis=AX.X), waits=wi)
        ts_ = p.act(lambda s: s.activation(qv, xb_ap, AF.Square, accum_out=sv[:, 1:2]), waits=wi)
        p.dve(lambda v: v.tensor_scalar(sv[:, 2:3], sv[:, 0:1], 1.0 / D, None, ALU.mult), waits=[tr])
        p.dve(lambda v: v.tensor_scalar(sv[:, 3:4], sv[:, 1:2], 1.0 / D, None, ALU.mult), waits=[ts_])
        p.dve(lambda v: v.tensor_mul(sv[:, 4:5], sv[:, 2:3], sv[:, 2:3]))
        p.dve(lambda v: v.tensor_sub(sv[:, 3:4], sv[:, 3:4], sv[:, 4:5]))
        tv = p.dve(lambda v: v.tensor_scalar(sv[:, 3:4], sv[:, 3:4], 1e-5, None, ALU.add))
        tw = p.act(lambda s: s.activation(sv[:, 5:6], sv[:, 3:4], AF.Sqrt), waits=[tv])
        tx = p.dve(lambda v: v.reciprocal(sv[:, 5:6], sv[:, 5:6]), waits=[tw])
        tf = p.dve(lambda v: v.tensor_scalar(out_ap, xb_ap, sv[:, 2:3], sv[:, 5:6],
                                             ALU.subtract, ALU.mult), waits=[tx])
        ln_guard[0] = tf
        return tf

    sc1 = ExitStack()
    sb1 = mksb(sc1)
    PT = [sc1.enter_context(nc.psum_tensor(f"pt{i}", [128, 1024], BF16)) for i in range(2)]
    ln1_fm = sb1("ln1_fm", (128, NDC * T), BF16)
    xbuf = sb1("xbuf", (128, DM))
    sq_scr = sb1("sq_scr", (128, DM), BF16)
    lntb = [sb1(f"lntb{i}", (128, DM), BF16) for i in range(2)]
    slab = [sb1(f"slab{i}", (128, DM), BF16) for i in range(2)]
    raw = [sb1(f"raw{i}", (128, T + 3)) for i in range(3)]
    accv = [sb1(f"accv{i}", (128, T)) for i in range(2)]
    stage_bf = [sb1(f"stage{i}", (128, T), BF16) for i in range(2)]
    zstage = [sb1(f"zst{i}", (128, 512), BF16) for i in range(2)]
    dtraw = sb1("dtraw", (64, T))
    dts1 = sb1("dts1", (64, T))
    dts2 = sb1("dts2", (64, T))
    a_fm = sb1("a_fm", (64, T))
    u_fm = sb1("u_fm", (64, T))
    datm = sb1("datm", (128, 64))
    utm_buf = sb1("utm_buf", (128, CPT * 64))
    cd_buf = sb1("cd_buf", (64, CPT))
    cdrow_buf = sb1("cdrow_buf", (1, CPT * 64))
    shi_buf = sb1("shi_buf", (64, T), BF16)
    slo_buf = sb1("slo_buf", (64, T), BF16)
    ahi_buf = sb1("ahi_buf", (64, T), BF16)
    alo_buf = sb1("alo_buf", (64, T), BF16)
    bm_tm = sb1("bm_tm", (128, CPT * 128), BF16)
    xfm_c = [sb1("xfm_c0", (128, DI), BF16)]
    xfm_c = [xfm_c[0], xfm_c[0]]
    xtm_c = sb1("xtm_c", (128, DI), BF16)
    xu_c = sb1("xu_c", (128, DI), BF16)
    cdb_f = sb1("cdb_f", (128, 64))
    halo_fm = sb1("halo_fm", (128, NDC * 3), BF16)
    p0tile = [sb1(f"p0tile{i}", (128, 1024)) for i in range(3)]
    p0out = [sb1("p0out0", (128, 1024), BF16)]
    p0tmp = sb1("p0tmp", (128, 1024))
    p0sum = sb1("p0sum", (128, 4))

    dtp = dtraw
    ldt_v = dts1
    s32 = dts1
    dA_fm = dts2

    ld = p.dma("sync", "xin", xbuf[0:3, :], x_halo)
    thl = ln_block(xbuf[0:3, :], lntb[0][0:3, :], DM, [ld], sq_scr)
    hev = []
    for dc in range(NDC):
        tt_ = p.pe(lambda t, dc=dc: t.transpose(PT[dc % 2][:, 0:3],
                                                lntb[0][0:3, dc * 128:(dc + 1) * 128], ident_b[0:3, 0:3]),
                   waits=[thl, hev[dc - 2] if dc >= 2 else None])
        hev.append(p.dve(lambda v, dc=dc: v.tensor_copy(halo_fm[:, dc * 3:(dc + 1) * 3],
                                                        PT[dc % 2][:, 0:3]), waits=[tt_]))
    hslab_free = [None, None]
    hpb_free = [None, None]
    for k in range(NCC):
        fc = NZC + k
        i = k % 2
        sl = p.dma("sync", f"sl{i}", slab[i][:], wbin[fc], waits=[t_wbin, hslab_free[i]])
        mmt = None
        for dc in range(NDC):
            mmt = p.pe(lambda t, i=i, dc=dc: t.matmul(
                PB[i][:, 0:3], slab[i][:, dc * 128:(dc + 1) * 128], halo_fm[:, dc * 3:(dc + 1) * 3],
                start=(dc == 0), stop=(dc == NDC - 1)),
                waits=[sl, hev[dc], hpb_free[i]], inc=(dc == NDC - 1))
        hslab_free[i] = mmt
        hpb_free[i] = p.dve(lambda v, k=k, i=i: v.tensor_copy(carry[:, k * 3:(k + 1) * 3],
                                                              PB[i][:, 0:3]), waits=[mmt])
    t_carry = p.tok("dve")

    pbfree = [None] * 6
    ptfree = [None, None]
    slab_free = list(hslab_free)
    stage_free2 = [None, None]
    raw_free = [None] * 3
    acc_free = [None, None]
    stage_free = [None, None]
    zst_free = [None, None]
    lntb_free = [hev[-1], None]
    xbuf_free = [thl]
    xfm_free = XfmShared()
    xtm_free = [None]
    xtm_sp = [None]
    states_done = [None]
    pv_sp = [None]
    t_dt_prev = [None]
    conv_sp = {}
    bmtm_tok = [None]
    wr0 = [None, None, None]
    sum_free = [None, None]
    rsb_tok = [None]
    q_free = [None, None, None]
    wq_last = [None, None]
    t_tile_guard = [None]

    def p0b_gen_fn():
        for dmc in range(16):
            for qt in range(4):
                i = (dmc * 4 + qt) % 3
                oi = 0
                ld = p.dma("sync", f"p0_{i}", p0tile[i][:],
                           w_out_blk[dmc][:, qt * 1024:(qt + 1) * 1024],
                           waits=[wr0[i]])
                tq_ = p.dve(lambda v, qt=qt, i=i, oi=oi: v.tensor_tensor(
                    p0out[oi][:].rearrange("p (c m) -> p c m", m=128),
                    p0tile[i][:].rearrange("p (c m) -> p c m", m=128),
                    nw_cols[:, qt * 8:(qt + 1) * 8].rearrange(
                        "p (c o) -> p c o", o=1).broadcast_to((128, 8, 128)),
                    ALU.mult),
                    waits=[ld, wq_last[oi]])
                wr0[i] = tq_
                wq_last[oi] = p.dma("pool", f"wout{i}",
                                    wboutc[dmc][:, qt * 1024:(qt + 1) * 1024],
                                    p0out[oi][:], waits=[tq_])
                yield
        tz = p.dve(lambda v: v.memset(p0sum[:], 0.0))
        sum_free[0] = tz
        for k in range(8192 // 128 // n_cores):
            for hf in range(2):
                i = (k * 2 + hf) % 3
                ld = p.dma("sync", f"p0_{i}", p0tile[i][:],
                           w1s_in[k * 128:(k + 1) * 128, hf * 1024:(hf + 1) * 1024],
                           waits=[sum_free[0]])
                tr = p.dve(lambda v, i=i: v.reduce_sum(p0sum[:, 2:3], p0tile[i][:], axis=AX.X,
                                                       apply_absolute_value=True), waits=[ld])
                sum_free[0] = p.dve(lambda v: v.tensor_add(p0sum[:, 0:1], p0sum[:, 0:1],
                                                           p0sum[:, 2:3]), waits=[tr])
                yield
        for k in range(DM // 128 // n_cores):
            for hf in range(8):
                i = (k * 8 + hf) % 3
                ld = p.dma("sync", f"p0_{i}", p0tile[i][:],
                           w2s_in[k * 128:(k + 1) * 128, hf * 1024:(hf + 1) * 1024],
                           waits=[sum_free[0]])
                tr = p.dve(lambda v, i=i: v.reduce_sum(p0sum[:, 2:3], p0tile[i][:], axis=AX.X,
                                                       apply_absolute_value=True), waits=[ld])
                sum_free[0] = p.dve(lambda v: v.tensor_add(p0sum[:, 1:2], p0sum[:, 1:2],
                                                           p0sum[:, 2:3]), waits=[tr])
                yield
        tm1 = p.pe(lambda t_: t_.matmul(PB[3][0:1, 384:385], ones_col_f[:], p0sum[:, 0:1],
                                        start=True, stop=True), waits=[sum_free[0], pbfree[3]])
        tm2 = p.pe(lambda t_: t_.matmul(PB[3][0:1, 385:386], ones_col_f[:], p0sum[:, 1:2],
                                        start=True, stop=True))
        tv = p.dve(lambda v: v.tensor_copy(sums2x[:], PB[3][0:1, 384:386]), waits=[tm1, tm2])
        pbfree[3] = tv
        tdm = p.dma("pool", "rsum", rsum_in, sums2x[:], waits=[tv])
        tcl = p.coll(lambda g: g.collective_compute(
            "AllReduce", ALU.add, replica_groups=cfg["allcores"], ins=[rsum_in],
            outs=[rsum_out]), waits=[tdm])
        ld = p.dma("sync", "rsum2", rs_row[:], rsum_out, waits=[tcl])
        t1 = p.dve(lambda v: v.tensor_scalar(rs_row[:], rs_row[:], 1.0 / (8192.0 * DM), None,
                                             ALU.mult), waits=[ld])
        t2 = p.dve(lambda v: v.tensor_scalar(rs_row[:], rs_row[:], 1e-8, None, ALU.max),
                   waits=[t1])
        t3 = p.dve(lambda v: v.reciprocal(rs_row[:], rs_row[:]), waits=[t2])
        t4 = p.pe(lambda t_: t_.matmul(PB[3][:, 386:388], ones_row_f[:], rs_row[:],
                                       start=True, stop=True), waits=[t3, pbfree[3]])
        rsb_tok[0] = p.dve(lambda v: v.tensor_copy(rsb[:], PB[3][:, 386:388]), waits=[t4])
        pbfree[3] = rsb_tok[0]
        yield
        tmp_g = [None]
        for w2mode in (False, True):
            steps = ([(dmc, qc) for dmc in range(16) for qc in range(8)] if w2mode
                     else [(fc, qc) for fc in range(64) for qc in range(2)])
            for a, qc in steps:
                i = (a * (8 if w2mode else 2) + qc) % 3
                oi = 0
                srcw = (w2_blk[a][:, qc * 1024:(qc + 1) * 1024] if w2mode
                        else w1_blk[a][:, qc * 1024:(qc + 1) * 1024])
                dstw = (wb2c[a][:, qc * 1024:(qc + 1) * 1024] if w2mode
                        else wb1[a][:, qc * 1024:(qc + 1) * 1024])
                ld = p.dma("sync", f"p0_{i}", p0tile[i][:], srcw,
                           waits=[rsb_tok[0], q_free[i]])
                sc = 1 if w2mode else 0
                t1 = p.dve(lambda v, i=i, sc=sc: v.tensor_scalar(
                    p0tmp[:], p0tile[i][:], rsb[:, sc:sc + 1], None, ALU.mult),
                    waits=[ld, tmp_g[0]])
                t2 = p.dve(lambda v, i=i: v.tensor_scalar(
                    p0tile[i][:], p0tmp[:], 0.5, -0.5, ALU.min, ALU.max), waits=[t1])
                t3 = p.dve(lambda v, i=i: v.tensor_sub(p0tmp[:], p0tmp[:], p0tile[i][:]),
                           waits=[t2])
                q_free[i] = t3
                t4 = p.act(lambda s, oi=oi: s.activation(p0out[oi][:], p0tmp[:], AF.Sign),
                           waits=[t3, wq_last[oi]])
                tmp_g[0] = t4
                wq_last[oi] = p.dma("pool", "wq", dstw, p0out[oi][:], waits=[t4])
                yield
        yield

    def pass1_tile(t):
        gc0 = t * CPT
        t_srec = p.tok("q_srec") if "q_srec" in p.C else None
        for tb in range(TBT):
            i = tb % 2
            ldx = p.dma("sync", "xin", xbuf[:], x_main[t * T + tb * 128:t * T + (tb + 1) * 128, :],
                        waits=[xbuf_free[0]])
            tln = ln_block(xbuf[:], lntb[i][:], DM, [ldx, lntb_free[i]], sq_scr)
            xbuf_free[0] = tln
            for g8 in range(2):
                pti = (tb * 2 + g8) % 2
                tts = None
                for k in range(8):
                    dc = g8 * 8 + k
                    tts = p.pe(lambda t_, dc=dc, k=k, pti=pti, i=i: t_.transpose(
                        PT[pti][:, k * 128:(k + 1) * 128], lntb[i][:, dc * 128:(dc + 1) * 128],
                        ident_b[:]), waits=[tln, ptfree[pti]])
                tev = p.dve(lambda v, g8=g8, tb=tb, pti=pti: v.tensor_copy(
                    ln1_fm[:].rearrange("p (dc tok) -> p dc tok", tok=T)[:, g8 * 8:(g8 + 1) * 8,
                                                                        tb * 128:(tb + 1) * 128],
                    PT[pti][:].rearrange("p (k f) -> p k f", f=128)),
                    waits=[tts, t_tile_guard[0]])
                ptfree[pti] = tev
                lntb_free[i] = tev
        t_ln1 = p.tok("dve")

        NQ = T // 512
        dt_ev = [None] * NQ
        for fc in range(NFC):
            i = fc % 2
            sl = p.dma("sync", f"sl{i}", slab[i][:], wbin[fc], waits=[t_wbin, slab_free[i]])
            for q in range(NQ):
                r = (fc * NQ + q) % 3
                mmt = None
                for dc in range(NDC):
                    mmt = p.pe(lambda t_, i=i, dc=dc, q=q, r=r: t_.matmul(
                        PB[r][:], slab[i][:, dc * 128:(dc + 1) * 128],
                        ln1_fm[:, dc * T + q * 512:dc * T + (q + 1) * 512],
                        start=(dc == 0), stop=(dc == NDC - 1)),
                        waits=[sl, t_ln1, pbfree[r]], inc=(dc == NDC - 1))
                if q == NQ - 1:
                    slab_free[i] = mmt
                if fc < NZC:
                    zi = (fc * NQ + q) % 2
                    tz = p.act(lambda s, r=r, zi=zi: s.activation(zstage[zi][:], PB[r][:], AF.Copy),
                               waits=[mmt, zst_free[zi]])
                    pbfree[r] = tz
                    zst_free[zi] = p.dma("pool", f"zw{zi}",
                                         z_scr[t * CPT + q * 4:t * CPT + q * 4 + 4, :,
                                               fc * 128:(fc + 1) * 128].rearrange("c p f -> p c f"),
                                         zstage[zi][:].rearrange("p (c f) -> p c f", f=128),
                                         waits=[tz])
                elif fc < NZC + NCC:
                    cc = fc - NZC
                    ri = cc % 3
                    if q == 0:
                        p.dve(lambda v, cc=cc, ri=ri: v.tensor_copy(
                            raw[ri][:, 0:3], carry[:, cc * 3:(cc + 1) * 3]),
                            waits=[t_carry, raw_free[ri]])
                    te = p.act(lambda s, r=r, ri=ri, q=q: s.activation(
                        raw[ri][:, 3 + q * 512:3 + (q + 1) * 512], PB[r][:], AF.Copy),
                        waits=[mmt])
                    pbfree[r] = te
                    if q == NQ - 1:
                        ai = cc % 2
                        t0 = p.dve(lambda v, ri=ri, cc=cc, ai=ai: v.tensor_scalar(
                            accv[ai][:], raw[ri][:, 0:T], cw_cols[:, cc * 4:cc * 4 + 1], None,
                            ALU.mult), waits=[te, acc_free[ai]])
                        for k in range(1, 4):
                            t0 = p.dve(lambda v, ri=ri, cc=cc, ai=ai, k=k: v.scalar_tensor_tensor(
                                accv[ai][:], raw[ri][:, k:k + T],
                                cw_cols[:, cc * 4 + k:cc * 4 + k + 1],
                                accv[ai][:], ALU.mult, ALU.add), waits=[t0])
                        raw_free[ri] = p.dve(lambda v, ri=ri, cc=cc: v.tensor_copy(
                            carry[:, cc * 3:(cc + 1) * 3], raw[ri][:, T:T + 3]), waits=[te])
                        si = cc % 2
                        tsil = p.act(lambda s, ai=ai, cc=cc, si=si: s.activation(
                            stage_bf[si][:], accv[ai][:], AF.Silu, bias=cb_cols[:, cc:cc + 1]),
                            waits=[t0, stage_free[si], stage_free2[si]])
                        acc_free[ai] = tsil
                        if cc < NZC:
                            spt = p.dma("pool", f"st{si}",
                                        xfm_scr[t * CPT:(t + 1) * CPT, :,
                                                cc * 128:(cc + 1) * 128].rearrange("c p f -> p c f"),
                                        stage_bf[si][:].rearrange("p (c f) -> p c f", f=128),
                                        waits=[tsil])
                        else:
                            spt = p.dma("pool", f"st{si}",
                                        bc_scr[t * CPT:(t + 1) * CPT, :,
                                               (cc - NZC) * 128:(cc - NZC + 1) * 128].rearrange("c p f -> p c f"),
                                        stage_bf[si][:].rearrange("p (c f) -> p c f", f=128),
                                        waits=[tsil])
                        conv_sp[(t, cc)] = spt
                        stage_free[si] = spt
                        stage_free2[si] = None
                        if cc == NZC:
                            bev = None
                            bpe = None
                            for c in range(CPT):
                                pti = c % 2
                                bpe = p.pe(lambda t_, c=c, si=si, pti=pti: t_.transpose(
                                    PT[pti][:, 0:128], stage_bf[si][:, c * 128:(c + 1) * 128],
                                    ident_b[:]), waits=[tsil, ptfree[pti]])
                                bev = p.dve(lambda v, c=c, pti=pti: v.tensor_copy(
                                    bm_tm[:, c * 128:(c + 1) * 128], PT[pti][:, 0:128]),
                                    waits=[bpe])
                                ptfree[pti] = bev
                            bmtm_tok[0] = bev
                            stage_free2[si] = bpe
                else:
                    tdd = p.dve(lambda v, r=r, q=q: v.tensor_copy(dtraw[:, q * 512:(q + 1) * 512],
                                                                  PB[r][0:64, :]),
                                waits=[mmt, t_dt_prev[0]])
                    pbfree[r] = tdd
                    dt_ev[q] = tdd
            next(p0b_gen, None)
        w0 = list(dt_ev)
        ta = p.act(lambda s: s.activation(dts1[:], dtraw[:], AF.Relu, bias=dtb_col[:]), waits=w0)
        tb_ = p.act(lambda s: s.activation(dts2[:], dtraw[:], AF.Abs, bias=dtb_col[:]), waits=w0)
        tc = p.act(lambda s: s.activation(dts2[:], dts2[:], AF.Exp, scale=-1.0), waits=[tb_])
        tdv = p.dve(lambda v: v.tensor_scalar(dts2[:], dts2[:], 1.0, None, ALU.add), waits=[tc])
        te = p.act(lambda s: s.activation(dts2[:], dts2[:], AF.Ln), waits=[tdv])
        tf = p.dve(lambda v: v.tensor_add(dtp[:], dts1[:], dts2[:]), waits=[te, ta])
        tg = p.act(lambda s: s.activation(ldt_v[:], dtp[:], AF.Ln), waits=[tf])
        th = p.dve(lambda v: v.tensor_scalar(dA_fm[:], dtp[:], a_col[:], None, ALU.mult),
                   waits=[tf, te])
        for c in range(CPT):
            t1 = p.pe(lambda t_, c=c: t_.transpose(PB[3][0:128, 0:64],
                                                   dA_fm[:, c * 128:(c + 1) * 128],
                                                   ident_f[0:64, 0:64]),
                      waits=[th, pbfree[3]])
            t2 = p.dve(lambda v: v.tensor_copy(datm[:], PB[3][:, 0:64]), waits=[t1])
            t3 = p.pe(lambda t_: t_.matmul(PB[3][0:64, 64:192], datm[:], tlow[:], start=True,
                                           stop=True), waits=[t2])
            t4 = p.dve(lambda v, c=c: v.tensor_copy(a_fm[:, c * 128:(c + 1) * 128],
                                                    PB[3][0:64, 64:192]), waits=[t3])
            tcd = p.act(lambda s, c=c: s.activation(cd_buf[:, c:c + 1],
                                                    a_fm[:, c * 128 + 127:c * 128 + 128], AF.Exp),
                        waits=[t4])
            tu = p.act(lambda s, c=c: s.activation(u_fm[:, c * 128:(c + 1) * 128],
                                                   a_fm[:, c * 128:(c + 1) * 128], AF.Exp,
                                                   scale=-1.0,
                                                   bias=a_fm[:, c * 128 + 127:c * 128 + 128]),
                       waits=[t4])
            tu2 = p.dve(lambda v, c=c: v.tensor_mul(u_fm[:, c * 128:(c + 1) * 128],
                                                    u_fm[:, c * 128:(c + 1) * 128],
                                                    dtp[:, c * 128:(c + 1) * 128]), waits=[tu])
            t5 = p.pe(lambda t_, c=c: t_.transpose(PB[3][0:128, 192:256],
                                                   u_fm[:, c * 128:(c + 1) * 128],
                                                   ident_f[0:64, 0:64]),
                      waits=[tu2, t4])
            t6 = p.dve(lambda v, c=c: v.tensor_copy(utm_buf[:, c * 64:(c + 1) * 64],
                                                    PB[3][:, 192:256]), waits=[t5])
            t7 = p.pe(lambda t_, c=c: t_.transpose(PB[3][0:1, 256:320], cd_buf[:, c:c + 1],
                                                   ident_f[0:64, 0:64]), waits=[tcd, t6])
            t8 = p.dve(lambda v, c=c: v.tensor_copy(cdrow_buf[:, c * 64:(c + 1) * 64],
                                                    PB[3][0:1, 256:320]), waits=[t7, t_srec])
            pbfree[3] = ("dve", max(t6[1], t8[1]))
            ta1 = p.dve(lambda v, c=c: v.tensor_copy(ahi_buf[:, c * 128:(c + 1) * 128],
                                                     a_fm[:, c * 128:(c + 1) * 128]),
                        waits=[t4, t_srec])
            p.dve(lambda v, c=c: v.tensor_sub(alo_buf[:, c * 128:(c + 1) * 128],
                                              a_fm[:, c * 128:(c + 1) * 128],
                                              ahi_buf[:, c * 128:(c + 1) * 128]), waits=[ta1])
            ts1 = p.dve(lambda v, c=c: v.tensor_sub(s32[:, c * 128:(c + 1) * 128],
                                                    ldt_v[:, c * 128:(c + 1) * 128],
                                                    a_fm[:, c * 128:(c + 1) * 128]),
                        waits=[tg, t4, t_srec])
            ts2 = p.dve(lambda v, c=c: v.tensor_copy(shi_buf[:, c * 128:(c + 1) * 128],
                                                     s32[:, c * 128:(c + 1) * 128]), waits=[ts1])
            p.dve(lambda v, c=c: v.tensor_sub(slo_buf[:, c * 128:(c + 1) * 128],
                                              s32[:, c * 128:(c + 1) * 128],
                                              shi_buf[:, c * 128:(c + 1) * 128]), waits=[ts2])
        t_dt = p.tok("dve")
        p.dma("pool", "srec", srec_scr[gc0:gc0 + CPT, 0:64, :].rearrange("c h i -> h c i"),
              shi_buf[:].rearrange("h (c i) -> h c i", i=128), waits=[t_dt])
        p.dma("pool", "srec", srec_scr[gc0:gc0 + CPT, 64:128, :].rearrange("c h i -> h c i"),
              slo_buf[:].rearrange("h (c i) -> h c i", i=128), waits=[t_dt])
        p.dma("pool", "srec", aflat_scr[gc0:gc0 + CPT, 0].rearrange("c h i -> h c i"),
              ahi_buf[:].rearrange("h (c i) -> h c i", i=128), waits=[t_dt])
        p.dma("pool", "srec", aflat_scr[gc0:gc0 + CPT, 1].rearrange("c h i -> h c i"),
              alo_buf[:].rearrange("h (c i) -> h c i", i=128), waits=[t_dt])
        p.dma("pool", "srec", cd_scr[gc0:gc0 + CPT, :].rearrange("(o c) h -> o c h", o=1),
              cdrow_buf[:].rearrange("o (c h) -> o c h", h=64), waits=[t_dt])

        for c in range(CPT):
            gc = gc0 + c
            i = c % 2
            lxf = p.dma("act", f"rx{i}", xfm_c[i][:], xfm_scr[gc],
                        waits=[conv_sp[(t, NZC - 1)], conv_sp[(t, NZC - 2)], xfm_free[i]])
            pv_sp[0] = p.dma("pool", "pvw", pv_scr[gc], S_run[:], waits=[states_done[0], pv_sp[0]])
            t_xtm = None
            for g8 in range(4):
                pti = g8 % 2
                tts = None
                for k in range(8):
                    cc = g8 * 8 + k
                    tts = p.pe(lambda t_, cc=cc, k=k, pti=pti, i=i: t_.transpose(
                        PT[pti][:, k * 128:(k + 1) * 128], xfm_c[i][:, cc * 128:(cc + 1) * 128],
                        ident_b[:]), waits=[lxf, ptfree[pti]])
                t_xtm = p.dve(lambda v, g8=g8, pti=pti: v.tensor_copy(
                    xtm_c[:, g8 * 1024:(g8 + 1) * 1024], PT[pti][:]),
                    waits=[tts, xtm_free[0], xtm_sp[0]])
                ptfree[pti] = t_xtm
            xfm_free[i] = p.tok("pe")
            xtm_sp[0] = p.dma("pool", "xtw", xtm_scr[gc], xtm_c[:], waits=[t_xtm])
            txu = p.dve(lambda v, c=c: v.tensor_mul(
                xu_c[:].rearrange("p (h f) -> p h f", f=64),
                xtm_c[:].rearrange("p (h f) -> p h f", f=64),
                utm_buf[:, c * 64:(c + 1) * 64].rearrange("p (h f) -> p h f", f=1
                                                          ).broadcast_to((128, 64, 64))),
                waits=[t_xtm, t_dt])
            xtm_free[0] = txu
            tc1 = p.pe(lambda t_, c=c: t_.matmul(PB[3][:, 320:384], ones_row_f[:],
                                                 cdrow_buf[0:1, c * 64:(c + 1) * 64],
                                                 start=True, stop=True), waits=[t_dt, pbfree[3]])
            tc2 = p.dve(lambda v: v.tensor_copy(cdb_f[:], PB[3][:, 320:384]), waits=[tc1])
            pbfree[3] = tc2
            sfin = None
            for grp in range(8):
                r = 2 + grp % 4
                tmm = p.pe(lambda t_, grp=grp, c=c, r=r: t_.matmul(
                    PB[r][:], bm_tm[:, c * 128:(c + 1) * 128], xu_c[:, grp * 512:(grp + 1) * 512],
                    start=True, stop=True), waits=[txu, bmtm_tok[0], pbfree[r]])
                tsc_ = p.dve(lambda v, grp=grp: v.tensor_mul(
                    S_run[:, grp * 512:(grp + 1) * 512].rearrange("p (h f) -> p h f", f=64),
                    S_run[:, grp * 512:(grp + 1) * 512].rearrange("p (h f) -> p h f", f=64),
                    cdb_f[:, grp * 8:(grp + 1) * 8].rearrange(
                        "p (h o) -> p h o", o=1).broadcast_to((128, 8, 64))),
                    waits=[tc2, pv_sp[0]])
                sfin = p.dve(lambda v, grp=grp, r=r: v.tensor_add(
                    S_run[:, grp * 512:(grp + 1) * 512],
                    S_run[:, grp * 512:(grp + 1) * 512], PB[r][:]),
                    waits=[tsc_, tmm])
                pbfree[r] = sfin
            states_done[0] = sfin
        t_dt_prev[0] = p.tok("dve")

    p0b_gen = p0b_gen_fn()
    for t in range(NTIL):
        t_tile_guard[0] = p.tok("pe")
        pass1_tile(t)
    for _ in p0b_gen:
        pass

    tag = p.dma("pool", "ag", ag_in, S_run[:], waits=[states_done[0]])
    p.coll(lambda g: g.collective_compute(
        "AllGather", ALU.bypass, replica_groups=cfg["pairs"], ins=[ag_in], outs=[ag_out]),
        waits=[tag])
    p.barrier()
    sc1.close()

    sc2 = ExitStack()
    sb2 = mksb(sc2)
    PBX = [sc2.enter_context(nc.psum_tensor(f"pbx{i}", [128, 512], F32)) for i in range(2)]
    hseld = sb2("hseld", (128, 8192), BF16)
    fpad = [sb2("fpad0", (128, 8192), BF16)]
    ones128 = sb2("ones128", (128, 128), BF16)
    sq_acc = [sb2(f"sq_acc{i}", (128, 512)) for i in range(2)]
    S_init = sb2("S_init", (128, DI), BF16)
    xfm_c3 = [sb2("xfm_c30", (128, DI), BF16)] * 2
    d_grp = sb2("d_grp", (128, 32))
    cd_tile = sb2("cd_tile", (1, (CPT + 1) * 64))
    z_c = [sb2(f"z_c{i}", (128, DI), BF16) for i in range(2)]
    xtm_c2 = [sb2(f"xtm_c2{i}", (128, DI), BF16) for i in range(2)]
    pvr_c = [sb2("pvr_c0", (128, DI), BF16)] * 2
    bc_c = [sb2(f"bc_c{i}", (128, 256), BF16) for i in range(2)]
    srec_c = [sb2(f"srec_c{i}", (128, 128), BF16) for i in range(2)]
    cpall = sb2("cpall", (1, CPT * 64))
    cpb_all = sb2("cpb_all", (128, CPT * 64), BF16)
    g_rep = sb2("g_rep", (128, 512), BF16)
    m_sb = [sb2(f"m_sb{i}", (128, 512), BF16) for i in range(4)]
    e_sb = [sb2(f"e_sb{i}", (128, 512), BF16) for i in range(2)]
    yoff_sb = [sb2(f"yoff_sb{i}", (128, 512), BF16) for i in range(2)]
    gt1 = [sb2(f"gt1_{i}", (128, 512)) for i in range(2)]
    gt2 = [sb2(f"gt2_{i}", (128, 512), BF16) for i in range(2)]
    zs_sb = [sb2(f"zs_sb{i}", (128, 512), BF16) for i in range(2)]
    pcor = sb2("pcor", (128, 512), BF16)
    sq_sb = [sb2("sq_sb0", (128, 512))] * 2
    sq_free = [None, None]
    sqa_free = [None, None]
    rms_pend = []
    yn_q = sb2("yn_q", (128, NZC * HLF), BF16)
    rms2 = sb2("rms2", (1, 128))
    rstd_cols = sb2("rstd_cols", (128, CPT))
    oslab = [sb2(f"oslab{i}", (128, 32 * 128), BF16) for i in range(2)]
    o_ev = sb2("o_ev", (128, HLF))
    o_tbs = sb2("o_tbs", (128, CPH * DM))
    xres = sb2("xres", (128, DM))
    lntb2 = sb2("lntb2", (128, DM), BF16)
    sq_scr2 = sb2("sq_scr2", (128, DM), BF16)

    lhs = p.dma("sync", "setup", hseld[:], c_hseld)
    p.dve(lambda v: v.memset(fpad[0][:], 0.0))
    p.dve(lambda v: v.memset(ones128[:], 1.0))
    lds = p.dma("pool", "ag", S_init[:], ag_out[0:128, :], waits=[p.tok("coll")])
    tsi = p.dve(lambda v: v.tensor_scalar(S_init[:], S_init[:], smask_c[:, 0:1], None, ALU.mult),
                waits=[lds])
    for pr in range(32):
        p.dve(lambda v, pr=pr: v.tensor_copy(d_grp[0:64, pr:pr + 1], db_full[0:64, 2 * pr:2 * pr + 1]))
        p.dve(lambda v, pr=pr: v.tensor_copy(d_grp[64:128, pr:pr + 1],
                                             db_full[64:128, 2 * pr + 1:2 * pr + 2]))
    p.barrier()

    rl_free = {}
    m_tok = [None] * 4
    e_free = [None, None]
    yo_free = [None, None]
    yoffb_free = [None, None]
    ymm_free = [None, None]
    gt_free = [None, None]
    pcor_free = [None]
    ynf = [None]
    oslab_free = [None, None]
    epi_free = [None]
    lnt2_free = [None]
    xres_free = [None]

    ld_pend = {}
    cpg = [None]

    def emit_reloads(t, c):
        gc = t * CPT + c
        i = c % 2
        lz = p.dma("pool", f"rz{i}", z_c[i][:], z_scr[gc], waits=[rl_free.get(("z", i))])
        lx = p.dma("act", "rxf", xfm_c3[0][:], xfm_scr[gc], waits=[rl_free.get(("xf", 0))])
        lt = p.dma("act", f"rxt{i}", xtm_c2[i][:], xtm_scr[gc], waits=[rl_free.get(("xt", i))])
        lp = p.dma("sync", "rpv", pvr_c[0][:], pv_scr[gc], waits=[rl_free.get(("pv", 0))])
        lb = p.dma("pool", f"rbc{i}", bc_c[i][:], bc_scr[gc], waits=[rl_free.get(("bc", i))])
        lf = p.dma("act", "rfl", fpad[0][0:2, :],
                   aflat_scr[gc].rearrange("a h f -> a (h f)"),
                   waits=[rl_free.get(("fl", 0))])
        lr = p.dma("pool", f"rsr{i}", srec_c[i][:], srec_scr[gc], waits=[rl_free.get(("sr", i))])
        ld_pend[gc] = (lz, lx, lt, lp, lb, lf, lr)

    def pass2_attn_tile(t):
        gc0 = t * CPT
        if gc0 > 0:
            ldc = p.dma("pool", "rcd", cd_tile[:].rearrange("o (c h) -> o c h", h=64),
                        cd_scr[gc0 - 1:gc0 + CPT, :].rearrange("(o c) h -> o c h", o=1),
                        waits=[rl_free.get(("cd", 0))])
        else:
            ldc = p.dma("pool", "rcd", cd_tile[0:1, 64:].rearrange("o (c h) -> o c h", h=64),
                        cd_scr[0:CPT, :].rearrange("(o c) h -> o c h", o=1),
                        waits=[rl_free.get(("cd", 0))])
        tprev = None
        for c in range(CPT):
            if t == 0 and c == 0:
                tprev = p.dve(lambda v: v.tensor_copy(cpall[:, 0:64], cp_row[:]), waits=[cpg[0]])
            else:
                srcv = cp_row[:] if c == 0 else cpall[:, (c - 1) * 64:c * 64]
                tprev = p.dve(lambda v, c=c, srcv=srcv: v.tensor_mul(
                    cpall[:, c * 64:(c + 1) * 64], srcv, cd_tile[0:1, c * 64:(c + 1) * 64]),
                    waits=[ldc, cpg[0]])
        tcpr = p.dve(lambda v: v.tensor_copy(cp_row[:], cpall[:, (CPT - 1) * 64:CPT * 64]),
                     waits=[tprev])
        for off in range(0, CPT * 64, 512):
            w = min(512, CPT * 64 - off)
            cpmm = p.pe(lambda t_, off=off, w=w: t_.matmul(
                PB[3][:, 0:w], ones_row_f[:], cpall[0:1, off:off + w],
                start=True, stop=True), waits=[tprev, pbfree[3]])
            pbfree[3] = p.dve(lambda v, off=off, w=w: v.tensor_copy(
                cpb_all[:, off:off + w], PB[3][:, 0:w]), waits=[cpmm, cpg[0]])
        cpg[0] = pbfree[3]
        if gc0 == 0:
            emit_reloads(0, 0)
        for qh in range(NQH):
            yn_last = None
            for ci in range(CPH):
                c = qh * CPH + ci
                gc = t * CPT + c
                i = c % 2
                lz, lx, lt, lp, lb, lf, lr = ld_pend.pop(gc)

                def corr_grp(grp):
                    t1_ = p.pool(lambda g, grp=grp, c=c: g.tensor_mul(
                        pcor[:].rearrange("p (h f) -> p h f", f=64),
                        S_init[:, grp * 512:(grp + 1) * 512].rearrange("p (h f) -> p h f", f=64),
                        cpb_all[:, c * 64 + grp * 8:c * 64 + (grp + 1) * 8].rearrange(
                            "p (h o) -> p h o", o=1).broadcast_to((128, 8, 64))),
                        waits=[lp, tsi, cpg[0], pcor_free[0]])
                    tk = p.pool(lambda g, grp=grp, i=i: g.tensor_add(
                        pvr_c[i][:, grp * 512:(grp + 1) * 512],
                        pvr_c[i][:, grp * 512:(grp + 1) * 512], pcor[:]), waits=[t1_])
                    pcor_free[0] = tk
                    return tk
                corr_tok = [None] * 9
                corr_tok[0] = corr_grp(0)
                tgm = p.pe(lambda t_, i=i: t_.matmul(PB[2][:, 0:128], bc_c[i][:, 0:128],
                                                     bc_c[i][:, 128:256], start=True, stop=True),
                           waits=[lb, pbfree[2]])
                tge = p.dve(lambda v: v.tensor_copy(g_rep[:, 0:128], PB[2][:, 0:128]),
                            waits=[tgm])
                pbfree[2] = tge
                tgr = p.dve(lambda v: v.tensor_copy(
                    g_rep[:, 128:512].rearrange("p (a f) -> p a f", f=128),
                    g_rep[:, 0:128].rearrange("p (a f) -> p a f", a=1).broadcast_to((128, 3, 128))),
                    waits=[tge])
                def mbuild(g8):
                    for s4 in range(2):
                        g4 = g8 * 2 + s4
                        r = g4 % 2
                        mi = g4 % 4
                        p.pe(lambda t_, r=r: t_.matmul(PB[r][:], masknegT[:], irep[:],
                                                       start=True, stop=False),
                             waits=[lf, lr, pbfree[r]], inc=False)
                        p.pe(lambda t_, g4=g4, r=r, i=i: t_.matmul(
                            PB[r][:], srec_c[i][:], hseld[:, g4 * 512:(g4 + 1) * 512],
                            start=False, stop=False), inc=False)
                        mmf = p.pe(lambda t_, g4=g4, r=r: t_.matmul(
                            PB[r][:], ones128[:], fpad[0][:, g4 * 512:(g4 + 1) * 512],
                            start=False, stop=True))
                        tex = p.act(lambda s, r=r, mi=mi: s.activation(m_sb[mi][:], PB[r][:],
                                                                      AF.Exp),
                                    waits=[mmf, m_tok[mi]])
                        pbfree[r] = tex
                        m_tok[mi] = p.dve(lambda v, mi=mi: v.tensor_mul(
                            m_sb[mi][:], m_sb[mi][:], g_rep[:]), waits=[tex, tgr])
                mbuild(0)
                while rms_pend:
                    rms_pend.pop(0)()
                for g8 in range(8):
                    gi = g8 % 2
                    if g8 + 1 < 8:
                        corr_tok[g8 + 1] = corr_grp(g8 + 1)
                    tpc = corr_tok[g8]
                    tsl = p.act(lambda s, g8=g8, i=i, gi=gi: s.activation(
                        zs_sb[gi][:], z_c[i][:, g8 * 512:(g8 + 1) * 512], AF.Silu),
                        waits=[lz, gt_free[gi]])
                    tc2 = p.pool(lambda g, g8=g8, gi=gi: g.tensor_mul(
                        gt2[gi][:].rearrange("p (a f) -> p a f", f=128),
                        xfm_c3[i][:, g8 * 512:(g8 + 1) * 512].rearrange("p (a f) -> p a f", f=128),
                        d_grp[:, g8 * 4:(g8 + 1) * 4].rearrange("p (a o) -> p a o", o=1
                                                                ).broadcast_to((128, 4, 128))),
                        waits=[lx, gt_free[gi]])
                    fpv = fpad[0][:].rearrange("p (h f) -> p h f", f=128)
                    mmtop = p.pe(lambda t_, g8=g8, fpv=fpv: t_.matmul(
                        PB[2][0:64, :], ones128[:, 0:64], fpv[:, 8 * g8:8 * g8 + 8:2, :],
                        start=True, stop=True), waits=[lf, pbfree[2]])
                    mm3 = p.pe(lambda t_, g8=g8, fpv=fpv: t_.matmul(
                        PB[2][64:128, :], ones128[:, 0:64], fpv[:, 8 * g8 + 1:8 * g8 + 8:2, :],
                        start=True, stop=True, tile_position=(0, 64)))
                    tee = p.act(lambda s, gi=gi: s.activation(e_sb[gi][:], PB[2][:], AF.Exp),
                                waits=[mmtop, mm3, e_free[gi]])
                    pbfree[2] = tee
                    yob = PB[3] if gi == 0 else PBX[1]
                    yo = None
                    for pl in range(4):
                        pr = g8 * 4 + pl
                        yo = p.pe(lambda t_, pr=pr, pl=pl, i=i, yob=yob: t_.matmul(
                            yob[:, pl * 128:(pl + 1) * 128], pvr_c[i][:, pr * 128:(pr + 1) * 128],
                            bc_c[i][:, 128:256], start=True, stop=True),
                            waits=[tpc, yo_free[gi]] + ([pbfree[3]] if gi == 0 else []))
                    tyo = p.dve(lambda v, gi=gi, yob=yob: v.tensor_mul(
                        yoff_sb[gi][:], yob[:], e_sb[gi][:]),
                        waits=[yo, tee, yoffb_free[gi]])
                    yo_free[gi] = tyo
                    e_free[gi] = tyo
                    if gi == 0:
                        pbfree[3] = tyo
                    if g8 + 1 < 8:
                        mbuild(g8 + 1)
                    ymb = PB[4] if gi == 0 else PBX[0]
                    ymm = None
                    for k in range(8):
                        h = g8 * 8 + k
                        pl = k // 2
                        hf_ = k % 2
                        s4 = 0 if k < 4 else 1
                        mi = (g8 * 2 + s4) % 4
                        hl = k % 4
                        ymm = p.pe(lambda t_, h=h, pl=pl, hf_=hf_, mi=mi, hl=hl, ymb=ymb, i=i: t_.matmul(
                            ymb[hf_ * 64:(hf_ + 1) * 64, pl * 128:(pl + 1) * 128],
                            xtm_c2[i][:, h * 64:(h + 1) * 64], m_sb[mi][:, hl * 128:(hl + 1) * 128],
                            start=True, stop=True, tile_position=(0, hf_ * 64)),
                            waits=[lt, m_tok[mi], ymm_free[gi]])
                    tc1 = p.dve(lambda v, gi=gi, ymb=ymb: v.tensor_add(
                        gt1[gi][:], ymb[:], yoff_sb[gi][:]), waits=[ymm, tyo])
                    ymm_free[gi] = tc1
                    yoffb_free[gi] = tc1
                    tc3 = p.dve(lambda v, gi=gi: v.tensor_add(gt1[gi][:], gt1[gi][:], gt2[gi][:]),
                                waits=[tc1, tc2])
                    ynv = yn_q[:].rearrange("p (cc tok) -> p cc tok", tok=HLF)[
                        :, g8 * 4:(g8 + 1) * 4, ci * 128:(ci + 1) * 128]
                    tc4 = p.dve(lambda v, ynv=ynv, gi=gi: v.tensor_mul(
                        ynv, gt1[gi][:].rearrange("p (a f) -> p a f", f=128),
                        zs_sb[gi][:].rearrange("p (a f) -> p a f", f=128)),
                        waits=[tc3, tsl, ynf[0]])
                    gt_free[gi] = tc4
                    yn_last = tc4
                    sqi = 0
                    tsq = p.act(lambda s, sqi=sqi, ynv=ynv: s.activation(
                        sq_sb[sqi][:].rearrange("p (a f) -> p a f", f=128), ynv, AF.Square),
                        waits=[tc4, sq_free[sqi]])
                    sai = c % 2
                    tacc = p.pool(lambda g, sqi=sqi, g8=g8, sai=sai: (
                        g.tensor_copy(sq_acc[sai][:], sq_sb[sqi][:]) if g8 == 0
                        else g.tensor_add(sq_acc[sai][:], sq_acc[sai][:], sq_sb[sqi][:])),
                        waits=[tsq, sqa_free[sai] if g8 == 0 else None])
                    sq_free[sqi] = tacc
                dvnow = p.tok("dve")
                for nm, ii in [("z", i), ("xf", 0), ("xt", i), ("pv", 0), ("bc", i),
                               ("fl", 0), ("sr", i)]:
                    rl_free[(nm, ii)] = dvnow
                rl_free[("cd", 0)] = dvnow
                if gc + 1 < NCHUNK:
                    emit_reloads((gc + 1) // CPT, (gc + 1) % CPT)

                def rms_emit(c=c, tacc=tacc, sai=c % 2):
                    rms_mm = p.pe(lambda t_, sai=sai: t_.matmul(
                        PB[5][0:1, :], ones_col_f[:], sq_acc[sai][:],
                        start=True, stop=True), waits=[tacc, pbfree[5]])
                    sqa_free[sai] = rms_mm
                    tr1 = p.dve(lambda v: v.tensor_reduce(
                        rms2[:], PB[5][0:1, :].rearrange("o (s i) -> o i s", s=4), AX.X, ALU.add),
                        waits=[rms_mm])
                    tr2_ = p.dve(lambda v: v.tensor_scalar(rms2[:], rms2[:], 1.0 / DI, None,
                                                           ALU.mult), waits=[tr1])
                    tr3 = p.dve(lambda v: v.tensor_scalar(rms2[:], rms2[:], 1e-5, None, ALU.add),
                                waits=[tr2_])
                    tr4 = p.act(lambda s: s.activation(rms2[:], rms2[:], AF.Sqrt), waits=[tr3])
                    tr5 = p.dve(lambda v: v.reciprocal(rms2[:], rms2[:]), waits=[tr4])
                    tr6 = p.pe(lambda t_: t_.transpose(PB[5][0:128, 0:1], rms2[:],
                                                       ident_f[0:1, 0:1]), waits=[tr5, tr1])
                    tr7 = p.dve(lambda v, c=c: v.tensor_copy(rstd_cols[:, c:c + 1], PB[5][:, 0:1]),
                                waits=[tr6])
                    pbfree[5] = tr7
                rms_pend.append(rms_emit)
            while rms_pend:
                rms_pend.pop(0)()
            ot_done = None
            for dmc in range(16):
                io = dmc % 2
                lsl = p.dma("sync", f"os{io}", oslab[io][:], wboutc[dmc],
                            waits=[oslab_free[io]])
                r = dmc % 2
                mmt = None
                for dic in range(NZC):
                    mmt = p.pe(lambda t_, dmc=dmc, dic=dic, io=io, r=r: t_.matmul(
                        PB[r][:, 0:HLF], oslab[io][:, dic * 128:(dic + 1) * 128],
                        yn_q[:, dic * HLF:(dic + 1) * HLF], start=(dic == 0),
                        stop=(dic == NZC - 1)),
                        waits=[lsl, yn_last, pbfree[r]], inc=(dic == NZC - 1))
                oslab_free[io] = mmt
                tev = p.dve(lambda v, r=r: v.tensor_copy(o_ev[:], PB[r][:, 0:HLF]), waits=[mmt])
                pbfree[r] = tev
                for tb in range(CPH):
                    rr = 2 + tb % 2
                    tt_ = p.pe(lambda t_, tb=tb, rr=rr: t_.transpose(
                        PB[rr][:, 128:256], o_ev[:, tb * 128:(tb + 1) * 128], ident_f[:]),
                        waits=[tev, pbfree[rr]])
                    ot_done = p.dve(lambda v, tb=tb, dmc=dmc, rr=rr: v.tensor_copy(
                        o_tbs[:, tb * DM + dmc * 128:tb * DM + (dmc + 1) * 128],
                        PB[rr][:, 128:256]), waits=[tt_])
                    pbfree[rr] = ot_done
            ynf[0] = p.tok("pe")
            for tb in range(CPH):
                c = qh * CPH + tb
                row0 = t * T + qh * HLF + tb * 128
                ldr = p.dma("sync", "xr", xres[:], x_main[row0:row0 + 128, :], waits=[xres_free[0]])
                tsc = p.dve(lambda v, tb=tb, c=c: v.tensor_scalar(
                    o_tbs[:, tb * DM:(tb + 1) * DM], o_tbs[:, tb * DM:(tb + 1) * DM],
                    rstd_cols[:, c:c + 1], None, ALU.mult), waits=[ot_done])
                tln = ln_block(o_tbs[:, tb * DM:(tb + 1) * DM], lntb2[:], DM, [tsc, lnt2_free[0]],
                               sq_scr2)
                tx2 = p.dve(lambda v: v.tensor_add(xres[:], xres[:], lntb2[:]), waits=[tln, ldr])
                lnt2_free[0] = tx2
                xres_free[0] = p.dma("pool", "x2w", x2_scr[row0:row0 + 128, :], xres[:], waits=[tx2])

    for t in range(NTIL):
        pass2_attn_tile(t)
    p.barrier()
    sc2.close()

    sc3 = ExitStack()
    sb3 = mksb(sc3)
    PT3 = [sc3.enter_context(nc.psum_tensor(f"pt3{i}", [128, 1024], BF16)) for i in range(2)]
    ln3_fm = sb3("ln3_fm", (128, NDC * TF), BF16)
    h1_buf = sb3("h1_buf", (128, 64 * TF), BF16)
    fslab = [sb3(f"fslab{i}", (128, DM), BF16) for i in range(2)]
    f2slab = [sb3(f"f2slab{i}", (128, 64 * 128), BF16) for i in range(2)]
    x2buf = sb3("x2buf", (128, DM))
    lntb3 = [sb3(f"lntb3{i}", (128, DM), BF16) for i in range(2)]
    o_tbs3 = sb3("o_tbs3", (128, TBF * DM))
    o_ev3 = sb3("o_ev3", (128, TF))
    b2_rep = sb3("b2_rep", (128, DM))
    sq_scr3 = sb3("sq_scr3", (128, DM), BF16)
    xr3 = x2buf

    ldb2 = p.dma("sync", "setup", x2buf[0:1, :], b2_in.rearrange("(o m) -> o m", o=1))
    for q in range(4):
        tmq = p.pe(lambda t_, q=q: t_.matmul(PB[q][:], ones_row_f[:],
                                             x2buf[0:1, q * 512:(q + 1) * 512],
                                             start=True, stop=True), waits=[ldb2, pbfree[q]])
        pbfree[q] = p.dve(lambda v, q=q: v.tensor_copy(b2_rep[:, q * 512:(q + 1) * 512],
                                                       PB[q][:]), waits=[tmq])
    p.barrier()

    fsl_free = [None, None]
    f2_free = [None, None]
    x2_free = [None]
    xr3_free = [None]
    lnt3_free = [None, None]
    ffn_guard = [None]
    out_done = [None]

    def ffn_ln3(ft):
        row0 = ft * TF
        for tb in range(TBF):
            i = tb % 2
            ldx = p.dma("sync", "x2b", x2buf[:], x2_scr[row0 + tb * 128:row0 + (tb + 1) * 128, :],
                        waits=[x2_free[0], xr3_free[0]])
            tln = ln_block(x2buf[:], lntb3[i][:], DM, [ldx, lnt3_free[i]], sq_scr3)
            x2_free[0] = tln
            for g8 in range(2):
                pti = (tb * 2 + g8) % 2
                tts = None
                for k in range(8):
                    dc = g8 * 8 + k
                    tts = p.pe(lambda t_, dc=dc, k=k, pti=pti, i=i: t_.transpose(
                        PT3[pti][:, k * 128:(k + 1) * 128], lntb3[i][:, dc * 128:(dc + 1) * 128],
                        ident_b[:]), waits=[tln, ptfree[pti]])
                tev = p.dve(lambda v, g8=g8, tb=tb, pti=pti: v.tensor_copy(
                    ln3_fm[:].rearrange("p (dc tok) -> p dc tok", tok=TF)[:, g8 * 8:(g8 + 1) * 8,
                                                                         tb * 128:(tb + 1) * 128],
                    PT3[pti][:].rearrange("p (k f) -> p k f", f=128)),
                    waits=[tts, ffn_guard[0]])
                ptfree[pti] = tev
                lnt3_free[i] = tev
        ln3_tok[0] = p.tok("dve")

    def ffn_compute(ft):
        row0 = ft * TF
        t_ln3 = ln3_tok[0]
        for fc in range(64):
            i = fc % 2
            sl = p.dma("sync", f"fs{i}", fslab[i][:], wb1[fc], waits=[fsl_free[i]])
            r = fc % 2
            mmt = None
            for dc in range(NDC):
                mmt = p.pe(lambda t_, i=i, dc=dc, r=r: t_.matmul(
                    PB[r][:, 0:TF], fslab[i][:, dc * 128:(dc + 1) * 128],
                    ln3_fm[:, dc * TF:(dc + 1) * TF], start=(dc == 0), stop=(dc == NDC - 1)),
                    waits=[sl, t_ln3, pbfree[r]], inc=(dc == NDC - 1))
            fsl_free[i] = mmt
            th1 = p.act(lambda s, fc=fc, r=r: s.activation(
                h1_buf[:, fc * TF:(fc + 1) * TF], PB[r][:, 0:TF], AF.Gelu,
                bias=b1_cols[:, fc:fc + 1]),
                waits=[mmt])
            pbfree[r] = th1
        t_h1 = p.tok("act")
        ffn_guard[0] = p.tok("pe")
        while epi_pend:
            epi_pend.pop(0)()
        if ft + 1 < NTF:
            ffn_ln3(ft + 1)
        for dmc in range(16):
            i = dmc % 2
            lsl = p.dma("act", f"f2{i}", f2slab[i][:], wb2c[dmc],
                        waits=[f2_free[i]])
            r = 2 + dmc % 2
            mmt = None
            for fcc in range(64):
                mmt = p.pe(lambda t_, dmc=dmc, fcc=fcc, i=i, r=r: t_.matmul(
                    PB[r][:, 0:TF], f2slab[i][:, fcc * 128:(fcc + 1) * 128],
                    h1_buf[:, fcc * TF:(fcc + 1) * TF], start=(fcc == 0), stop=(fcc == 63)),
                    waits=[lsl, t_h1, pbfree[r]], inc=(fcc == 63))
            f2_free[i] = mmt
            tev = p.dve(lambda v, r=r: v.tensor_copy(o_ev3[:], PB[r][:, 0:TF]),
                        waits=[mmt])
            pbfree[r] = tev
            for tb in range(TBF):
                rr = 4 + tb % 2
                tt_ = p.pe(lambda t_, tb=tb, rr=rr: t_.transpose(
                    PB[rr][:, 128:256], o_ev3[:, tb * 128:(tb + 1) * 128], ident_f[:]),
                    waits=[tev, pbfree[rr]])
                tot = p.dve(lambda v, tb=tb, dmc=dmc, rr=rr: v.tensor_copy(
                    o_tbs3[:, tb * DM + dmc * 128:tb * DM + (dmc + 1) * 128], PB[rr][:, 128:256]),
                    waits=[tt_])
                pbfree[rr] = tot
                out_done[0] = tot
        def epi_emit(row0=row0, od=out_done[0]):
            for tb in range(TBF):
                i = tb % 2
                ldr = p.dma("sync", "x2b", xr3[:], x2_scr[row0 + tb * 128:row0 + (tb + 1) * 128, :],
                            waits=[xr3_free[0], x2_free[0]])
                tb2 = p.dve(lambda v, tb=tb: v.tensor_add(o_tbs3[:, tb * DM:(tb + 1) * DM],
                                                          o_tbs3[:, tb * DM:(tb + 1) * DM],
                                                          b2_rep[:]),
                            waits=[od])
                tln = ln_block(o_tbs3[:, tb * DM:(tb + 1) * DM], lntb3[i][:], DM,
                               [tb2, lnt3_free[i]], sq_scr3)
                tfin = p.dve(lambda v, i=i: v.tensor_add(xr3[:], xr3[:], lntb3[i][:]),
                             waits=[tln, ldr])
                lnt3_free[i] = tfin
                x2_free[0] = tfin
                xr3_free[0] = p.dma("pool", "outw",
                                    out_d[row0 + tb * 128:row0 + (tb + 1) * 128, :], xr3[:],
                                    waits=[tfin])
        epi_pend.append(epi_emit)

    ln3_tok = [None]
    epi_pend = []
    ffn_guard[0] = p.tok("pe")
    ffn_ln3(0)
    for ft in range(NTF):
        ffn_compute(ft)
    while epi_pend:
        epi_pend.pop(0)()

    p.barrier()
    sc3.close()

    blk = st.enter_context(nc.Block())
    p.finalize(blk)
    st.close()
    return nc


_CACHE = {}


def _consts():
    maskneg = np.where(np.arange(128)[None, :] >= np.arange(128)[:, None], 0.0, -30000.0)
    return {
        "c_ident_b": np.eye(128, dtype=bfd),
        "c_ident_f": np.eye(128, dtype=np.float32),
        "c_ones_row": np.ones((1, 128), dtype=bfd),
        "c_ones_row_f": np.ones((1, 128), dtype=np.float32),
        "c_ones_col_f": np.ones((128, 1), dtype=np.float32),
        "c_ones_col_b": np.ones((128, 1), dtype=bfd),
        "c_masknegT": np.ascontiguousarray(maskneg.T).astype(bfd),
        "c_irep": np.tile(np.eye(128, dtype=bfd), (1, 4)),
        "c_hseld": ((np.arange(128)[:, None] % 64) == (np.arange(8192)[None, :] // 128)).astype(bfd),
        "c_tlow": np.triu(np.ones((128, 128), dtype=np.float32)),
    }


def make_in_maps(cfg, L0_attn_hidden_states, L0_attn_W_in, L0_attn_conv_w, L0_attn_conv_b,
                 L0_attn_A, L0_attn_D, L0_attn_dt_bias, L0_attn_norm_w, L0_attn_W_out,
                 L0_ffn_w1, L0_ffn_b1, L0_ffn_w2, L0_ffn_b2):
    n = cfg["n_cores"]
    SC = cfg["s_core"]
    hs = np.asarray(L0_attn_hidden_states, dtype=np.float32)
    w_in = np.asarray(L0_attn_W_in, dtype=np.float32)
    w_in_p = np.zeros((DM, DINP), dtype=np.float32)
    w_in_p[:, :w_in.shape[1]] = w_in
    w1 = np.ascontiguousarray(np.asarray(L0_ffn_w1, dtype=np.float32))
    w2 = np.ascontiguousarray(np.asarray(L0_ffn_w2, dtype=np.float32))
    w_out = np.asarray(L0_attn_W_out, dtype=np.float32)
    w_in_blk = np.ascontiguousarray(
        w_in_p.reshape(16, 128, NFC, 128).transpose(2, 1, 0, 3).reshape(NFC, 128, DM))
    w_out_blk = np.ascontiguousarray(
        w_out.reshape(32, 128, 16, 128).transpose(2, 1, 0, 3).reshape(16, 128, 4096))
    w1_blk = np.ascontiguousarray(
        w1.reshape(64, 128, 16, 128).transpose(0, 3, 2, 1).reshape(64, 128, DM))
    w2_blk = np.ascontiguousarray(
        w2.reshape(16, 128, 64, 128).transpose(0, 3, 2, 1).reshape(16, 128, 8192))
    base = dict(
        w_in_blk=w_in_blk,
        conv_w=np.ascontiguousarray(np.asarray(L0_attn_conv_w, dtype=np.float32)),
        conv_b=np.ascontiguousarray(np.asarray(L0_attn_conv_b, dtype=np.float32)),
        a_in=np.ascontiguousarray(np.asarray(L0_attn_A, dtype=np.float32)),
        d_in=np.ascontiguousarray(np.asarray(L0_attn_D, dtype=np.float32)),
        dtb_in=np.ascontiguousarray(np.asarray(L0_attn_dt_bias, dtype=np.float32)),
        nw_in=np.ascontiguousarray(np.asarray(L0_attn_norm_w, dtype=np.float32)),
        w_out_blk=w_out_blk,
        w1_blk=w1_blk, b1_in=np.ascontiguousarray(np.asarray(L0_ffn_b1, dtype=np.float32)),
        w2_blk=w2_blk, b2_in=np.ascontiguousarray(np.asarray(L0_ffn_b2, dtype=np.float32)),
    )
    base.update(_consts())
    in_maps = []
    for k in range(n):
        b_, h = k // 2, k % 2
        m = dict(base)
        m["x_main"] = np.ascontiguousarray(hs[b_, h * SC:(h + 1) * SC, :])
        m["x_halo"] = (np.zeros((3, DM), np.float32) if h == 0
                       else np.ascontiguousarray(hs[b_, SC - 3:SC, :]))
        m["c_smask"] = np.full((1, 1), 1.0 if h == 1 else 0.0, np.float32)
        m["w1s_in"] = np.ascontiguousarray(w1[k * (8192 // n):(k + 1) * (8192 // n), :])
        m["w2s_in"] = np.ascontiguousarray(w2[k * (DM // n):(k + 1) * (DM // n), :])
        in_maps.append(m)
    return in_maps


def kernel(**inputs):
    cfg = default_cfg()
    n = cfg["n_cores"]
    SC = cfg["s_core"]
    in_maps = make_in_maps(cfg, **inputs)
    if "nc" not in _CACHE:
        _CACHE["nc"] = build_program(cfg)
    res = run_bass_kernel_spmd(_CACHE["nc"], in_maps, core_ids=list(range(n)))
    nb = n // 2
    out = np.empty((nb, 2 * SC, DM), dtype=np.float32)
    for k in range(n):
        b_, h = k // 2, k % 2
        out[b_, h * SC:(h + 1) * SC, :] = np.asarray(res.results[k]["out"])
    return out


# version 13
# speedup vs baseline: 1.1529x; 1.0023x over previous
import numpy as np
import ml_dtypes
from contextlib import ExitStack
import concourse.bass as bass
import concourse.mybir as mybir
from concourse.bass_utils import run_bass_kernel_spmd

F32 = mybir.dt.float32
BF16 = mybir.dt.bfloat16
AF = mybir.ActivationFunctionType
ALU = mybir.AluOpType
AX = mybir.AxisListType
bfd = ml_dtypes.bfloat16

DM, DI, NH = 2048, 4096, 64
CONVD, DINP = 4352, 8576
NFC, NZC, NCC, NDC = 67, 32, 34, 16
ENG_OF = {"pe": "tensor", "dve": "vector", "act": "scalar", "pool": "gpsimd", "sync": "sync"}


def default_cfg():
    return dict(n_cores=8, s_core=4096, T=1024, TF=512,
                pairs=[[0, 1], [2, 3], [4, 5], [6, 7]],
                allcores=[[0, 1, 2, 3, 4, 5, 6, 7]])


class P:

    def __init__(self, nc, st):
        self.nc = nc
        self.st = st
        names = ["pe", "dve", "act", "pool", "coll", "bar"]
        self.S = {n: st.enter_context(nc.semaphore("s_" + n)) for n in names}
        self.C = {n: 0 for n in names}
        self.waited = {}
        self.thunks = {e: [] for e in ENG_OF}
        self.nbar = 0

    def _dsem(self, slot):
        name = "q_" + slot
        if name not in self.S:
            self.S[name] = self.st.enter_context(self.nc.semaphore(name))
            self.C[name] = 0
        return name

    def _waits(self, ename, toks):
        out = []
        for t in toks:
            if t is None:
                continue
            s, v = t
            if v <= 0:
                continue
            key = (ename, s)
            if self.waited.get(key, 0) >= v:
                continue
            self.waited[key] = v
            out.append((self.S[s], v))
        return out

    def _emit(self, ename, make, waits, sem, n):
        ws = self._waits(ename, waits)
        self.C[sem] += n
        semobj = self.S[sem]

        def th(e, ws=ws, make=make, semobj=semobj, n=n):
            for s, v in ws:
                e.wait_ge(s, v)
            make(e).then_inc(semobj, n)
        self.thunks[ename].append(th)
        return (sem, self.C[sem])

    def pe(self, make, waits=(), inc=True):
        if inc:
            return self._emit("pe", make, waits, "pe", 1)
        ws = self._waits("pe", waits)

        def th(e, ws=ws, make=make):
            for s, v in ws:
                e.wait_ge(s, v)
            make(e)
        self.thunks["pe"].append(th)
        return ("pe", self.C["pe"])

    def dve(self, make, waits=()):
        return self._emit("dve", make, (("dve", self.C["dve"]),) + tuple(waits), "dve", 1)

    def act(self, make, waits=()):
        return self._emit("act", make, (("act", self.C["act"]),) + tuple(waits), "act", 1)

    def pool(self, make, waits=()):
        return self._emit("pool", make, (("pool", self.C["pool"]),) + tuple(waits), "pool", 1)

    def dma(self, issuer, slot, out, in_, waits=(), **kw):
        q = self._dsem(slot)
        return self._emit(issuer, lambda e: e.dma_start(out=out, in_=in_, **kw), waits, q, 16)

    def coll(self, make, waits=()):
        return self._emit("pool", make, waits, "coll", 1)

    def tok(self, sem):
        return (sem, self.C[sem])

    def barrier(self):
        self.nbar += 1
        target = 5 * self.nbar
        snap = dict(self.C)
        dq = [k for k in snap if k.startswith("q_")] + ["coll"]
        for ename in ("sync", "act", "pool", "dve", "pe"):
            toks = [("pe", snap["pe"]), ("dve", snap["dve"]), ("act", snap["act"]),
                    ("pool", snap["pool"])] + ([(q, snap[q]) for q in dq]
                                               if ename == "sync" else [])
            ws = self._waits(ename, toks)
            self.C["bar"] += 1
            bs = self.S["bar"]

            def th(e, ws=ws, bs=bs, tgt=target):
                for s, v in ws:
                    e.wait_ge(s, v)
                e.nop().then_inc(bs, 1)
                e.wait_ge(bs, tgt)
            self.thunks[ename].append(th)
            self.waited[(ename, "bar")] = target
        for en in ENG_OF:
            for s, v in snap.items():
                self.waited[(en, s)] = max(self.waited.get((en, s), 0), v)

    def finalize(self, blk):
        for ename, ths in self.thunks.items():
            def runner(e, ths=ths):
                for th in ths:
                    th(e)
            getattr(blk, ENG_OF[ename])(runner)


class XfmShared:
    def __init__(self):
        self.v = None

    def __getitem__(self, i):
        return self.v

    def __setitem__(self, i, val):
        self.v = val


def mx(*toks):
    best = None
    for t in toks:
        if t is None:
            continue
        if best is None or (t[0] == best[0] and t[1] > best[1]):
            best = t
    return best


def build_program(cfg):
    n_cores = cfg["n_cores"]
    SC, T, TF = cfg["s_core"], cfg["T"], cfg["TF"]
    NTIL, CPT, NCHUNK = SC // T, T // 128, SC // 128
    NTF = SC // TF
    TBT, TBF = T // 128, TF // 128
    HLF = 256
    NQH = T // HLF
    CPH = HLF // 128

    nc = bass.Bass("TRN2", target_bir_lowering=False, debug=False)
    dt_ = nc.dram_tensor

    x_main = dt_("x_main", [SC, DM], F32, kind="ExternalInput").ap()
    x_halo = dt_("x_halo", [3, DM], F32, kind="ExternalInput").ap()
    w_in_blk = dt_("w_in_blk", [NFC, 128, DM], F32, kind="ExternalInput").ap()
    conv_w = dt_("conv_w", [CONVD, 4], F32, kind="ExternalInput").ap()
    conv_b = dt_("conv_b", [CONVD], F32, kind="ExternalInput").ap()
    a_in = dt_("a_in", [NH], F32, kind="ExternalInput").ap()
    d_in = dt_("d_in", [NH], F32, kind="ExternalInput").ap()
    dtb_in = dt_("dtb_in", [NH], F32, kind="ExternalInput").ap()
    nw_in = dt_("nw_in", [DI], F32, kind="ExternalInput").ap()
    w_out_blk = dt_("w_out_blk", [16, 128, 4096], F32, kind="ExternalInput").ap()
    w1_blk = dt_("w1_blk", [64, 128, DM], F32, kind="ExternalInput").ap()
    w1s_in = dt_("w1s_in", [8192 // n_cores, DM], F32, kind="ExternalInput").ap()
    b1_in = dt_("b1_in", [8192], F32, kind="ExternalInput").ap()
    w2_blk = dt_("w2_blk", [16, 128, 8192], F32, kind="ExternalInput").ap()
    w2s_in = dt_("w2s_in", [DM // n_cores, 8192], F32, kind="ExternalInput").ap()
    b2_in = dt_("b2_in", [DM], F32, kind="ExternalInput").ap()
    c_ident_b = dt_("c_ident_b", [128, 128], BF16, kind="ExternalInput").ap()
    c_ident_f = dt_("c_ident_f", [128, 128], F32, kind="ExternalInput").ap()
    c_ones_row = dt_("c_ones_row", [1, 128], BF16, kind="ExternalInput").ap()
    c_ones_row_f = dt_("c_ones_row_f", [1, 128], F32, kind="ExternalInput").ap()
    c_ones_col_f = dt_("c_ones_col_f", [128, 1], F32, kind="ExternalInput").ap()
    c_ones_col_b = dt_("c_ones_col_b", [128, 1], BF16, kind="ExternalInput").ap()
    c_masknegT = dt_("c_masknegT", [128, 128], BF16, kind="ExternalInput").ap()
    c_irep = dt_("c_irep", [128, 512], BF16, kind="ExternalInput").ap()
    c_hseld = dt_("c_hseld", [128, 8192], BF16, kind="ExternalInput").ap()
    c_tlow = dt_("c_tlow", [128, 128], F32, kind="ExternalInput").ap()
    c_smask = dt_("c_smask", [1, 1], F32, kind="ExternalInput").ap()
    out_d = dt_("out", [SC, DM], F32, kind="ExternalOutput").ap()

    wbin = dt_("wbin", [NFC, 128, DM], BF16, kind="Internal").ap()
    wboutc = dt_("wboutc", [16, 128, 4096], BF16, kind="Internal").ap()
    wb1 = dt_("wb1", [64, 128, DM], BF16, kind="Internal").ap()
    wb2c = dt_("wb2c", [16, 128, 8192], BF16, kind="Internal").ap()
    z_scr = dt_("z_scr", [NCHUNK, 128, NZC * 128], BF16, kind="Internal").ap()
    xfm_scr = dt_("xfm_scr", [NCHUNK, 128, NZC * 128], BF16, kind="Internal").ap()
    bc_scr = dt_("bc_scr", [NCHUNK, 128, 256], BF16, kind="Internal").ap()
    xtm_scr = dt_("xtm_scr", [NCHUNK, 128, DI], BF16, kind="Internal").ap()
    pv_scr = dt_("pv_scr", [NCHUNK, 128, DI], BF16, kind="Internal").ap()
    srec_scr = dt_("srec_scr", [NCHUNK, 128, 128], BF16, kind="Internal").ap()
    aflat_scr = dt_("aflat_scr", [NCHUNK, 2, 64, 128], BF16, kind="Internal").ap()
    cd_scr = dt_("cd_scr", [NCHUNK, 64], F32, kind="Internal").ap()
    x2_scr = dt_("x2_scr", [SC, DM], F32, kind="Internal").ap()
    rsum_in = dt_("rsum_in", [1, 2], F32, kind="Internal").ap()
    rsum_out = dt_("rsum_out", [1, 2], F32, kind="Internal").ap()
    ag_in = dt_("ag_in", [128, DI], F32, kind="Internal").ap()
    ag_out = dt_("ag_out", [2 * 128, DI], F32, kind="Internal").ap()

    st = ExitStack()
    p = P(nc, st)
    PB = [st.enter_context(nc.psum_tensor(f"pb{i}", [128, 512], F32)) for i in range(6)]

    def mksb(stk):
        return lambda name, shape, dt=F32: stk.enter_context(
            nc.sbuf_tensor(name, list(shape), dt))
    sbuf = mksb(st)

    ident_b = sbuf("ident_b", (128, 128), BF16)
    ident_f = sbuf("ident_f", (128, 128))
    ones_row = sbuf("ones_row", (1, 128), BF16)
    ones_row_f = sbuf("ones_row_f", (1, 128))
    ones_col_f = sbuf("ones_col_f", (128, 1))
    ones_col_b = sbuf("ones_col_b", (128, 1), BF16)
    masknegT = sbuf("masknegT", (128, 128), BF16)
    irep = sbuf("irep", (128, 512), BF16)
    tlow = sbuf("tlow", (128, 128))
    cw_cols = sbuf("cw_cols", (128, NCC * 4))
    cb_cols = sbuf("cb_cols", (128, NCC))
    a_col = sbuf("a_col", (64, 1))
    dtb_col = sbuf("dtb_col", (64, 1))
    nw_cols = sbuf("nw_cols", (128, NZC))
    b1_cols = sbuf("b1_cols", (128, 64))
    smask_c = sbuf("smask_c", (128, 1))
    rsb = sbuf("rsb", (128, 2))
    S_run = sbuf("S_run", (128, DI))
    carry = sbuf("carry", (128, NCC * 3))
    cp_row = sbuf("cp_row", (1, 64))
    drow = sbuf("drow", (1, 64))
    db_full = sbuf("db_full", (128, 64))
    rs_row = sbuf("rs_row", (1, 2))
    sums2x = sbuf("sums2x", (1, 2))
    stat = sbuf("stat", (128, 8))

    for o, i_, kw in [
        (ident_b[:], c_ident_b, {}), (ident_f[:], c_ident_f, {}),
        (ones_row[:], c_ones_row, {}), (ones_row_f[:], c_ones_row_f, {}),
        (ones_col_f[:], c_ones_col_f, {}), (ones_col_b[:], c_ones_col_b, {}),
        (masknegT[:], c_masknegT, {}), (irep[:], c_irep, {}), (tlow[:], c_tlow, {}),
        (cw_cols[:].rearrange("p (cc t) -> p cc t", t=4),
         conv_w.rearrange("(cc p) t -> p cc t", p=128), dict(allow_slow_non_contiguous=True)),
        (cb_cols[:], conv_b.rearrange("(cc p) -> p cc", p=128), dict(allow_slow_non_contiguous=True)),
        (a_col[:], a_in.rearrange("(h o) -> h o", o=1), {}),
        (dtb_col[:], dtb_in.rearrange("(h o) -> h o", o=1), {}),
        (nw_cols[:], nw_in.rearrange("(c p) -> p c", p=128), dict(allow_slow_non_contiguous=True)),
        (b1_cols[:], b1_in.rearrange("(c p) -> p c", p=128), dict(allow_slow_non_contiguous=True)),
        (smask_c[:], c_smask.broadcast_to((128, 1)), {}),
        (drow[:], d_in.rearrange("(o h) -> o h", o=1), {}),
    ]:
        p.dma("sync", "setup", o, i_, **kw)
    t_setup = p.tok("q_setup")
    td = p.pe(lambda t: t.matmul(PB[5][:, 0:64], ones_row_f[:], drow[:], start=True, stop=True),
              waits=[t_setup])
    p.dve(lambda v: v.tensor_copy(db_full[:], PB[5][:, 0:64]), waits=[td])
    p.dve(lambda v: v.memset(cp_row[:], 1.0))
    p.dve(lambda v: v.memset(S_run[:], 0.0))
    p.barrier()

    sc0 = ExitStack()
    sb0 = mksb(sc0)
    wtile = [sb0(f"p0_wt{i}", (128, DM)) for i in range(2)]
    wtb = [sb0(f"p0_wb{i}", (128, DM), BF16) for i in range(2)]
    cast_tok = [None, None]
    wr_tok = [None, None]
    for fc in range(NFC):
        i = fc % 2
        ld = p.dma("sync", f"wt{i}", wtile[i][:], w_in_blk[fc], waits=[cast_tok[i]])
        cast_tok[i] = p.dve(lambda v, i=i: v.tensor_copy(wtb[i][:], wtile[i][:]),
                            waits=[ld, wr_tok[i]])
        wr_tok[i] = p.dma("pool", f"wbin{i}", wbin[fc], wtb[i][:], waits=[cast_tok[i]])
    p.barrier()
    t_wbin = p.tok("bar")
    sc0.close()

    ln_guard = [None]

    def ln_block(xb_ap, out_ap, D, waits_in, sqv):
        Pq = xb_ap.shape[0]
        sv = stat[0:Pq, :]
        qv = sqv[0:Pq, 0:D]
        wi = list(waits_in) + [ln_guard[0]]
        tr = p.dve(lambda v: v.reduce_sum(sv[:, 0:1], xb_ap, axis=AX.X), waits=wi)
        ts_ = p.act(lambda s: s.activation(qv, xb_ap, AF.Square, accum_out=sv[:, 1:2]), waits=wi)
        p.dve(lambda v: v.tensor_scalar(sv[:, 2:3], sv[:, 0:1], 1.0 / D, None, ALU.mult), waits=[tr])
        p.dve(lambda v: v.tensor_scalar(sv[:, 3:4], sv[:, 1:2], 1.0 / D, None, ALU.mult), waits=[ts_])
        p.dve(lambda v: v.tensor_mul(sv[:, 4:5], sv[:, 2:3], sv[:, 2:3]))
        p.dve(lambda v: v.tensor_sub(sv[:, 3:4], sv[:, 3:4], sv[:, 4:5]))
        tv = p.dve(lambda v: v.tensor_scalar(sv[:, 3:4], sv[:, 3:4], 1e-5, None, ALU.add))
        tw = p.act(lambda s: s.activation(sv[:, 5:6], sv[:, 3:4], AF.Sqrt), waits=[tv])
        tx = p.dve(lambda v: v.reciprocal(sv[:, 5:6], sv[:, 5:6]), waits=[tw])
        tf = p.dve(lambda v: v.tensor_scalar(out_ap, xb_ap, sv[:, 2:3], sv[:, 5:6],
                                             ALU.subtract, ALU.mult), waits=[tx])
        ln_guard[0] = tf
        return tf

    sc1 = ExitStack()
    sb1 = mksb(sc1)
    PT = [sc1.enter_context(nc.psum_tensor(f"pt{i}", [128, 1024], BF16)) for i in range(2)]
    ln1_fm = sb1("ln1_fm", (128, NDC * T), BF16)
    xbuf = sb1("xbuf", (128, DM))
    sq_scr = sb1("sq_scr", (128, DM), BF16)
    lntb = [sb1(f"lntb{i}", (128, DM), BF16) for i in range(2)]
    slab = [sb1(f"slab{i}", (128, DM), BF16) for i in range(2)]
    raw = [sb1(f"raw{i}", (128, T + 3)) for i in range(3)]
    accv = [sb1(f"accv{i}", (128, T)) for i in range(2)]
    stage_bf = [sb1(f"stage{i}", (128, T), BF16) for i in range(2)]
    zstage = [sb1(f"zst{i}", (128, 512), BF16) for i in range(2)]
    dtraw = sb1("dtraw", (64, T))
    dts1 = sb1("dts1", (64, T))
    dts2 = sb1("dts2", (64, T))
    a_fm = sb1("a_fm", (64, T))
    u_fm = sb1("u_fm", (64, T))
    datm = sb1("datm", (128, 64))
    utm_buf = sb1("utm_buf", (128, CPT * 64))
    cd_buf = sb1("cd_buf", (64, CPT))
    cdrow_buf = sb1("cdrow_buf", (1, CPT * 64))
    shi_buf = sb1("shi_buf", (64, T), BF16)
    slo_buf = sb1("slo_buf", (64, T), BF16)
    ahi_buf = sb1("ahi_buf", (64, T), BF16)
    alo_buf = sb1("alo_buf", (64, T), BF16)
    bm_tm = sb1("bm_tm", (128, CPT * 128), BF16)
    xfm_c = [sb1("xfm_c0", (128, DI), BF16)]
    xfm_c = [xfm_c[0], xfm_c[0]]
    xtm_c = sb1("xtm_c", (128, DI), BF16)
    xu_c = sb1("xu_c", (128, DI), BF16)
    cdb_f = sb1("cdb_f", (128, 64))
    halo_fm = sb1("halo_fm", (128, NDC * 3), BF16)
    p0tile = [sb1(f"p0tile{i}", (128, DM)) for i in range(2)]
    p0out = [sb1("p0out0", (128, DM), BF16)]
    p0tmp = sb1("p0tmp", (128, DM))
    p0sum = sb1("p0sum", (128, 4))

    dtp = dtraw
    ldt_v = dts1
    s32 = dts1
    dA_fm = dts2

    ld = p.dma("sync", "xin", xbuf[0:3, :], x_halo)
    thl = ln_block(xbuf[0:3, :], lntb[0][0:3, :], DM, [ld], sq_scr)
    hev = []
    for dc in range(NDC):
        tt_ = p.pe(lambda t, dc=dc: t.transpose(PT[dc % 2][:, 0:3],
                                                lntb[0][0:3, dc * 128:(dc + 1) * 128], ident_b[0:3, 0:3]),
                   waits=[thl, hev[dc - 2] if dc >= 2 else None])
        hev.append(p.dve(lambda v, dc=dc: v.tensor_copy(halo_fm[:, dc * 3:(dc + 1) * 3],
                                                        PT[dc % 2][:, 0:3]), waits=[tt_]))
    hslab_free = [None, None]
    hpb_free = [None, None]
    for k in range(NCC):
        fc = NZC + k
        i = k % 2
        sl = p.dma("sync", f"sl{i}", slab[i][:], wbin[fc], waits=[t_wbin, hslab_free[i]])
        mmt = None
        for dc in range(NDC):
            mmt = p.pe(lambda t, i=i, dc=dc: t.matmul(
                PB[i][:, 0:3], slab[i][:, dc * 128:(dc + 1) * 128], halo_fm[:, dc * 3:(dc + 1) * 3],
                start=(dc == 0), stop=(dc == NDC - 1)),
                waits=[sl, hev[dc], hpb_free[i]], inc=(dc == NDC - 1))
        hslab_free[i] = mmt
        hpb_free[i] = p.dve(lambda v, k=k, i=i: v.tensor_copy(carry[:, k * 3:(k + 1) * 3],
                                                              PB[i][:, 0:3]), waits=[mmt])
    t_carry = p.tok("dve")

    pbfree = [None] * 6
    ptfree = [None, None]
    slab_free = list(hslab_free)
    stage_free2 = [None, None]
    raw_free = [None] * 3
    acc_free = [None, None]
    stage_free = [None, None]
    zst_free = [None, None]
    lntb_free = [hev[-1], None]
    xbuf_free = [thl]
    xfm_free = XfmShared()
    xtm_free = [None]
    xtm_sp = [None]
    states_done = [None]
    pv_sp = [None]
    t_dt_prev = [None]
    conv_sp = {}
    bmtm_tok = [None]
    wr0 = [None, None, None]
    sum_free = [None, None]
    rsb_tok = [None]
    q_free = [None, None, None]
    wq_last = [None, None]
    t_tile_guard = [None]

    def p0b_gen_fn():
        for dmc in range(16):
            for half in range(2):
                i = (dmc * 2 + half) % 2
                oi = 0
                ld = p.dma("sync", f"p0_{i}", p0tile[i][:],
                           w_out_blk[dmc][:, half * 2048:(half + 1) * 2048],
                           waits=[wr0[i]])
                tq_ = p.dve(lambda v, half=half, i=i, oi=oi: v.tensor_tensor(
                    p0out[oi][:].rearrange("p (c m) -> p c m", m=128),
                    p0tile[i][:].rearrange("p (c m) -> p c m", m=128),
                    nw_cols[:, half * 16:(half + 1) * 16].rearrange(
                        "p (c o) -> p c o", o=1).broadcast_to((128, 16, 128)),
                    ALU.mult),
                    waits=[ld, wq_last[oi]])
                wr0[i] = tq_
                wq_last[oi] = p.dma("pool", f"wout{i}",
                                    wboutc[dmc][:, half * 2048:(half + 1) * 2048],
                                    p0out[oi][:], waits=[tq_])
                yield
        tz = p.dve(lambda v: v.memset(p0sum[:], 0.0))
        sum_free[0] = tz
        for k in range(8192 // 128 // n_cores):
            i = k % 2
            ld = p.dma("sync", f"p0_{i}", p0tile[i][:], w1s_in[k * 128:(k + 1) * 128, :],
                       waits=[sum_free[0]])
            tr = p.dve(lambda v, i=i: v.reduce_sum(p0sum[:, 2:3], p0tile[i][:], axis=AX.X,
                                                   apply_absolute_value=True), waits=[ld])
            sum_free[0] = p.dve(lambda v: v.tensor_add(p0sum[:, 0:1], p0sum[:, 0:1],
                                                       p0sum[:, 2:3]), waits=[tr])
            yield
        for k in range(DM // 128 // n_cores):
            for hf in range(4):
                i = (k * 4 + hf) % 2
                ld = p.dma("sync", f"p0_{i}", p0tile[i][:],
                           w2s_in[k * 128:(k + 1) * 128, hf * 2048:(hf + 1) * 2048],
                           waits=[sum_free[0]])
                tr = p.dve(lambda v, i=i: v.reduce_sum(p0sum[:, 2:3], p0tile[i][:], axis=AX.X,
                                                       apply_absolute_value=True), waits=[ld])
                sum_free[0] = p.dve(lambda v: v.tensor_add(p0sum[:, 1:2], p0sum[:, 1:2],
                                                           p0sum[:, 2:3]), waits=[tr])
                yield
        tm1 = p.pe(lambda t_: t_.matmul(PB[3][0:1, 384:385], ones_col_f[:], p0sum[:, 0:1],
                                        start=True, stop=True), waits=[sum_free[0], pbfree[3]])
        tm2 = p.pe(lambda t_: t_.matmul(PB[3][0:1, 385:386], ones_col_f[:], p0sum[:, 1:2],
                                        start=True, stop=True))
        tv = p.dve(lambda v: v.tensor_copy(sums2x[:], PB[3][0:1, 384:386]), waits=[tm1, tm2])
        pbfree[3] = tv
        tdm = p.dma("pool", "rsum", rsum_in, sums2x[:], waits=[tv])
        tcl = p.coll(lambda g: g.collective_compute(
            "AllReduce", ALU.add, replica_groups=cfg["allcores"], ins=[rsum_in],
            outs=[rsum_out]), waits=[tdm])
        ld = p.dma("sync", "rsum2", rs_row[:], rsum_out, waits=[tcl])
        t1 = p.dve(lambda v: v.tensor_scalar(rs_row[:], rs_row[:], 1.0 / (8192.0 * DM), None,
                                             ALU.mult), waits=[ld])
        t2 = p.dve(lambda v: v.tensor_scalar(rs_row[:], rs_row[:], 1e-8, None, ALU.max),
                   waits=[t1])
        t3 = p.dve(lambda v: v.reciprocal(rs_row[:], rs_row[:]), waits=[t2])
        t4 = p.pe(lambda t_: t_.matmul(PB[3][:, 386:388], ones_row_f[:], rs_row[:],
                                       start=True, stop=True), waits=[t3, pbfree[3]])
        rsb_tok[0] = p.dve(lambda v: v.tensor_copy(rsb[:], PB[3][:, 386:388]), waits=[t4])
        pbfree[3] = rsb_tok[0]
        yield
        tmp_g = [None]
        for w2mode in (False, True):
            steps = ([(dmc, qc) for dmc in range(16) for qc in range(4)] if w2mode
                     else [(fc, 0) for fc in range(64)])
            for a, qc in steps:
                i = (a * (4 if w2mode else 1) + qc) % 2
                oi = 0
                srcw = (w2_blk[a][:, qc * 2048:(qc + 1) * 2048] if w2mode
                        else w1_blk[a])
                dstw = (wb2c[a][:, qc * 2048:(qc + 1) * 2048] if w2mode
                        else wb1[a])
                ld = p.dma("sync", f"p0_{i}", p0tile[i][:], srcw,
                           waits=[rsb_tok[0], q_free[i]])
                sc = 1 if w2mode else 0
                t1 = p.dve(lambda v, i=i, sc=sc: v.tensor_scalar(
                    p0tmp[:], p0tile[i][:], rsb[:, sc:sc + 1], None, ALU.mult),
                    waits=[ld, tmp_g[0]])
                t2 = p.dve(lambda v, i=i: v.tensor_scalar(
                    p0tile[i][:], p0tmp[:], 0.5, -0.5, ALU.min, ALU.max), waits=[t1])
                t3 = p.dve(lambda v, i=i: v.tensor_sub(p0tmp[:], p0tmp[:], p0tile[i][:]),
                           waits=[t2])
                q_free[i] = t3
                t4 = p.act(lambda s, oi=oi: s.activation(p0out[oi][:], p0tmp[:], AF.Sign),
                           waits=[t3, wq_last[oi]])
                tmp_g[0] = t4
                wq_last[oi] = p.dma("pool", "wq", dstw, p0out[oi][:], waits=[t4])
                yield
        yield

    def pass1_tile(t):
        gc0 = t * CPT
        t_srec = p.tok("q_srec") if "q_srec" in p.C else None
        for tb in range(TBT):
            i = tb % 2
            ldx = p.dma("sync", "xin", xbuf[:], x_main[t * T + tb * 128:t * T + (tb + 1) * 128, :],
                        waits=[xbuf_free[0]])
            tln = ln_block(xbuf[:], lntb[i][:], DM, [ldx, lntb_free[i]], sq_scr)
            xbuf_free[0] = tln
            for g8 in range(2):
                pti = (tb * 2 + g8) % 2
                tts = None
                for k in range(8):
                    dc = g8 * 8 + k
                    tts = p.pe(lambda t_, dc=dc, k=k, pti=pti, i=i: t_.transpose(
                        PT[pti][:, k * 128:(k + 1) * 128], lntb[i][:, dc * 128:(dc + 1) * 128],
                        ident_b[:]), waits=[tln, ptfree[pti]])
                tev = p.dve(lambda v, g8=g8, tb=tb, pti=pti: v.tensor_copy(
                    ln1_fm[:].rearrange("p (dc tok) -> p dc tok", tok=T)[:, g8 * 8:(g8 + 1) * 8,
                                                                        tb * 128:(tb + 1) * 128],
                    PT[pti][:].rearrange("p (k f) -> p k f", f=128)),
                    waits=[tts, t_tile_guard[0]])
                ptfree[pti] = tev
                lntb_free[i] = tev
        t_ln1 = p.tok("dve")

        NQ = T // 512
        dt_ev = [None] * NQ
        for fc in range(NFC):
            i = fc % 2
            sl = p.dma("sync", f"sl{i}", slab[i][:], wbin[fc], waits=[t_wbin, slab_free[i]])
            for q in range(NQ):
                r = (fc * NQ + q) % 3
                mmt = None
                for dc in range(NDC):
                    mmt = p.pe(lambda t_, i=i, dc=dc, q=q, r=r: t_.matmul(
                        PB[r][:], slab[i][:, dc * 128:(dc + 1) * 128],
                        ln1_fm[:, dc * T + q * 512:dc * T + (q + 1) * 512],
                        start=(dc == 0), stop=(dc == NDC - 1)),
                        waits=[sl, t_ln1, pbfree[r]], inc=(dc == NDC - 1))
                if q == NQ - 1:
                    slab_free[i] = mmt
                if fc < NZC:
                    zi = (fc * NQ + q) % 2
                    tz = p.act(lambda s, r=r, zi=zi: s.activation(zstage[zi][:], PB[r][:], AF.Copy),
                               waits=[mmt, zst_free[zi]])
                    pbfree[r] = tz
                    zst_free[zi] = p.dma("pool", f"zw{zi}",
                                         z_scr[t * CPT + q * 4:t * CPT + q * 4 + 4, :,
                                               fc * 128:(fc + 1) * 128].rearrange("c p f -> p c f"),
                                         zstage[zi][:].rearrange("p (c f) -> p c f", f=128),
                                         waits=[tz])
                elif fc < NZC + NCC:
                    cc = fc - NZC
                    ri = cc % 3
                    if q == 0:
                        p.dve(lambda v, cc=cc, ri=ri: v.tensor_copy(
                            raw[ri][:, 0:3], carry[:, cc * 3:(cc + 1) * 3]),
                            waits=[t_carry, raw_free[ri]])
                    te = p.act(lambda s, r=r, ri=ri, q=q: s.activation(
                        raw[ri][:, 3 + q * 512:3 + (q + 1) * 512], PB[r][:], AF.Copy),
                        waits=[mmt])
                    pbfree[r] = te
                    if q == NQ - 1:
                        ai = cc % 2
                        t0 = p.dve(lambda v, ri=ri, cc=cc, ai=ai: v.tensor_scalar(
                            accv[ai][:], raw[ri][:, 0:T], cw_cols[:, cc * 4:cc * 4 + 1], None,
                            ALU.mult), waits=[te, acc_free[ai]])
                        for k in range(1, 4):
                            t0 = p.dve(lambda v, ri=ri, cc=cc, ai=ai, k=k: v.scalar_tensor_tensor(
                                accv[ai][:], raw[ri][:, k:k + T],
                                cw_cols[:, cc * 4 + k:cc * 4 + k + 1],
                                accv[ai][:], ALU.mult, ALU.add), waits=[t0])
                        raw_free[ri] = p.dve(lambda v, ri=ri, cc=cc: v.tensor_copy(
                            carry[:, cc * 3:(cc + 1) * 3], raw[ri][:, T:T + 3]), waits=[te])
                        si = cc % 2
                        tsil = p.act(lambda s, ai=ai, cc=cc, si=si: s.activation(
                            stage_bf[si][:], accv[ai][:], AF.Silu, bias=cb_cols[:, cc:cc + 1]),
                            waits=[t0, stage_free[si], stage_free2[si]])
                        acc_free[ai] = tsil
                        if cc < NZC:
                            spt = p.dma("pool", f"st{si}",
                                        xfm_scr[t * CPT:(t + 1) * CPT, :,
                                                cc * 128:(cc + 1) * 128].rearrange("c p f -> p c f"),
                                        stage_bf[si][:].rearrange("p (c f) -> p c f", f=128),
                                        waits=[tsil])
                        else:
                            spt = p.dma("pool", f"st{si}",
                                        bc_scr[t * CPT:(t + 1) * CPT, :,
                                               (cc - NZC) * 128:(cc - NZC + 1) * 128].rearrange("c p f -> p c f"),
                                        stage_bf[si][:].rearrange("p (c f) -> p c f", f=128),
                                        waits=[tsil])
                        conv_sp[(t, cc)] = spt
                        stage_free[si] = spt
                        stage_free2[si] = None
                        if cc == NZC:
                            bev = None
                            bpe = None
                            for c in range(CPT):
                                pti = c % 2
                                bpe = p.pe(lambda t_, c=c, si=si, pti=pti: t_.transpose(
                                    PT[pti][:, 0:128], stage_bf[si][:, c * 128:(c + 1) * 128],
                                    ident_b[:]), waits=[tsil, ptfree[pti]])
                                bev = p.dve(lambda v, c=c, pti=pti: v.tensor_copy(
                                    bm_tm[:, c * 128:(c + 1) * 128], PT[pti][:, 0:128]),
                                    waits=[bpe])
                                ptfree[pti] = bev
                            bmtm_tok[0] = bev
                            stage_free2[si] = bpe
                else:
                    tdd = p.dve(lambda v, r=r, q=q: v.tensor_copy(dtraw[:, q * 512:(q + 1) * 512],
                                                                  PB[r][0:64, :]),
                                waits=[mmt, t_dt_prev[0]])
                    pbfree[r] = tdd
                    dt_ev[q] = tdd
            next(p0b_gen, None)
        w0 = list(dt_ev)
        ta = p.act(lambda s: s.activation(dts1[:], dtraw[:], AF.Relu, bias=dtb_col[:]), waits=w0)
        tb_ = p.act(lambda s: s.activation(dts2[:], dtraw[:], AF.Abs, bias=dtb_col[:]), waits=w0)
        tc = p.act(lambda s: s.activation(dts2[:], dts2[:], AF.Exp, scale=-1.0), waits=[tb_])
        tdv = p.dve(lambda v: v.tensor_scalar(dts2[:], dts2[:], 1.0, None, ALU.add), waits=[tc])
        te = p.act(lambda s: s.activation(dts2[:], dts2[:], AF.Ln), waits=[tdv])
        tf = p.dve(lambda v: v.tensor_add(dtp[:], dts1[:], dts2[:]), waits=[te, ta])
        tg = p.act(lambda s: s.activation(ldt_v[:], dtp[:], AF.Ln), waits=[tf])
        th = p.dve(lambda v: v.tensor_scalar(dA_fm[:], dtp[:], a_col[:], None, ALU.mult),
                   waits=[tf, te])
        for c in range(CPT):
            t1 = p.pe(lambda t_, c=c: t_.transpose(PB[3][0:128, 0:64],
                                                   dA_fm[:, c * 128:(c + 1) * 128],
                                                   ident_f[0:64, 0:64]),
                      waits=[th, pbfree[3]])
            t2 = p.dve(lambda v: v.tensor_copy(datm[:], PB[3][:, 0:64]), waits=[t1])
            t3 = p.pe(lambda t_: t_.matmul(PB[3][0:64, 64:192], datm[:], tlow[:], start=True,
                                           stop=True), waits=[t2])
            t4 = p.dve(lambda v, c=c: v.tensor_copy(a_fm[:, c * 128:(c + 1) * 128],
                                                    PB[3][0:64, 64:192]), waits=[t3])
            tcd = p.act(lambda s, c=c: s.activation(cd_buf[:, c:c + 1],
                                                    a_fm[:, c * 128 + 127:c * 128 + 128], AF.Exp),
                        waits=[t4])
            tu = p.act(lambda s, c=c: s.activation(u_fm[:, c * 128:(c + 1) * 128],
                                                   a_fm[:, c * 128:(c + 1) * 128], AF.Exp,
                                                   scale=-1.0,
                                                   bias=a_fm[:, c * 128 + 127:c * 128 + 128]),
                       waits=[t4])
            tu2 = p.dve(lambda v, c=c: v.tensor_mul(u_fm[:, c * 128:(c + 1) * 128],
                                                    u_fm[:, c * 128:(c + 1) * 128],
                                                    dtp[:, c * 128:(c + 1) * 128]), waits=[tu])
            t5 = p.pe(lambda t_, c=c: t_.transpose(PB[3][0:128, 192:256],
                                                   u_fm[:, c * 128:(c + 1) * 128],
                                                   ident_f[0:64, 0:64]),
                      waits=[tu2, t4])
            t6 = p.dve(lambda v, c=c: v.tensor_copy(utm_buf[:, c * 64:(c + 1) * 64],
                                                    PB[3][:, 192:256]), waits=[t5])
            t7 = p.pe(lambda t_, c=c: t_.transpose(PB[3][0:1, 256:320], cd_buf[:, c:c + 1],
                                                   ident_f[0:64, 0:64]), waits=[tcd, t6])
            t8 = p.dve(lambda v, c=c: v.tensor_copy(cdrow_buf[:, c * 64:(c + 1) * 64],
                                                    PB[3][0:1, 256:320]), waits=[t7, t_srec])
            pbfree[3] = ("dve", max(t6[1], t8[1]))
            ta1 = p.dve(lambda v, c=c: v.tensor_copy(ahi_buf[:, c * 128:(c + 1) * 128],
                                                     a_fm[:, c * 128:(c + 1) * 128]),
                        waits=[t4, t_srec])
            p.dve(lambda v, c=c: v.tensor_sub(alo_buf[:, c * 128:(c + 1) * 128],
                                              a_fm[:, c * 128:(c + 1) * 128],
                                              ahi_buf[:, c * 128:(c + 1) * 128]), waits=[ta1])
            ts1 = p.dve(lambda v, c=c: v.tensor_sub(s32[:, c * 128:(c + 1) * 128],
                                                    ldt_v[:, c * 128:(c + 1) * 128],
                                                    a_fm[:, c * 128:(c + 1) * 128]),
                        waits=[tg, t4, t_srec])
            ts2 = p.dve(lambda v, c=c: v.tensor_copy(shi_buf[:, c * 128:(c + 1) * 128],
                                                     s32[:, c * 128:(c + 1) * 128]), waits=[ts1])
            p.dve(lambda v, c=c: v.tensor_sub(slo_buf[:, c * 128:(c + 1) * 128],
                                              s32[:, c * 128:(c + 1) * 128],
                                              shi_buf[:, c * 128:(c + 1) * 128]), waits=[ts2])
        t_dt = p.tok("dve")
        p.dma("pool", "srec", srec_scr[gc0:gc0 + CPT, 0:64, :].rearrange("c h i -> h c i"),
              shi_buf[:].rearrange("h (c i) -> h c i", i=128), waits=[t_dt])
        p.dma("pool", "srec", srec_scr[gc0:gc0 + CPT, 64:128, :].rearrange("c h i -> h c i"),
              slo_buf[:].rearrange("h (c i) -> h c i", i=128), waits=[t_dt])
        p.dma("pool", "srec", aflat_scr[gc0:gc0 + CPT, 0].rearrange("c h i -> h c i"),
              ahi_buf[:].rearrange("h (c i) -> h c i", i=128), waits=[t_dt])
        p.dma("pool", "srec", aflat_scr[gc0:gc0 + CPT, 1].rearrange("c h i -> h c i"),
              alo_buf[:].rearrange("h (c i) -> h c i", i=128), waits=[t_dt])
        p.dma("pool", "srec", cd_scr[gc0:gc0 + CPT, :].rearrange("(o c) h -> o c h", o=1),
              cdrow_buf[:].rearrange("o (c h) -> o c h", h=64), waits=[t_dt])

        for c in range(CPT):
            gc = gc0 + c
            i = c % 2
            lxf = p.dma("act", f"rx{i}", xfm_c[i][:], xfm_scr[gc],
                        waits=[conv_sp[(t, NZC - 1)], conv_sp[(t, NZC - 2)], xfm_free[i]])
            pv_sp[0] = p.dma("pool", "pvw", pv_scr[gc], S_run[:], waits=[states_done[0], pv_sp[0]])
            t_xtm = None
            for g8 in range(4):
                pti = g8 % 2
                tts = None
                for k in range(8):
                    cc = g8 * 8 + k
                    tts = p.pe(lambda t_, cc=cc, k=k, pti=pti, i=i: t_.transpose(
                        PT[pti][:, k * 128:(k + 1) * 128], xfm_c[i][:, cc * 128:(cc + 1) * 128],
                        ident_b[:]), waits=[lxf, ptfree[pti]])
                t_xtm = p.dve(lambda v, g8=g8, pti=pti: v.tensor_copy(
                    xtm_c[:, g8 * 1024:(g8 + 1) * 1024], PT[pti][:]),
                    waits=[tts, xtm_free[0], xtm_sp[0]])
                ptfree[pti] = t_xtm
            xfm_free[i] = p.tok("pe")
            xtm_sp[0] = p.dma("pool", "xtw", xtm_scr[gc], xtm_c[:], waits=[t_xtm])
            txu = p.dve(lambda v, c=c: v.tensor_mul(
                xu_c[:].rearrange("p (h f) -> p h f", f=64),
                xtm_c[:].rearrange("p (h f) -> p h f", f=64),
                utm_buf[:, c * 64:(c + 1) * 64].rearrange("p (h f) -> p h f", f=1
                                                          ).broadcast_to((128, 64, 64))),
                waits=[t_xtm, t_dt])
            xtm_free[0] = txu
            tc1 = p.pe(lambda t_, c=c: t_.matmul(PB[3][:, 320:384], ones_row_f[:],
                                                 cdrow_buf[0:1, c * 64:(c + 1) * 64],
                                                 start=True, stop=True), waits=[t_dt, pbfree[3]])
            tc2 = p.dve(lambda v: v.tensor_copy(cdb_f[:], PB[3][:, 320:384]), waits=[tc1])
            pbfree[3] = tc2
            sfin = None
            for grp in range(8):
                r = 2 + grp % 4
                tmm = p.pe(lambda t_, grp=grp, c=c, r=r: t_.matmul(
                    PB[r][:], bm_tm[:, c * 128:(c + 1) * 128], xu_c[:, grp * 512:(grp + 1) * 512],
                    start=True, stop=True), waits=[txu, bmtm_tok[0], pbfree[r]])
                tsc_ = p.dve(lambda v, grp=grp: v.tensor_mul(
                    S_run[:, grp * 512:(grp + 1) * 512].rearrange("p (h f) -> p h f", f=64),
                    S_run[:, grp * 512:(grp + 1) * 512].rearrange("p (h f) -> p h f", f=64),
                    cdb_f[:, grp * 8:(grp + 1) * 8].rearrange(
                        "p (h o) -> p h o", o=1).broadcast_to((128, 8, 64))),
                    waits=[tc2, pv_sp[0]])
                sfin = p.dve(lambda v, grp=grp, r=r: v.tensor_add(
                    S_run[:, grp * 512:(grp + 1) * 512],
                    S_run[:, grp * 512:(grp + 1) * 512], PB[r][:]),
                    waits=[tsc_, tmm])
                pbfree[r] = sfin
            states_done[0] = sfin
        t_dt_prev[0] = p.tok("dve")

    p0b_gen = p0b_gen_fn()
    for t in range(NTIL):
        t_tile_guard[0] = p.tok("pe")
        pass1_tile(t)
    for _ in p0b_gen:
        pass

    tag = p.dma("pool", "ag", ag_in, S_run[:], waits=[states_done[0]])
    p.coll(lambda g: g.collective_compute(
        "AllGather", ALU.bypass, replica_groups=cfg["pairs"], ins=[ag_in], outs=[ag_out]),
        waits=[tag])
    p.barrier()
    sc1.close()

    sc2 = ExitStack()
    sb2 = mksb(sc2)
    PBX = [sc2.enter_context(nc.psum_tensor(f"pbx{i}", [128, 512], F32)) for i in range(2)]
    hseld = sb2("hseld", (128, 8192), BF16)
    fpad = [sb2("fpad0", (128, 8192), BF16)]
    ones128 = sb2("ones128", (128, 128), BF16)
    sq_acc = [sb2(f"sq_acc{i}", (128, 512)) for i in range(2)]
    S_init = sb2("S_init", (128, DI), BF16)
    xfm_c3 = [sb2("xfm_c30", (128, DI), BF16)] * 2
    d_grp = sb2("d_grp", (128, 32))
    cd_tile = sb2("cd_tile", (1, (CPT + 1) * 64))
    z_c = [sb2(f"z_c{i}", (128, DI), BF16) for i in range(2)]
    xtm_c2 = [sb2(f"xtm_c2{i}", (128, DI), BF16) for i in range(2)]
    pvr_c = [sb2("pvr_c0", (128, DI), BF16)] * 2
    bc_c = [sb2(f"bc_c{i}", (128, 256), BF16) for i in range(2)]
    srec_c = [sb2(f"srec_c{i}", (128, 128), BF16) for i in range(2)]
    cpall = sb2("cpall", (1, CPT * 64))
    cpb_all = sb2("cpb_all", (128, CPT * 64), BF16)
    g_rep = sb2("g_rep", (128, 512), BF16)
    m_sb = [sb2(f"m_sb{i}", (128, 512), BF16) for i in range(4)]
    e_sb = [sb2(f"e_sb{i}", (128, 512), BF16) for i in range(2)]
    yoff_sb = [sb2(f"yoff_sb{i}", (128, 512), BF16) for i in range(2)]
    gt1 = [sb2(f"gt1_{i}", (128, 512)) for i in range(2)]
    gt2 = [sb2(f"gt2_{i}", (128, 512), BF16) for i in range(2)]
    zs_sb = [sb2(f"zs_sb{i}", (128, 512), BF16) for i in range(2)]
    pcor = sb2("pcor", (128, 512), BF16)
    sq_sb = [sb2("sq_sb0", (128, 512))] * 2
    sq_free = [None, None]
    sqa_free = [None, None]
    rms_pend = []
    yn_q = sb2("yn_q", (128, NZC * HLF), BF16)
    rms2 = sb2("rms2", (1, 128))
    rstd_cols = sb2("rstd_cols", (128, CPT))
    oslab = [sb2(f"oslab{i}", (128, 32 * 128), BF16) for i in range(2)]
    o_ev = sb2("o_ev", (128, HLF))
    o_tbs = sb2("o_tbs", (128, CPH * DM))
    xres = sb2("xres", (128, DM))
    lntb2 = sb2("lntb2", (128, DM), BF16)
    sq_scr2 = sb2("sq_scr2", (128, DM), BF16)

    lhs = p.dma("sync", "setup", hseld[:], c_hseld)
    p.dve(lambda v: v.memset(fpad[0][:], 0.0))
    p.dve(lambda v: v.memset(ones128[:], 1.0))
    lds = p.dma("pool", "ag", S_init[:], ag_out[0:128, :], waits=[p.tok("coll")])
    tsi = p.dve(lambda v: v.tensor_scalar(S_init[:], S_init[:], smask_c[:, 0:1], None, ALU.mult),
                waits=[lds])
    for pr in range(32):
        p.dve(lambda v, pr=pr: v.tensor_copy(d_grp[0:64, pr:pr + 1], db_full[0:64, 2 * pr:2 * pr + 1]))
        p.dve(lambda v, pr=pr: v.tensor_copy(d_grp[64:128, pr:pr + 1],
                                             db_full[64:128, 2 * pr + 1:2 * pr + 2]))
    p.barrier()

    rl_free = {}
    m_tok = [None] * 4
    e_free = [None, None]
    yo_free = [None, None]
    yoffb_free = [None, None]
    ymm_free = [None, None]
    gt_free = [None, None]
    pcor_free = [None]
    ynf = [None]
    oslab_free = [None, None]
    epi_free = [None]
    lnt2_free = [None]
    xres_free = [None]

    ld_pend = {}
    cpg = [None]

    def emit_reloads(t, c):
        gc = t * CPT + c
        i = c % 2
        lz = p.dma("pool", f"rz{i}", z_c[i][:], z_scr[gc], waits=[rl_free.get(("z", i))])
        lx = p.dma("act", "rxf", xfm_c3[0][:], xfm_scr[gc], waits=[rl_free.get(("xf", 0))])
        lt = p.dma("act", f"rxt{i}", xtm_c2[i][:], xtm_scr[gc], waits=[rl_free.get(("xt", i))])
        lp = p.dma("sync", "rpv", pvr_c[0][:], pv_scr[gc], waits=[rl_free.get(("pv", 0))])
        lb = p.dma("pool", f"rbc{i}", bc_c[i][:], bc_scr[gc], waits=[rl_free.get(("bc", i))])
        lf = p.dma("act", "rfl", fpad[0][0:2, :],
                   aflat_scr[gc].rearrange("a h f -> a (h f)"),
                   waits=[rl_free.get(("fl", 0))])
        lr = p.dma("pool", f"rsr{i}", srec_c[i][:], srec_scr[gc], waits=[rl_free.get(("sr", i))])
        ld_pend[gc] = (lz, lx, lt, lp, lb, lf, lr)

    def pass2_attn_tile(t):
        gc0 = t * CPT
        if gc0 > 0:
            ldc = p.dma("pool", "rcd", cd_tile[:].rearrange("o (c h) -> o c h", h=64),
                        cd_scr[gc0 - 1:gc0 + CPT, :].rearrange("(o c) h -> o c h", o=1),
                        waits=[rl_free.get(("cd", 0))])
        else:
            ldc = p.dma("pool", "rcd", cd_tile[0:1, 64:].rearrange("o (c h) -> o c h", h=64),
                        cd_scr[0:CPT, :].rearrange("(o c) h -> o c h", o=1),
                        waits=[rl_free.get(("cd", 0))])
        tprev = None
        for c in range(CPT):
            if t == 0 and c == 0:
                tprev = p.dve(lambda v: v.tensor_copy(cpall[:, 0:64], cp_row[:]), waits=[cpg[0]])
            else:
                srcv = cp_row[:] if c == 0 else cpall[:, (c - 1) * 64:c * 64]
                tprev = p.dve(lambda v, c=c, srcv=srcv: v.tensor_mul(
                    cpall[:, c * 64:(c + 1) * 64], srcv, cd_tile[0:1, c * 64:(c + 1) * 64]),
                    waits=[ldc, cpg[0]])
        tcpr = p.dve(lambda v: v.tensor_copy(cp_row[:], cpall[:, (CPT - 1) * 64:CPT * 64]),
                     waits=[tprev])
        for off in range(0, CPT * 64, 512):
            w = min(512, CPT * 64 - off)
            cpmm = p.pe(lambda t_, off=off, w=w: t_.matmul(
                PB[3][:, 0:w], ones_row_f[:], cpall[0:1, off:off + w],
                start=True, stop=True), waits=[tprev, pbfree[3]])
            pbfree[3] = p.dve(lambda v, off=off, w=w: v.tensor_copy(
                cpb_all[:, off:off + w], PB[3][:, 0:w]), waits=[cpmm, cpg[0]])
        cpg[0] = pbfree[3]
        if gc0 == 0:
            emit_reloads(0, 0)
        for qh in range(NQH):
            yn_last = None
            for ci in range(CPH):
                c = qh * CPH + ci
                gc = t * CPT + c
                i = c % 2
                lz, lx, lt, lp, lb, lf, lr = ld_pend.pop(gc)

                def corr_grp(grp):
                    t1_ = p.pool(lambda g, grp=grp, c=c: g.tensor_mul(
                        pcor[:].rearrange("p (h f) -> p h f", f=64),
                        S_init[:, grp * 512:(grp + 1) * 512].rearrange("p (h f) -> p h f", f=64),
                        cpb_all[:, c * 64 + grp * 8:c * 64 + (grp + 1) * 8].rearrange(
                            "p (h o) -> p h o", o=1).broadcast_to((128, 8, 64))),
                        waits=[lp, tsi, cpg[0], pcor_free[0]])
                    tk = p.pool(lambda g, grp=grp, i=i: g.tensor_add(
                        pvr_c[i][:, grp * 512:(grp + 1) * 512],
                        pvr_c[i][:, grp * 512:(grp + 1) * 512], pcor[:]), waits=[t1_])
                    pcor_free[0] = tk
                    return tk
                corr_tok = [None] * 9
                corr_tok[0] = corr_grp(0)
                tgm = p.pe(lambda t_, i=i: t_.matmul(PB[2][:, 0:128], bc_c[i][:, 0:128],
                                                     bc_c[i][:, 128:256], start=True, stop=True),
                           waits=[lb, pbfree[2]])
                tge = p.dve(lambda v: v.tensor_copy(g_rep[:, 0:128], PB[2][:, 0:128]),
                            waits=[tgm])
                pbfree[2] = tge
                tgr = p.dve(lambda v: v.tensor_copy(
                    g_rep[:, 128:512].rearrange("p (a f) -> p a f", f=128),
                    g_rep[:, 0:128].rearrange("p (a f) -> p a f", a=1).broadcast_to((128, 3, 128))),
                    waits=[tge])
                def mbuild(g8):
                    for s4 in range(2):
                        g4 = g8 * 2 + s4
                        r = g4 % 2
                        mi = g4 % 4
                        p.pe(lambda t_, r=r: t_.matmul(PB[r][:], masknegT[:], irep[:],
                                                       start=True, stop=False),
                             waits=[lf, lr, pbfree[r]], inc=False)
                        p.pe(lambda t_, g4=g4, r=r, i=i: t_.matmul(
                            PB[r][:], srec_c[i][:], hseld[:, g4 * 512:(g4 + 1) * 512],
                            start=False, stop=False), inc=False)
                        mmf = p.pe(lambda t_, g4=g4, r=r: t_.matmul(
                            PB[r][:], ones128[:], fpad[0][:, g4 * 512:(g4 + 1) * 512],
                            start=False, stop=True))
                        tex = p.act(lambda s, r=r, mi=mi: s.activation(m_sb[mi][:], PB[r][:],
                                                                      AF.Exp),
                                    waits=[mmf, m_tok[mi]])
                        pbfree[r] = tex
                        m_tok[mi] = p.dve(lambda v, mi=mi: v.tensor_mul(
                            m_sb[mi][:], m_sb[mi][:], g_rep[:]), waits=[tex, tgr])
                mbuild(0)
                while rms_pend:
                    rms_pend.pop(0)()
                for g8 in range(8):
                    gi = g8 % 2
                    if g8 + 1 < 8:
                        corr_tok[g8 + 1] = corr_grp(g8 + 1)
                    tpc = corr_tok[g8]
                    fpv = fpad[0][:].rearrange("p (h f) -> p h f", f=128)
                    mmtop = p.pe(lambda t_, g8=g8, fpv=fpv: t_.matmul(
                        PB[2][0:64, :], ones128[:, 0:64], fpv[:, 8 * g8:8 * g8 + 8:2, :],
                        start=True, stop=True), waits=[lf, pbfree[2]])
                    mm3 = p.pe(lambda t_, g8=g8, fpv=fpv: t_.matmul(
                        PB[2][64:128, :], ones128[:, 0:64], fpv[:, 8 * g8 + 1:8 * g8 + 8:2, :],
                        start=True, stop=True, tile_position=(0, 64)))
                    tee = p.act(lambda s, gi=gi: s.activation(e_sb[gi][:], PB[2][:], AF.Exp),
                                waits=[mmtop, mm3, e_free[gi]])
                    pbfree[2] = tee
                    tsl = p.act(lambda s, g8=g8, i=i, gi=gi: s.activation(
                        zs_sb[gi][:], z_c[i][:, g8 * 512:(g8 + 1) * 512], AF.Silu),
                        waits=[lz, gt_free[gi]])
                    tc2 = p.pool(lambda g, g8=g8, gi=gi: g.tensor_mul(
                        gt2[gi][:].rearrange("p (a f) -> p a f", f=128),
                        xfm_c3[i][:, g8 * 512:(g8 + 1) * 512].rearrange("p (a f) -> p a f", f=128),
                        d_grp[:, g8 * 4:(g8 + 1) * 4].rearrange("p (a o) -> p a o", o=1
                                                                ).broadcast_to((128, 4, 128))),
                        waits=[lx, gt_free[gi]])
                    yob = PB[3] if gi == 0 else PBX[1]
                    yo = None
                    for pl in range(4):
                        pr = g8 * 4 + pl
                        yo = p.pe(lambda t_, pr=pr, pl=pl, i=i, yob=yob: t_.matmul(
                            yob[:, pl * 128:(pl + 1) * 128], pvr_c[i][:, pr * 128:(pr + 1) * 128],
                            bc_c[i][:, 128:256], start=True, stop=True),
                            waits=[tpc, yo_free[gi]] + ([pbfree[3]] if gi == 0 else []))
                    tyo = p.dve(lambda v, gi=gi, yob=yob: v.tensor_mul(
                        yoff_sb[gi][:], yob[:], e_sb[gi][:]),
                        waits=[yo, tee, yoffb_free[gi]])
                    yo_free[gi] = tyo
                    e_free[gi] = tyo
                    if gi == 0:
                        pbfree[3] = tyo
                    if g8 + 1 < 8:
                        mbuild(g8 + 1)
                    ymb = PB[4] if gi == 0 else PBX[0]
                    ymm = None
                    for k in range(8):
                        h = g8 * 8 + k
                        pl = k // 2
                        hf_ = k % 2
                        s4 = 0 if k < 4 else 1
                        mi = (g8 * 2 + s4) % 4
                        hl = k % 4
                        ymm = p.pe(lambda t_, h=h, pl=pl, hf_=hf_, mi=mi, hl=hl, ymb=ymb, i=i: t_.matmul(
                            ymb[hf_ * 64:(hf_ + 1) * 64, pl * 128:(pl + 1) * 128],
                            xtm_c2[i][:, h * 64:(h + 1) * 64], m_sb[mi][:, hl * 128:(hl + 1) * 128],
                            start=True, stop=True, tile_position=(0, hf_ * 64)),
                            waits=[lt, m_tok[mi], ymm_free[gi]])
                    tc1 = p.dve(lambda v, gi=gi, ymb=ymb: v.tensor_add(
                        gt1[gi][:], ymb[:], yoff_sb[gi][:]), waits=[ymm, tyo])
                    ymm_free[gi] = tc1
                    yoffb_free[gi] = tc1
                    tc3 = p.dve(lambda v, gi=gi: v.tensor_add(gt1[gi][:], gt1[gi][:], gt2[gi][:]),
                                waits=[tc1, tc2])
                    ynv = yn_q[:].rearrange("p (cc tok) -> p cc tok", tok=HLF)[
                        :, g8 * 4:(g8 + 1) * 4, ci * 128:(ci + 1) * 128]
                    tc4 = p.dve(lambda v, ynv=ynv, gi=gi: v.tensor_mul(
                        ynv, gt1[gi][:].rearrange("p (a f) -> p a f", f=128),
                        zs_sb[gi][:].rearrange("p (a f) -> p a f", f=128)),
                        waits=[tc3, tsl, ynf[0]])
                    gt_free[gi] = tc4
                    yn_last = tc4
                    sqi = 0
                    tsq = p.act(lambda s, sqi=sqi, ynv=ynv: s.activation(
                        sq_sb[sqi][:].rearrange("p (a f) -> p a f", f=128), ynv, AF.Square),
                        waits=[tc4, sq_free[sqi]])
                    sai = c % 2
                    tacc = p.pool(lambda g, sqi=sqi, g8=g8, sai=sai: (
                        g.tensor_copy(sq_acc[sai][:], sq_sb[sqi][:]) if g8 == 0
                        else g.tensor_add(sq_acc[sai][:], sq_acc[sai][:], sq_sb[sqi][:])),
                        waits=[tsq, sqa_free[sai] if g8 == 0 else None])
                    sq_free[sqi] = tacc
                dvnow = p.tok("dve")
                for nm, ii in [("z", i), ("xf", 0), ("xt", i), ("pv", 0), ("bc", i),
                               ("fl", 0), ("sr", i)]:
                    rl_free[(nm, ii)] = dvnow
                rl_free[("cd", 0)] = dvnow
                if gc + 1 < NCHUNK:
                    emit_reloads((gc + 1) // CPT, (gc + 1) % CPT)

                def rms_emit(c=c, tacc=tacc, sai=c % 2):
                    rms_mm = p.pe(lambda t_, sai=sai: t_.matmul(
                        PB[5][0:1, :], ones_col_f[:], sq_acc[sai][:],
                        start=True, stop=True), waits=[tacc, pbfree[5]])
                    sqa_free[sai] = rms_mm
                    tr1 = p.dve(lambda v: v.tensor_reduce(
                        rms2[:], PB[5][0:1, :].rearrange("o (s i) -> o i s", s=4), AX.X, ALU.add),
                        waits=[rms_mm])
                    tr2_ = p.dve(lambda v: v.tensor_scalar(rms2[:], rms2[:], 1.0 / DI, None,
                                                           ALU.mult), waits=[tr1])
                    tr3 = p.dve(lambda v: v.tensor_scalar(rms2[:], rms2[:], 1e-5, None, ALU.add),
                                waits=[tr2_])
                    tr4 = p.act(lambda s: s.activation(rms2[:], rms2[:], AF.Sqrt), waits=[tr3])
                    tr5 = p.dve(lambda v: v.reciprocal(rms2[:], rms2[:]), waits=[tr4])
                    tr6 = p.pe(lambda t_: t_.transpose(PB[5][0:128, 0:1], rms2[:],
                                                       ident_f[0:1, 0:1]), waits=[tr5, tr1])
                    tr7 = p.dve(lambda v, c=c: v.tensor_copy(rstd_cols[:, c:c + 1], PB[5][:, 0:1]),
                                waits=[tr6])
                    pbfree[5] = tr7
                rms_pend.append(rms_emit)
            while rms_pend:
                rms_pend.pop(0)()
            ot_done = None
            for dmc in range(16):
                io = dmc % 2
                lsl = p.dma("sync", f"os{io}", oslab[io][:], wboutc[dmc],
                            waits=[oslab_free[io]])
                r = dmc % 2
                mmt = None
                for dic in range(NZC):
                    mmt = p.pe(lambda t_, dmc=dmc, dic=dic, io=io, r=r: t_.matmul(
                        PB[r][:, 0:HLF], oslab[io][:, dic * 128:(dic + 1) * 128],
                        yn_q[:, dic * HLF:(dic + 1) * HLF], start=(dic == 0),
                        stop=(dic == NZC - 1)),
                        waits=[lsl, yn_last, pbfree[r]], inc=(dic == NZC - 1))
                oslab_free[io] = mmt
                tev = p.dve(lambda v, r=r: v.tensor_copy(o_ev[:], PB[r][:, 0:HLF]), waits=[mmt])
                pbfree[r] = tev
                for tb in range(CPH):
                    rr = 2 + tb % 2
                    tt_ = p.pe(lambda t_, tb=tb, rr=rr: t_.transpose(
                        PB[rr][:, 128:256], o_ev[:, tb * 128:(tb + 1) * 128], ident_f[:]),
                        waits=[tev, pbfree[rr]])
                    ot_done = p.dve(lambda v, tb=tb, dmc=dmc, rr=rr: v.tensor_copy(
                        o_tbs[:, tb * DM + dmc * 128:tb * DM + (dmc + 1) * 128],
                        PB[rr][:, 128:256]), waits=[tt_])
                    pbfree[rr] = ot_done
            ynf[0] = p.tok("pe")
            for tb in range(CPH):
                c = qh * CPH + tb
                row0 = t * T + qh * HLF + tb * 128
                ldr = p.dma("sync", "xr", xres[:], x_main[row0:row0 + 128, :], waits=[xres_free[0]])
                tsc = p.dve(lambda v, tb=tb, c=c: v.tensor_scalar(
                    o_tbs[:, tb * DM:(tb + 1) * DM], o_tbs[:, tb * DM:(tb + 1) * DM],
                    rstd_cols[:, c:c + 1], None, ALU.mult), waits=[ot_done])
                tln = ln_block(o_tbs[:, tb * DM:(tb + 1) * DM], lntb2[:], DM, [tsc, lnt2_free[0]],
                               sq_scr2)
                tx2 = p.dve(lambda v: v.tensor_add(xres[:], xres[:], lntb2[:]), waits=[tln, ldr])
                lnt2_free[0] = tx2
                xres_free[0] = p.dma("pool", "x2w", x2_scr[row0:row0 + 128, :], xres[:], waits=[tx2])

    for t in range(NTIL):
        pass2_attn_tile(t)
    p.barrier()
    sc2.close()

    sc3 = ExitStack()
    sb3 = mksb(sc3)
    PT3 = [sc3.enter_context(nc.psum_tensor(f"pt3{i}", [128, 1024], BF16)) for i in range(2)]
    ln3_fm = sb3("ln3_fm", (128, NDC * TF), BF16)
    h1_buf = sb3("h1_buf", (128, 64 * TF), BF16)
    fslab = [sb3(f"fslab{i}", (128, DM), BF16) for i in range(2)]
    f2slab = [sb3(f"f2slab{i}", (128, 64 * 128), BF16) for i in range(2)]
    x2buf = sb3("x2buf", (128, DM))
    lntb3 = [sb3(f"lntb3{i}", (128, DM), BF16) for i in range(2)]
    o_tbs3 = sb3("o_tbs3", (128, TBF * DM))
    o_ev3 = sb3("o_ev3", (128, TF))
    b2_rep = sb3("b2_rep", (128, DM))
    sq_scr3 = sb3("sq_scr3", (128, DM), BF16)
    xr3 = x2buf

    ldb2 = p.dma("sync", "setup", x2buf[0:1, :], b2_in.rearrange("(o m) -> o m", o=1))
    for q in range(4):
        tmq = p.pe(lambda t_, q=q: t_.matmul(PB[q][:], ones_row_f[:],
                                             x2buf[0:1, q * 512:(q + 1) * 512],
                                             start=True, stop=True), waits=[ldb2, pbfree[q]])
        pbfree[q] = p.dve(lambda v, q=q: v.tensor_copy(b2_rep[:, q * 512:(q + 1) * 512],
                                                       PB[q][:]), waits=[tmq])
    p.barrier()

    fsl_free = [None, None]
    f2_free = [None, None]
    x2_free = [None]
    xr3_free = [None]
    lnt3_free = [None, None]
    ffn_guard = [None]
    out_done = [None]

    def ffn_ln3(ft):
        row0 = ft * TF
        for tb in range(TBF):
            i = tb % 2
            ldx = p.dma("sync", "x2b", x2buf[:], x2_scr[row0 + tb * 128:row0 + (tb + 1) * 128, :],
                        waits=[x2_free[0], xr3_free[0]])
            tln = ln_block(x2buf[:], lntb3[i][:], DM, [ldx, lnt3_free[i]], sq_scr3)
            x2_free[0] = tln
            for g8 in range(2):
                pti = (tb * 2 + g8) % 2
                tts = None
                for k in range(8):
                    dc = g8 * 8 + k
                    tts = p.pe(lambda t_, dc=dc, k=k, pti=pti, i=i: t_.transpose(
                        PT3[pti][:, k * 128:(k + 1) * 128], lntb3[i][:, dc * 128:(dc + 1) * 128],
                        ident_b[:]), waits=[tln, ptfree[pti]])
                tev = p.dve(lambda v, g8=g8, tb=tb, pti=pti: v.tensor_copy(
                    ln3_fm[:].rearrange("p (dc tok) -> p dc tok", tok=TF)[:, g8 * 8:(g8 + 1) * 8,
                                                                         tb * 128:(tb + 1) * 128],
                    PT3[pti][:].rearrange("p (k f) -> p k f", f=128)),
                    waits=[tts, ffn_guard[0]])
                ptfree[pti] = tev
                lnt3_free[i] = tev
        ln3_tok[0] = p.tok("dve")

    def ffn_compute(ft):
        row0 = ft * TF
        t_ln3 = ln3_tok[0]
        for fc in range(64):
            i = fc % 2
            sl = p.dma("sync", f"fs{i}", fslab[i][:], wb1[fc], waits=[fsl_free[i]])
            r = fc % 2
            mmt = None
            for dc in range(NDC):
                mmt = p.pe(lambda t_, i=i, dc=dc, r=r: t_.matmul(
                    PB[r][:, 0:TF], fslab[i][:, dc * 128:(dc + 1) * 128],
                    ln3_fm[:, dc * TF:(dc + 1) * TF], start=(dc == 0), stop=(dc == NDC - 1)),
                    waits=[sl, t_ln3, pbfree[r]], inc=(dc == NDC - 1))
            fsl_free[i] = mmt
            th1 = p.act(lambda s, fc=fc, r=r: s.activation(
                h1_buf[:, fc * TF:(fc + 1) * TF], PB[r][:, 0:TF], AF.Gelu,
                bias=b1_cols[:, fc:fc + 1]),
                waits=[mmt])
            pbfree[r] = th1
        t_h1 = p.tok("act")
        ffn_guard[0] = p.tok("pe")
        while epi_pend:
            epi_pend.pop(0)()
        if ft + 1 < NTF:
            ffn_ln3(ft + 1)
        for dmc in range(16):
            i = dmc % 2
            lsl = p.dma("act", f"f2{i}", f2slab[i][:], wb2c[dmc],
                        waits=[f2_free[i]])
            r = 2 + dmc % 2
            mmt = None
            for fcc in range(64):
                mmt = p.pe(lambda t_, dmc=dmc, fcc=fcc, i=i, r=r: t_.matmul(
                    PB[r][:, 0:TF], f2slab[i][:, fcc * 128:(fcc + 1) * 128],
                    h1_buf[:, fcc * TF:(fcc + 1) * TF], start=(fcc == 0), stop=(fcc == 63)),
                    waits=[lsl, t_h1, pbfree[r]], inc=(fcc == 63))
            f2_free[i] = mmt
            tev = p.dve(lambda v, r=r: v.tensor_copy(o_ev3[:], PB[r][:, 0:TF]),
                        waits=[mmt])
            pbfree[r] = tev
            for tb in range(TBF):
                rr = 4 + tb % 2
                tt_ = p.pe(lambda t_, tb=tb, rr=rr: t_.transpose(
                    PB[rr][:, 128:256], o_ev3[:, tb * 128:(tb + 1) * 128], ident_f[:]),
                    waits=[tev, pbfree[rr]])
                tot = p.dve(lambda v, tb=tb, dmc=dmc, rr=rr: v.tensor_copy(
                    o_tbs3[:, tb * DM + dmc * 128:tb * DM + (dmc + 1) * 128], PB[rr][:, 128:256]),
                    waits=[tt_])
                pbfree[rr] = tot
                out_done[0] = tot
        def epi_emit(row0=row0, od=out_done[0]):
            for tb in range(TBF):
                i = tb % 2
                ldr = p.dma("sync", "x2b", xr3[:], x2_scr[row0 + tb * 128:row0 + (tb + 1) * 128, :],
                            waits=[xr3_free[0], x2_free[0]])
                tb2 = p.dve(lambda v, tb=tb: v.tensor_add(o_tbs3[:, tb * DM:(tb + 1) * DM],
                                                          o_tbs3[:, tb * DM:(tb + 1) * DM],
                                                          b2_rep[:]),
                            waits=[od])
                tln = ln_block(o_tbs3[:, tb * DM:(tb + 1) * DM], lntb3[i][:], DM,
                               [tb2, lnt3_free[i]], sq_scr3)
                tfin = p.dve(lambda v, i=i: v.tensor_add(xr3[:], xr3[:], lntb3[i][:]),
                             waits=[tln, ldr])
                lnt3_free[i] = tfin
                x2_free[0] = tfin
                xr3_free[0] = p.dma("pool", "outw",
                                    out_d[row0 + tb * 128:row0 + (tb + 1) * 128, :], xr3[:],
                                    waits=[tfin])
        epi_pend.append(epi_emit)

    ln3_tok = [None]
    epi_pend = []
    ffn_guard[0] = p.tok("pe")
    ffn_ln3(0)
    for ft in range(NTF):
        ffn_compute(ft)
    while epi_pend:
        epi_pend.pop(0)()

    p.barrier()
    sc3.close()

    blk = st.enter_context(nc.Block())
    p.finalize(blk)
    st.close()
    return nc


_CACHE = {}


def _consts():
    maskneg = np.where(np.arange(128)[None, :] >= np.arange(128)[:, None], 0.0, -30000.0)
    return {
        "c_ident_b": np.eye(128, dtype=bfd),
        "c_ident_f": np.eye(128, dtype=np.float32),
        "c_ones_row": np.ones((1, 128), dtype=bfd),
        "c_ones_row_f": np.ones((1, 128), dtype=np.float32),
        "c_ones_col_f": np.ones((128, 1), dtype=np.float32),
        "c_ones_col_b": np.ones((128, 1), dtype=bfd),
        "c_masknegT": np.ascontiguousarray(maskneg.T).astype(bfd),
        "c_irep": np.tile(np.eye(128, dtype=bfd), (1, 4)),
        "c_hseld": ((np.arange(128)[:, None] % 64) == (np.arange(8192)[None, :] // 128)).astype(bfd),
        "c_tlow": np.triu(np.ones((128, 128), dtype=np.float32)),
    }


def make_in_maps(cfg, L0_attn_hidden_states, L0_attn_W_in, L0_attn_conv_w, L0_attn_conv_b,
                 L0_attn_A, L0_attn_D, L0_attn_dt_bias, L0_attn_norm_w, L0_attn_W_out,
                 L0_ffn_w1, L0_ffn_b1, L0_ffn_w2, L0_ffn_b2):
    n = cfg["n_cores"]
    SC = cfg["s_core"]
    hs = np.asarray(L0_attn_hidden_states, dtype=np.float32)
    w_in = np.asarray(L0_attn_W_in, dtype=np.float32)
    w_in_p = np.zeros((DM, DINP), dtype=np.float32)
    w_in_p[:, :w_in.shape[1]] = w_in
    w1 = np.ascontiguousarray(np.asarray(L0_ffn_w1, dtype=np.float32))
    w2 = np.ascontiguousarray(np.asarray(L0_ffn_w2, dtype=np.float32))
    w_out = np.asarray(L0_attn_W_out, dtype=np.float32)
    w_in_blk = np.ascontiguousarray(
        w_in_p.reshape(16, 128, NFC, 128).transpose(2, 1, 0, 3).reshape(NFC, 128, DM))
    w_out_blk = np.ascontiguousarray(
        w_out.reshape(32, 128, 16, 128).transpose(2, 1, 0, 3).reshape(16, 128, 4096))
    w1_blk = np.ascontiguousarray(
        w1.reshape(64, 128, 16, 128).transpose(0, 3, 2, 1).reshape(64, 128, DM))
    w2_blk = np.ascontiguousarray(
        w2.reshape(16, 128, 64, 128).transpose(0, 3, 2, 1).reshape(16, 128, 8192))
    base = dict(
        w_in_blk=w_in_blk,
        conv_w=np.ascontiguousarray(np.asarray(L0_attn_conv_w, dtype=np.float32)),
        conv_b=np.ascontiguousarray(np.asarray(L0_attn_conv_b, dtype=np.float32)),
        a_in=np.ascontiguousarray(np.asarray(L0_attn_A, dtype=np.float32)),
        d_in=np.ascontiguousarray(np.asarray(L0_attn_D, dtype=np.float32)),
        dtb_in=np.ascontiguousarray(np.asarray(L0_attn_dt_bias, dtype=np.float32)),
        nw_in=np.ascontiguousarray(np.asarray(L0_attn_norm_w, dtype=np.float32)),
        w_out_blk=w_out_blk,
        w1_blk=w1_blk, b1_in=np.ascontiguousarray(np.asarray(L0_ffn_b1, dtype=np.float32)),
        w2_blk=w2_blk, b2_in=np.ascontiguousarray(np.asarray(L0_ffn_b2, dtype=np.float32)),
    )
    base.update(_consts())
    in_maps = []
    for k in range(n):
        b_, h = k // 2, k % 2
        m = dict(base)
        m["x_main"] = np.ascontiguousarray(hs[b_, h * SC:(h + 1) * SC, :])
        m["x_halo"] = (np.zeros((3, DM), np.float32) if h == 0
                       else np.ascontiguousarray(hs[b_, SC - 3:SC, :]))
        m["c_smask"] = np.full((1, 1), 1.0 if h == 1 else 0.0, np.float32)
        m["w1s_in"] = np.ascontiguousarray(w1[k * (8192 // n):(k + 1) * (8192 // n), :])
        m["w2s_in"] = np.ascontiguousarray(w2[k * (DM // n):(k + 1) * (DM // n), :])
        in_maps.append(m)
    return in_maps


def kernel(**inputs):
    cfg = default_cfg()
    n = cfg["n_cores"]
    SC = cfg["s_core"]
    in_maps = make_in_maps(cfg, **inputs)
    if "nc" not in _CACHE:
        _CACHE["nc"] = build_program(cfg)
    res = run_bass_kernel_spmd(_CACHE["nc"], in_maps, core_ids=list(range(n)))
    nb = n // 2
    out = np.empty((nb, 2 * SC, DM), dtype=np.float32)
    for k in range(n):
        b_, h = k // 2, k % 2
        out[b_, h * SC:(h + 1) * SC, :] = np.asarray(res.results[k]["out"])
    return out
